# version 10
# baseline (speedup 1.0000x reference)
import contextlib
import numpy as np
import ml_dtypes

import concourse.bass as bass
import concourse.mybir as mybir
from concourse.bass_utils import run_bass_kernel_spmd

F32 = mybir.dt.float32
BF16 = mybir.dt.bfloat16
AF = mybir.ActivationFunctionType
ALU = mybir.AluOpType
AX = mybir.AxisListType

S = 4096
DM = 1024
INNER = 2048
H, KVH, HD = 16, 4, 64
NST = 16
DTR = 64
NE, IFF = 8, 2048
CH = 512
NCH = S // CH
NTT = CH // 128
NIT = INNER // 128
NDC = DM // 128
LN_EPS = 1e-5
RMS_EPS = float(np.finfo(np.float32).eps)
NKP = S // 128
ROPE_ = 32


class Ctr:
    def __init__(self, h):
        self.h = h
        self.v = 0


class DStream:
    def __init__(self, nc, es, name, nbuf):
        self.ss = [Ctr(es.enter_context(nc.semaphore(f"{name}{i}")))
                   for i in range(nbuf)]
        self.k = 0

    def dma(self, q, out, in_, **kw):
        s = self.ss[self.k % len(self.ss)]
        self.k += 1
        if s.v > 0:
            q.wait_ge(s.h, s.v)
        q.dma_start(out, in_, **kw).then_inc(s.h, 16)
        s.v += 16
        return (s.h, s.v)

    def totals(self):
        return [(s.h, s.v) for s in self.ss if s.v > 0]


class _Ar:
    def __init__(self, nc, es, name, n, dt):
        self.t = es.enter_context(nc.sbuf_tensor(name, [128, n], dt))
        self.off = 0
        self.cap = n
        self.marks = []

    def take(self, n):
        n = int(n)
        assert self.off + n <= self.cap, f"arena overflow {self.off}+{n}>{self.cap}"
        ap = self.t[:, self.off:self.off + n]
        self.off += n
        return ap

    def push(self):
        self.marks.append(self.off)

    def pop(self):
        self.off = self.marks.pop()


def build(upto="full", dbg=(), nch=NCH, race_check=False):
    nc = bass.Bass("TRN2", target_bir_lowering=False,
                   detect_race_conditions=race_check)
    dbg = set(dbg)

    d_in = {}
    def din(name, shape, dt=F32):
        d_in[name] = nc.dram_tensor(name, list(shape), dt, kind="ExternalInput").ap()
        return d_in[name]

    x_h = din("x", [S, DM])
    wcqT_h = din("wcqT", [DM, DM])
    winT_h = din("winT", [DM, 4608])
    wxpT_h = din("wxpT", [INNER, 96])
    wdtT_h = din("wdtT", [DTR, INNER])
    wmoT_h = din("wmoT", [INNER, DM])
    wprT_h = din("wprT", [DM, DM])
    alog_h = din("alog", [128, NIT * NST])
    cw_h = din("cw", [128, NIT * 4])
    cb_h = din("cb", [128, NIT])
    bdt_h = din("bdt", [128, NIT])
    dpar_h = din("dpar", [128, NIT])
    grep_h = din("grep", [128, H])
    alpha_h = din("alpha", [128, 1])
    wr_h = din("wr", [DM, NE])
    br_h = din("br", [128, NE])
    w1_h = din("w1", [NE, DM, IFF])
    w3_h = din("w3", [NE, DM, IFF])
    w2_h = din("w2", [NE, IFF, DM])
    cos_h = din("cosr", [S, 256], BF16)
    sin_h = din("sinr", [S, 256], BF16)
    ident_h = din("ident", [128, 128], BF16)
    ones1_h = din("ones1", [1, 128], BF16)
    oneh_h = din("oneh", [16, 16 * 128], BF16)
    mask4_h = din("mask4", [4, 128, CH], BF16)

    out_h = nc.dram_tensor("out", [S, DM], F32, kind="ExternalOutput").ap()

    d_dbg = {}
    def ddbg(name, shape, dt=F32):
        if name in dbg:
            d_dbg[name] = nc.dram_tensor("dbg_" + name, list(shape), dt,
                                         kind="ExternalOutput").ap()

    ddbg("xn", [NDC, 128, S], BF16)
    ddbg("c_coef", [128, (S // 128) * NE])
    ddbg("bc", [32, S], BF16)
    ddbg("gt", [128, NIT, S], BF16)
    ddbg("merged", [128, NDC, S], BF16)
    ddbg("qt", [128, H, S], BF16)
    ddbg("xs", [128, NIT, S], BF16)

    sc_xnT = nc.dram_tensor("sc_xnT", [NDC, 128, S], BF16, kind="Internal").ap()
    sc_kT = nc.dram_tensor("sc_kT", [NKP, 65, KVH, 128], BF16, kind="Internal").ap()
    sc_vh = nc.dram_tensor("sc_vh", [NKP, 128, KVH * 65], BF16, kind="Internal").ap()
    sc_wcq = nc.dram_tensor("sc_wcq", [NDC, 128, DM], BF16, kind="Internal").ap()
    sc_win = nc.dram_tensor("sc_win", [NDC, 128, 4608], BF16, kind="Internal").ap()
    sc_wpr = nc.dram_tensor("sc_wpr", [NDC, 128, DM], BF16, kind="Internal").ap()
    sc_wmo = nc.dram_tensor("sc_wmo", [NIT, 128, DM], BF16, kind="Internal").ap()

    es = contextlib.ExitStack()
    with es:
        afL = _Ar(nc, es, "aL32", 1984, F32)
        abL = _Ar(nc, es, "aLbf", 8704, BF16)
        af = _Ar(nc, es, "aR32", 10496, F32)
        ab = _Ar(nc, es, "aRbf", 72704, BF16)

        ident = abL.take(128)
        ones1 = abL.take(128)
        oneh = abL.take(16 * 128)
        mask4 = abL.take(4 * CH)
        wr_bf = abL.take(NDC * NE)
        wr_lo = abL.take(NDC * NE)
        wxpT_bf = abL.take(NIT * 96)
        wdtT_bf = abL.take(INNER)
        linv_bf = abL.take(CH)
        A_sb = afL.take(NIT * NST)
        cwv = afL.take(NIT * 4)
        cb_sb = afL.take(NIT)
        bdt_sb = afL.take(NIT)
        Dw_sb = afL.take(NIT)
        grep = afL.take(H)
        negg64 = afL.take(H)
        w_col = afL.take(1)
        br_rep = afL.take(NE)
        c_coef = afL.take((S // 128) * NE)
        scan_carry = afL.take(NIT * NST)
        conv_carry = afL.take(NIT * 3)
        stat = afL.take(16)
        stat2 = afL.take(16)
        linv = afL.take(CH)
        wr_f32 = afL.take(NDC * NE)

        af.push(); ab.push()
        xbuf = [af.take(DM) for _ in range(2)]
        junk = af.take(DM)
        q_sb = af.take(DM)
        kv_sb = af.take(256)
        yT_sb = af.take(CH)
        delta_b = [af.take(CH) for _ in range(4)]
        dA_b = [af.take(CH) for _ in range(2)]
        dA_b += [junk[:, 0:512], q_sb[:, 0:512]]
        sp_tmp = af.take(CH)
        out_st = [af.take(DM) for _ in range(2)]

        xnT_ch = ab.take(NDC * CH)
        wq_sl = [ab.take(CH) for _ in range(2)]
        win_sl = [ab.take(NDC * 128) for _ in range(2)]
        wing_sl = [ab.take(NDC * 128) for _ in range(2)]
        wmo_sl = [ab.take(NIT * 128) for _ in range(2)]
        cosr = ab.take(NTT * 256)
        sinr = ab.take(NTT * 256)
        qT_sb = ab.take(H * CH)
        xn_bf = qT_sb[:, 0:DM]
        kT_ch = ab.take(KVH * CH)
        vh_st = ab.take(KVH * 65)
        kT_sl = [ab.take(KVH * 128) for _ in range(2)]
        vh_sl = [ab.take(KVH * 65) for _ in range(2)]
        probs = [ab.take(2 * CH) for _ in range(2)]
        mergedT = ab.take(NDC * CH)
        xin_bf = ab.take(CH + 3)
        cvt = ab.take(CH)
        xs_bf = ab.take(NIT * CH)
        dtlowT_bf = ab.take(CH)
        B_bf = ab.take(CH)
        C_bf = ab.take(CH)
        Brep = ab.take(NST * CH)
        Crep = ab.take(NST * CH)
        h_stk = ab.take(NST * CH)
        silu_g = [ab.take(CH) for _ in range(4)]
        du_b = ab.take(CH)
        dBu_b = [ab.take(CH) for _ in range(2)]
        wbounce = Brep
        xnloT_ch = h_stk[:, 0:NDC * CH]
        xnlo_t = Crep[:, 0:DM]

        ps_q = es.enter_context(nc.psum_tensor("ps_q", [128, 1024], F32))
        ps_mo = es.enter_context(nc.psum_tensor("ps_mo", [128, 1024], F32))
        ps_y = es.enter_context(nc.psum_tensor("ps_y", [128, 1024], F32))
        ps_e = ps_y[:, 0:512]
        ps_tt = es.enter_context(nc.psum_tensor("ps_tt", [128, 2048], BF16))
        ps_t = ps_tt[:, 0:128]
        ps_t2 = ps_tt[:, 1024:1152]

        sT = Ctr(es.enter_context(nc.semaphore("sT")))
        sV = Ctr(es.enter_context(nc.semaphore("sV")))
        sS = Ctr(es.enter_context(nc.semaphore("sS")))
        sG = Ctr(es.enter_context(nc.semaphore("sG")))

        stPre = DStream(nc, es, "pre", 2)
        stPreG = DStream(nc, es, "prg", 2)
        stX = DStream(nc, es, "x", 2)
        stXnW = DStream(nc, es, "xnw", 2)
        stXnR = DStream(nc, es, "xnr", 1)
        stWB = DStream(nc, es, "wb", 2)
        stWS = DStream(nc, es, "ws", 2)
        stWq = DStream(nc, es, "wq", 2)
        stWin = DStream(nc, es, "wi", 2)
        stWing = DStream(nc, es, "wg", 2)
        stWmo = DStream(nc, es, "wm", 2)
        stCos = DStream(nc, es, "cs", 2)
        stKW = DStream(nc, es, "kw", 2)
        stKR = DStream(nc, es, "kr", 2)
        stVR = DStream(nc, es, "vr", 2)
        stEpi = DStream(nc, es, "ep", 2)
        stOut = DStream(nc, es, "ou", 2)
        stDbg = DStream(nc, es, "db", 1)
        stW1 = DStream(nc, es, "w1s", 1)
        stW2 = DStream(nc, es, "w2s", 1)
        stW3 = DStream(nc, es, "w3s", 1)
        stXn3 = DStream(nc, es, "x3", 2)
        stAcc = DStream(nc, es, "ac", 2)

        blk = es.enter_context(nc.Block())

        class _Proxy:
            __slots__ = ("incs",)
            def __init__(self):
                self.incs = []

            def then_inc(self, sem, n):
                self.incs.append((sem, n))
                return self

        class _Rec:
            def __init__(self, calls):
                self.calls = calls
                self.last = None

            def __getattr__(self, n):
                def f(*a, **k):
                    p = _Proxy()
                    self.calls.append((n, a, k, p))
                    self.last = p
                    return p
                return f

        _onq = []

        def on(engname, fn):
            calls = []
            fn(_Rec(calls))
            if calls:
                _onq.append((engname, calls))

        def flush():
            if not _onq:
                return
            q = _onq[:]
            _onq.clear()
            for engname in ("sync", "gpsimd", "tensor", "scalar", "vector"):
                seq = [c for e, cs in q if e == engname for c in cs]
                if not seq:
                    continue

                def body(e, seq=seq):
                    for n, a, k, p in seq:
                        r = getattr(e, n)(*a, **k)
                        for sem, cnt in p.incs:
                            r.then_inc(sem, cnt)
                getattr(blk, engname)(body)

        def vinc(v):
            v.last.then_inc(sV.h, 1); sV.v += 1; return sV.v

        def sinc(s):
            s.last.then_inc(sS.h, 1); sS.v += 1; return sS.v

        def tinc(t):
            t.last.then_inc(sT.h, 1); sT.v += 1; return sT.v

        def sedge(s):
            s.last.then_inc(sS.h, 1); sS.v += 1
            s.wait_ge(sS.h, sS.v)

        def vedge(v):
            v.last.then_inc(sV.h, 1); sV.v += 1
            v.wait_ge(sV.h, sV.v)

        M1 = {}

        def phase_pre():
            def g(q):
                stPreG.dma(q, ident[:, :], ident_h)
                stPreG.dma(q, ones1[0:1, :], ones1_h)
                stPreG.dma(q, oneh[0:16, :].rearrange("a (b c) -> a b c", b=16),
                          oneh_h.rearrange("a (b c) -> a b c", b=16))
                stPreG.dma(q, mask4[:, :].rearrange("p (a c) -> p a c", a=4),
                          mask4_h.rearrange("a p c -> p a c"))
                stPreG.dma(q, wxpT_bf[:, :].rearrange("p (a e) -> p a e", a=NIT),
                          wxpT_h.rearrange("(a p) e -> p a e", p=128))
                stPreG.dma(q, wdtT_bf[0:64, :], wdtT_h)
            on("gpsimd", g)

            def q0(q):
                stPre.dma(q, A_sb[:, :], alog_h)
                stPre.dma(q, cwv[:, :], cw_h)
                stPre.dma(q, cb_sb[:, :], cb_h)
                stPre.dma(q, bdt_sb[:, :], bdt_h)
                stPre.dma(q, Dw_sb[:, :], dpar_h)
                stPre.dma(q, grep[:, :], grep_h)
                stPre.dma(q, w_col[:, :], alpha_h)
                stPre.dma(q, br_rep[:, :], br_h)
                stPre.dma(q, wr_f32[:, :].rearrange("p (a e) -> p a e", a=NDC),
                          wr_h.rearrange("(a p) e -> p a e", p=128))
            on("sync", q0)
            tot = stPre.totals() + stPreG.totals()

            def fs(s):
                for hh, vv in tot:
                    s.wait_ge(hh, vv)
                s.activation(A_sb[:, :], A_sb[:, :], AF.Exp)
                s.activation(w_col[:, :], w_col[:, :], AF.Exp, scale=-1.0)
                s.activation(negg64[:, :], grep[:, :], AF.Abs)
                sinc(s)
            on("scalar", fs)
            m = sS.v

            def fv(v, m=m):
                v.wait_ge(sS.h, m)
                v.tensor_scalar_mul(A_sb[:, :], A_sb[:, :], -1.0)
                v.tensor_scalar_mul(negg64[:, :], negg64[:, :], -64.0)
                v.tensor_scalar_add(w_col[:, :], w_col[:, :], 1.0)
                vedge(v)
                v.reciprocal(w_col[:, :], w_col[:, :])
                vedge(v)
                v.tensor_scalar(stat[:, 0:1], w_col[:, :], -1.0, 1.0, ALU.mult, ALU.add)
                vedge(v)
                v.tensor_scalar_mul(Dw_sb[:, :], Dw_sb[:, :], stat[:, 0:1])
                v.memset(scan_carry[:, :], 0.0)
                v.memset(conv_carry[:, :], 0.0)
                v.tensor_copy(wr_bf[:, :], wr_f32[:, :])
                vedge(v)
                v.tensor_tensor(wr_lo[:, :], wr_f32[:, :], wr_bf[:, :], ALU.subtract)
                vinc(v)
            on("vector", fv)

        def phase0_weights():
            jobs = []
            for p in range(NDC):
                jobs.append((wcqT_h[p * 128:(p + 1) * 128, :], sc_wcq[p, :, :], DM))
            for p in range(NDC):
                jobs.append((winT_h[p * 128:(p + 1) * 128, 0:2304],
                             sc_win[p, :, 0:2304], 2304))
                jobs.append((winT_h[p * 128:(p + 1) * 128, 2304:4608],
                             sc_win[p, :, 2304:4608], 2304))
            for p in range(NDC):
                jobs.append((wprT_h[p * 128:(p + 1) * 128, :], sc_wpr[p, :, :], DM))
            for p in range(NIT):
                jobs.append((wmoT_h[p * 128:(p + 1) * 128, :], sc_wmo[p, :, :], DM))

            ws_m = []
            for i, (src, dst, ncols) in enumerate(jobs):
                bnc = wbounce[:, (i % 2) * 2304:(i % 2) * 2304 + ncols]

                def g(q, i=i, src=src, bnc=bnc):
                    if i >= 2:
                        hh, vv = ws_m[i - 2]
                        q.wait_ge(hh, vv)
                    stWB.dma(q, bnc[:, :], src)
                on("gpsimd", g)
                wb = stWB.ss[(stWB.k - 1) % 2]
                mwb = wb.v

                def q1(q, dst=dst, bnc=bnc, wb=wb, mwb=mwb):
                    q.wait_ge(wb.h, mwb)
                    ws_m.append(stWS.dma(q, dst, bnc[:, :]))
                on("sync", q1)
            return stWS.totals()

        p0 = {"x": {}, "ln": {}, "tp": {}, "evac": {}, "c": {}, "rt": {}, "wr": {}}

        def phase0():
            for c in range(nch):
                for tt in range(NTT):
                    ti = c * NTT + tt
                    xb = xbuf[tt % 2]

                    def fq(q, c=c, tt=tt, ti=ti):
                        if ti >= 2:
                            q.wait_ge(sV.h, p0["c"][ti - 2])
                        r0 = c * CH + tt * 128
                        p0["x"][ti] = stX.dma(q, xbuf[tt % 2][:, :], x_h[r0:r0 + 128, :])
                    on("sync", fq)

                    def fv(v, ti=ti, xb=xb):
                        hh, vv = p0["x"][ti]
                        v.wait_ge(hh, vv)
                        v.tensor_reduce(stat[:, 0:1], xb[:, :], AX.X, ALU.add, negate=True)
                        vedge(v)
                        v.tensor_scalar_mul(stat[:, 0:1], stat[:, 0:1], 1.0 / DM)
                        vinc(v)
                    on("vector", fv)
                    m_nm = sV.v

                    def fs(s, m=m_nm, xb=xb):
                        s.wait_ge(sV.h, m)
                        s.activation(junk[:, :], xb[:, :], AF.Square,
                                     bias=stat[:, 0:1], accum_out=stat2[:, 0:1])
                        sinc(s)
                    on("scalar", fs)
                    m_sq = sS.v

                    def fv2(v, m=m_sq):
                        v.wait_ge(sS.h, m)
                        v.tensor_scalar(stat2[:, 0:1], stat2[:, 0:1], 1.0 / DM, LN_EPS,
                                        ALU.mult, ALU.add)
                        vinc(v)
                    on("vector", fv2)
                    m_v2 = sV.v

                    def fs2(s, m=m_v2):
                        s.wait_ge(sV.h, m)
                        s.activation(stat2[:, 0:1], stat2[:, 0:1], AF.Ln)
                        sedge(s)
                        s.activation(stat2[:, 0:1], stat2[:, 0:1], AF.Exp, scale=-0.5)
                        sinc(s)
                    on("scalar", fs2)
                    m_rs = sS.v

                    def fv3(v, m=m_rs, ti=ti, xb=xb):
                        v.wait_ge(sS.h, m)
                        if ti >= 1:
                            v.wait_ge(sT.h, p0["tp"][ti - 1])
                        v.tensor_scalar(junk[:, :], xb[:, :], stat[:, 0:1],
                                        stat2[:, 0:1], ALU.add, ALU.mult)
                        vedge(v)
                        v.tensor_copy(xn_bf[:, :], junk[:, :])
                        vedge(v)
                        v.tensor_tensor(xnlo_t[:, :], junk[:, :], xn_bf[:, :],
                                        ALU.subtract)
                        vinc(v)
                    on("vector", fv3)
                    p0["ln"][ti] = sV.v
                    evac_base = sV.v

                    def ft(t, ti=ti, evac_base=evac_base):
                        t.wait_ge(sV.h, p0["ln"][ti])
                        for k in range(2 * NDC):
                            dc = k % NDC
                            srcb = xn_bf if k < NDC else xnlo_t
                            if k >= 2:
                                t.wait_ge(sV.h, evac_base + k - 1)
                            elif ti > 0:
                                t.wait_ge(sV.h, p0["evac"][ti - 1] - (1 - k))
                            t.transpose((ps_t if k % 2 == 0 else ps_t2)[:, :],
                                        srcb[:, dc * 128:(dc + 1) * 128], ident[:, :])
                            tinc(t)
                    on("tensor", ft)
                    p0["tp"][ti] = sT.v
                    m_tp0 = sT.v - 2 * NDC

                    def fv4(v, m0=m_tp0, tt=tt, c=c):
                        if tt == 0 and c >= 1:
                            hh, vv = p0["wr"][c - 1]
                            v.wait_ge(hh, vv)
                            v.wait_ge(sT.h, p0["rt"][c - 1])
                        for k in range(2 * NDC):
                            dc = k % NDC
                            dstb = xnT_ch if k < NDC else xnloT_ch
                            v.wait_ge(sT.h, m0 + k + 1)
                            v.tensor_copy(
                                dstb[:, dc * CH + tt * 128: dc * CH + tt * 128 + 128],
                                (ps_t if k % 2 == 0 else ps_t2)[:, :]).then_inc(sV.h, 1)
                            sV.v += 1
                    on("vector", fv4)
                    p0["evac"][ti] = sV.v

                    def ft2(t, tt=tt, ti=ti):
                        t.wait_ge(sV.h, p0["evac"][ti])
                        if ti >= 1:
                            t.wait_ge(sV.h, p0["c"][ti - 1])
                        for ji, (xT, wB) in enumerate(((xnT_ch, wr_bf),
                                                       (xnloT_ch, wr_bf),
                                                       (xnT_ch, wr_lo))):
                            for dc in range(NDC):
                                t.matmul(ps_e[:, 0:NE],
                                         xT[:, dc * CH + tt * 128: dc * CH + tt * 128 + 128],
                                         wB[:, dc * NE:(dc + 1) * NE],
                                         start=(ji == 0 and dc == 0),
                                         stop=(ji == 2 and dc == NDC - 1))
                        tinc(t)
                    on("tensor", ft2)
                    m_lg = sT.v
                    p0["rt"][c] = sT.v

                    def fv5(v, m=m_lg, ti=ti):
                        v.wait_ge(sT.h, m)
                        cc = c_coef[:, ti * NE:(ti + 1) * NE]
                        v.tensor_tensor(cc, ps_e[:, 0:NE], br_rep[:, :], ALU.add)
                        vedge(v)
                        v.tensor_reduce(stat[:, 1:2], cc, AX.X, ALU.max)
                        vedge(v)
                        v.tensor_single_scalar(stat[:, 8:16], cc, stat[:, 1:2], ALU.is_equal)
                        vedge(v)
                        v.scalar_tensor_tensor(junk[:, 0:NE], stat[:, 8:16], -1e30, cc,
                                               ALU.mult, ALU.add)
                        vedge(v)
                        v.tensor_reduce(stat[:, 2:3], junk[:, 0:NE], AX.X, ALU.max)
                        vedge(v)
                        v.tensor_tensor(stat[:, 3:4], stat[:, 2:3], stat[:, 1:2],
                                        ALU.subtract)
                        vinc(v)
                    on("vector", fv5)
                    m_r1 = sV.v

                    def fs3(s, m=m_r1):
                        s.wait_ge(sV.h, m)
                        s.activation(stat[:, 3:4], stat[:, 3:4], AF.Exp)
                        sinc(s)
                    on("scalar", fs3)
                    m_r2 = sS.v

                    def fv6(v, m=m_r2, ti=ti):
                        v.wait_ge(sS.h, m)
                        cc = c_coef[:, ti * NE:(ti + 1) * NE]
                        v.tensor_scalar_add(stat[:, 4:5], stat[:, 3:4], 1.0)
                        vedge(v)
                        v.reciprocal(stat[:, 4:5], stat[:, 4:5])
                        vedge(v)
                        v.tensor_tensor(stat[:, 5:6], stat[:, 3:4], stat[:, 4:5], ALU.mult)
                        v.tensor_single_scalar(junk[:, 0:NE], junk[:, 0:NE], stat[:, 2:3],
                                               ALU.is_equal)
                        v.tensor_scalar_mul(junk[:, 8:16], stat[:, 8:16], stat[:, 4:5])
                        vedge(v)
                        v.scalar_tensor_tensor(cc, junk[:, 0:NE], stat[:, 5:6],
                                               junk[:, 8:16], ALU.mult, ALU.add)
                        vinc(v)
                    on("vector", fv6)
                    p0["c"][ti] = sV.v

                def fq3(q, c=c):
                    q.wait_ge(sV.h, p0["evac"][c * NTT + NTT - 1])
                    for dc in range(NDC):
                        last = stXnW.dma(q, sc_xnT[dc, :, c * CH:(c + 1) * CH],
                                         xnT_ch[:, dc * CH:(dc + 1) * CH])
                    p0["wr"][c] = last
                on("sync", fq3)
                flush()

        def chunk1(c):
            K4 = NTT * (c + 1)
            M = {}
            M1[c] = M

            def fq(q, c=c):
                if c == 0:
                    for hh, vv in stXnW.totals():
                        q.wait_ge(hh, vv)
                    for hh, vv in wsT:
                        q.wait_ge(hh, vv)
                else:
                    q.wait_ge(sT.h, M1[c - 1]["xnT_done"])
                M["xnr"] = stXnR.dma(
                    q, xnT_ch[:, :].rearrange("p (dc t) -> p dc t", dc=NDC),
                    sc_xnT[:, :, c * CH:(c + 1) * CH].rearrange("dc p t -> p dc t"))
                if c >= 2:
                    q.wait_ge(sV.h, M1[c - 2]["rope_done"])
                stCos.dma(q, cosr[:, :].rearrange("p (tt f) -> p tt f", tt=NTT),
                          cos_h[c * CH:(c + 1) * CH, :].rearrange("(tt p) f -> p tt f", p=128))
                M["cos"] = stCos.dma(
                    q, sinr[:, :].rearrange("p (tt f) -> p tt f", tt=NTT),
                    sin_h[c * CH:(c + 1) * CH, :].rearrange("(tt p) f -> p tt f", p=128))
            on("sync", fq)
            m_cos = stCos.totals()

            kvq_sl = []
            rel_sl = []
            slabs = [("kv", dc, 0, sc_win[dc, :, 0:512]) for dc in range(NDC)] + \
                    [("q", dc, og, sc_wcq[dc, :, og * 512:og * 512 + 512])
                     for og in range(2) for dc in range(NDC)]
            for i, (kind, dc, og, src) in enumerate(slabs):
                def fqs(q, i=i, src=src, c=c):
                    if i >= 2:
                        q.wait_ge(sT.h, rel_sl[i - 2])
                    elif c >= 1:
                        q.wait_ge(sT.h, M1[c - 1]["prmm_last"])
                    kvq_sl.append(stWq.dma(q, wq_sl[i % 2][:, :], src))
                on("sync", fqs)
                grp = i // NDC

                def ftm(t, i=i, dc=dc, grp=grp, c=c):
                    hh, vv = kvq_sl[i]
                    t.wait_ge(hh, vv)
                    if dc == 0:
                        hh2, vv2 = M["xnr"]
                        t.wait_ge(hh2, vv2)
                        if grp == 0:
                            if c >= 1:
                                t.wait_ge(sV.h, M1[c - 1]["ps_free"])
                            else:
                                t.wait_ge(sV.h, p0["c"][nch * NTT - 1])
                        elif grp == 1:
                            t.wait_ge(sS.h, M["kv_evac_s"])
                        else:
                            t.wait_ge(sS.h, M["q_evac_s"][0])
                    for tt in range(NTT):
                        pt = (ps_q if tt < 2 else ps_mo)[:, (tt % 2) * 512:(tt % 2) * 512 + 512]
                        t.matmul(pt, xnT_ch[:, dc * CH + tt * 128: dc * CH + tt * 128 + 128],
                                 wq_sl[i % 2][:, :], start=(dc == 0), stop=(dc == NDC - 1))
                    tinc(t)
                on("tensor", ftm)
                rel_sl.append(sT.v)
                if grp == 0 and dc == NDC - 1:
                    M["kv_mm"] = sT.v
                    emit_kv_evac(c, M, m_cos)
                if grp == 1 and dc == NDC - 1:
                    M["q_mm0"] = sT.v
                    emit_q_evac(c, M, og=0)
                if grp == 2 and dc == NDC - 1:
                    M["q_mm1"] = sT.v
                    emit_q_evac(c, M, og=1)
            M["a_sl_done"] = rel_sl[-1]

            emit_qrow64(c, M)
            emit_mamba_front(c, M)
            att_state = emit_attention_init(c, M)
            scn_state = emit_scan_init(c, M)
            emit_scan_head(c, M, scn_state, 0)
            emit_scan_head(c, M, scn_state, 1)
            for hp in range(8):
                ki, ni = 0, 0
                heads_done = hp == 7
                while ki < K4 or ni < 32:
                    if not heads_done and ni >= 16:
                        emit_scan_head(c, M, scn_state, 2 * hp + 2)
                        emit_scan_head(c, M, scn_state, 2 * hp + 3)
                        heads_done = True
                    if ki < K4 and (ni * K4 >= ki * 32 or ni >= 32):
                        emit_attn_kp(c, M, att_state, hp, ki, K4)
                        ki += 1
                    else:
                        it, n = (2 * hp + ni // 16, ni % 16)
                        emit_scan_n(c, M, scn_state, it, n)
                        ni += 1
                if not heads_done:
                    emit_scan_head(c, M, scn_state, 2 * hp + 2)
                    emit_scan_head(c, M, scn_state, 2 * hp + 3)
                emit_attn_pass_end(c, M, att_state, hp)
                emit_scan_tail(c, M, scn_state, 2 * hp)
                emit_scan_tail(c, M, scn_state, 2 * hp + 1)
            M["attn_done"] = M["attn_done_v"]
            M["scan_last4"] = [scn_state["scan_m"][(NIT - 1, 12 + k)] for k in range(4)]
            emit_tail(c, M)

        def emit_kv_evac(c, M, m_cos):
            M["ktp_evac"] = []
            for tt in range(NTT):
                pt = (ps_q if tt < 2 else ps_mo)[:, (tt % 2) * 512:(tt % 2) * 512 + 512]

                def fsv(s, pt=pt, tt=tt, c=c):
                    s.wait_ge(sT.h, M["kv_mm"])
                    for hh, vv in m_cos:
                        s.wait_ge(hh, vv)
                    if tt == 0 and c >= 1:
                        for hh, vv in M1[c - 1]["kT_wr"]:
                            s.wait_ge(hh, vv)
                    if tt >= 1:
                        hh, vv = M["vh_wr"][tt - 1]
                        s.wait_ge(hh, vv)
                        s.wait_ge(sV.h, M["kv_rope"][tt - 1])
                    s.copy(kv_sb[:, :], pt[:, 0:256])
                    for kv in range(KVH):
                        s.activation(vh_st[:, kv * 65:kv * 65 + 64],
                                     pt[:, 256 + kv * 64:256 + kv * 64 + 64], AF.Copy)
                    sinc(s)
                on("scalar", fsv)
                m_k = sS.v

                def fvv(v, m=m_k):
                    v.wait_ge(sS.h, m)
                    for kv in range(KVH):
                        v.memset(vh_st[:, kv * 65 + 64:kv * 65 + 65], 1.0)
                    v.tensor_tensor(junk[:, 0:256], kv_sb[:, :], kv_sb[:, :], ALU.mult)
                    vedge(v)
                    v.tensor_reduce(stat[:, 0:4],
                                    junk[:, 0:256].rearrange("p (h d) -> p h d", h=KVH),
                                    AX.X, ALU.add)
                    vedge(v)
                    v.tensor_scalar(stat[:, 0:4], stat[:, 0:4], 1.0 / HD, RMS_EPS,
                                    ALU.mult, ALU.add)
                    vinc(v)
                on("vector", fvv)
                m_ms = sV.v

                def fsr(s, m=m_ms):
                    s.wait_ge(sV.h, m)
                    s.activation(stat[:, 0:4], stat[:, 0:4], AF.Ln)
                    sedge(s)
                    s.activation(stat[:, 0:4], stat[:, 0:4], AF.Exp, scale=-0.5)
                    sinc(s)
                on("scalar", fsr)
                m_rs = sS.v

                def fvr(v, m=m_rs, tt=tt):
                    v.wait_ge(sS.h, m)
                    for kv in range(KVH):
                        v.tensor_scalar_mul(kv_sb[:, kv * 64:(kv + 1) * 64],
                                            kv_sb[:, kv * 64:(kv + 1) * 64],
                                            stat[:, kv:kv + 1])
                    kvv = kv_sb[:, :].rearrange("p (h d) -> p h d", h=KVH)
                    cv = cosr[:, tt * 256:(tt + 1) * 256].rearrange("p (h d) -> p h d", h=H)[:, 0:KVH, :]
                    sv = sinr[:, tt * 256:(tt + 1) * 256].rearrange("p (h d) -> p h d", h=H)[:, 0:KVH, :]
                    a = kvv[:, :, 0:16]; b = kvv[:, :, 16:32]
                    vedge(v)
                    j3 = lambda o: junk[:, o:o + 64].rearrange("p (h d) -> p h d", h=KVH)
                    v.tensor_tensor(j3(0), a, cv, ALU.mult)
                    v.tensor_tensor(j3(64), b, sv, ALU.mult)
                    v.tensor_tensor(j3(128), b, cv, ALU.mult)
                    v.tensor_tensor(j3(192), a, sv, ALU.mult)
                    vedge(v)
                    v.tensor_tensor(a, j3(0), j3(64), ALU.add)
                    v.tensor_tensor(b, j3(128), j3(192), ALU.subtract)
                    vedge(v)
                    if tt >= 1:
                        v.wait_ge(sT.h, M["ktp_t"][tt - 1])
                    v.tensor_copy(cvt[:, 0:256], kv_sb[:, :])
                    vinc(v)
                on("vector", fvr)
                M.setdefault("kv_rope", []).append(sV.v)
                m_kro = sV.v
                evb = sV.v

                def ftk(t, m=m_kro, tt=tt, evb=evb, c=c):
                    t.wait_ge(sV.h, m)
                    for kv in range(KVH):
                        if kv >= 2:
                            t.wait_ge(sV.h, evb + kv - 1)
                        elif tt > 0:
                            t.wait_ge(sV.h, M["ktp_evac"][tt - 1] - (1 - kv))
                        elif c >= 1:
                            t.wait_ge(sV.h, M1[c - 1]["qtp_evac"][-1] - (1 - kv))
                            t.wait_ge(sV.h, M1[c - 1]["mg"][-1])
                        t.transpose((ps_t if kv % 2 == 0 else ps_t2)[0:64, 0:128],
                                    cvt[:, kv * 64:(kv + 1) * 64], ident[:, :])
                        tinc(t)
                on("tensor", ftk)
                M.setdefault("ktp_t", []).append(sT.v)
                m_tk0 = sT.v - KVH

                def fvk(v, m0=m_tk0, tt=tt):
                    for kv in range(KVH):
                        v.wait_ge(sT.h, m0 + kv + 1)
                        v.tensor_copy(
                            kT_ch[0:64, kv * CH + tt * 128: kv * CH + tt * 128 + 128],
                            (ps_t if kv % 2 == 0 else ps_t2)[0:64, 0:128]).then_inc(sV.h, 1)
                        sV.v += 1
                    M["ktp_evac"].append(sV.v)
                on("vector", fvk)
                m_vst = sV.v

                def fqv(q, tt=tt, c=c, m=m_vst):
                    q.wait_ge(sV.h, m)
                    M.setdefault("vh_wr", []).append(
                        stKW.dma(q, sc_vh[c * NTT + tt, :, :], vh_st[:, :]))
                on("sync", fqv)
            M["kv_evac_s"] = sS.v

            def fvo(v):
                v.memset(kT_ch[64:65, :], 1.0)
                vinc(v)
            on("vector", fvo)
            m_ones = sV.v

            def fqk(q, c=c, m=m_ones):
                q.wait_ge(sV.h, m)
                for tt in range(NTT):
                    stKW.dma(q, sc_kT[c * NTT + tt, :, :, :],
                             kT_ch[0:65, :].rearrange("p (kv t) -> p kv t", kv=KVH)[:, :, tt * 128:(tt + 1) * 128])
            on("sync", fqk)
            M["kT_wr"] = stKW.totals()

        def emit_q_evac(c, M, og):
            M.setdefault("q_evac_s", {})
            M.setdefault("qtp_evac", [0])
            M.setdefault("q_rope", [])
            M.setdefault("qtp_t", [])
            mm_m = M["q_mm0"] if og == 0 else M["q_mm1"]
            for tt in range(NTT):
                pt = (ps_q if tt < 2 else ps_mo)[:, (tt % 2) * 512:(tt % 2) * 512 + 512]

                def fse(s, pt=pt, og=og, tt=tt, mm_m=mm_m):
                    s.wait_ge(sT.h, mm_m)
                    if M["q_rope"]:
                        s.wait_ge(sV.h, M["q_rope"][-1])
                    s.copy(q_sb[:, og * 512:og * 512 + 512], pt[:, :])
                    sinc(s)
                on("scalar", fse)
                m_c = sS.v

                def fvq(v, m=m_c, og=og):
                    v.wait_ge(sS.h, m)
                    qs = q_sb[:, og * 512:og * 512 + 512]
                    v.tensor_tensor(junk[:, 0:512], qs, qs, ALU.mult)
                    vedge(v)
                    v.tensor_reduce(stat[:, 0:8],
                                    junk[:, 0:512].rearrange("p (h d) -> p h d", h=8),
                                    AX.X, ALU.add)
                    vedge(v)
                    v.tensor_scalar(stat[:, 0:8], stat[:, 0:8], 1.0 / HD, RMS_EPS,
                                    ALU.mult, ALU.add)
                    vinc(v)
                on("vector", fvq)
                m_ms = sV.v

                def fsr(s, m=m_ms):
                    s.wait_ge(sV.h, m)
                    s.activation(stat[:, 0:8], stat[:, 0:8], AF.Ln)
                    sedge(s)
                    s.activation(stat[:, 0:8], stat[:, 0:8], AF.Exp, scale=-0.5)
                    sinc(s)
                on("scalar", fsr)
                m_rs = sS.v

                def fvr(v, m=m_rs, og=og, tt=tt):
                    v.wait_ge(sS.h, m)
                    v.tensor_tensor(stat[:, 0:8], stat[:, 0:8],
                                    grep[:, og * 8:og * 8 + 8], ALU.mult)
                    qs = q_sb[:, og * 512:og * 512 + 512]
                    for h8 in range(8):
                        v.tensor_scalar_mul(qs[:, h8 * 64:(h8 + 1) * 64],
                                            qs[:, h8 * 64:(h8 + 1) * 64],
                                            stat[:, h8:h8 + 1])
                    qv3 = qs.rearrange("p (h d) -> p h d", h=8)
                    cv = cosr[:, tt * 256:(tt + 1) * 256].rearrange("p (h d) -> p h d", h=H)[:, 0:8, :]
                    sv = sinr[:, tt * 256:(tt + 1) * 256].rearrange("p (h d) -> p h d", h=H)[:, 0:8, :]
                    a = qv3[:, :, 0:16]; b = qv3[:, :, 16:32]
                    vedge(v)
                    j3 = lambda o: junk[:, o:o + 128].rearrange("p (h d) -> p h d", h=8)
                    v.tensor_tensor(j3(0), a, cv, ALU.mult)
                    v.tensor_tensor(j3(128), b, sv, ALU.mult)
                    v.tensor_tensor(j3(256), b, cv, ALU.mult)
                    v.tensor_tensor(j3(384), a, sv, ALU.mult)
                    vedge(v)
                    v.tensor_tensor(a, j3(0), j3(128), ALU.add)
                    v.tensor_tensor(b, j3(256), j3(384), ALU.subtract)
                    vedge(v)
                    if M["qtp_t"]:
                        v.wait_ge(sT.h, M["qtp_t"][-1])
                    else:
                        v.wait_ge(sT.h, M["ktp_t"][-1])
                    v.tensor_copy(cvt[:, 0:512], qs)
                    vinc(v)
                on("vector", fvr)
                M["q_rope"].append(sV.v)
                m_ro = sV.v
                evb = sV.v

                def ftq(t, m=m_ro, evb=evb, og=og, tt=tt):
                    t.wait_ge(sV.h, m)
                    for h8 in range(8):
                        if h8 >= 2:
                            t.wait_ge(sV.h, evb + h8 - 1)
                        else:
                            prev = M["qtp_evac"][-1] if (og > 0 or tt > 0) else M["ktp_evac"][-1]
                            t.wait_ge(sV.h, prev - (1 - h8))
                        t.transpose((ps_t if h8 % 2 == 0 else ps_t2)[0:64, 0:128],
                                    cvt[:, h8 * 64:(h8 + 1) * 64], ident[:, :])
                        tinc(t)
                on("tensor", ftq)
                M["qtp_t"].append(sT.v)
                m_tq0 = sT.v - 8

                def fvk(v, m0=m_tq0, og=og, tt=tt):
                    for h8 in range(8):
                        h = og * 8 + h8
                        v.wait_ge(sT.h, m0 + h8 + 1)
                        v.tensor_copy(
                            qT_sb[0:64, h * CH + tt * 128: h * CH + tt * 128 + 128],
                            (ps_t if h8 % 2 == 0 else ps_t2)[0:64, 0:128]).then_inc(sV.h, 1)
                        sV.v += 1
                    M["qtp_evac"].append(sV.v)
                on("vector", fvk)
            M["q_evac_s"][og] = sS.v

        def emit_qrow64(c, M):
            def fv(v):
                v.memset(qT_sb[64:65, :], 1.0)
                vedge(v)
                for h in range(H):
                    v.tensor_scalar_mul(qT_sb[64:65, h * CH:(h + 1) * CH],
                                        qT_sb[64:65, h * CH:(h + 1) * CH],
                                        negg64[64:65, h:h + 1])
                vinc(v)
            on("vector", fv)
            M["qrow"] = sV.v
            M["rope_done"] = sV.v

        def emit_attention_init(c, M):
            M["mg"] = []
            M["yT_evac"] = [0] * 8
            return {"kr": [], "vr": [], "av": [], "exp": [], "gi": 0}

        def emit_attn_kp(c, M, st, hp, kp, K4):
            kr, vr, av_done, exp_done = st["kr"], st["vr"], st["av"], st["exp"]
            h0, h1 = 2 * hp, 2 * hp + 1
            gi = st["gi"]; st["gi"] += 1

            def fqs(q, gi=gi, kp=kp, c=c):
                if gi == 0:
                    for hh, vv in stKW.totals():
                        q.wait_ge(hh, vv)
                    q.wait_ge(sV.h, M["qrow"])
                if gi >= 2:
                    q.wait_ge(sT.h, av_done[gi - 2])
                kr.append(stKR.dma(q, kT_sl[gi % 2][0:65, :],
                                   sc_kT[kp, :, :, :].rearrange("p kv t -> p (kv t)")))
                vr.append(stVR.dma(q, vh_sl[gi % 2][:, :], sc_vh[kp, :, :]))
            on("sync", fqs)

            def fts(t, gi=gi, h0=h0, h1=h1, kp=kp,
                    m_pe=(exp_done[-1] if exp_done else None),
                    m_mg=(M["mg"][-1] if (kp == 0 and M["mg"]) else None)):
                hh, vv = kr[gi]
                t.wait_ge(hh, vv)
                if m_mg is not None:
                    t.wait_ge(sV.h, m_mg)
                if gi == 0:
                    t.wait_ge(sS.h, M["q_evac_s"][1])
                    t.wait_ge(sS.h, M["reps"])
                    t.wait_ge(sV.h, M["bc"])
                if m_pe is not None:
                    t.wait_ge(sS.h, m_pe)
                for ih, h in enumerate((h0, h1)):
                    t.matmul(ps_q[:, ih * 512:(ih + 1) * 512],
                             kT_sl[gi % 2][0:65, (h // 4) * 128:(h // 4) * 128 + 128],
                             qT_sb[0:65, h * CH:(h + 1) * CH],
                             start=True, stop=True)
                tinc(t)
            on("tensor", fts)
            m_sc = sT.v

            def fse(s, m=m_sc, gi=gi):
                s.wait_ge(sT.h, m)
                if gi >= 2:
                    s.wait_ge(sT.h, av_done[gi - 2])
                s.activation(probs[gi % 2][:, :], ps_q[:, :], AF.Exp, scale=0.125)
                sinc(s)
            on("scalar", fse)
            exp_done.append(sS.v)
            m_exp = sS.v

            m_mask = None
            if kp >= 4 * c:
                j = kp - 4 * c
                def fgm(g, m=m_exp, gi=gi, j=j, c=c):
                    g.wait_ge(sS.h, m)
                    for ih in range(2):
                        g.affine_select(probs[gi % 2][:, ih * 512:(ih + 1) * 512],
                                        probs[gi % 2][:, ih * 512:(ih + 1) * 512],
                                        [[1, 512]], ALU.is_ge, 0.0,
                                        base=-128 * j, channel_multiplier=-1)
                    g.last.then_inc(sG.h, 1)
                on("gpsimd", fgm)
                sG.v += 1
                m_mask = sG.v

            def fta(t, gi=gi, kp=kp, h0=h0, h1=h1, hp=hp, c=c,
                    m_exp=m_exp, m_mask=m_mask, K4=K4):
                hh, vv = vr[gi]
                t.wait_ge(hh, vv)
                if m_mask is None:
                    t.wait_ge(sS.h, m_exp)
                else:
                    t.wait_ge(sG.h, m_mask)
                if kp == 0 and hp >= 1:
                    t.wait_ge(sS.h, M["yTc"][hp - 1])
                elif kp == 0:
                    t.wait_ge(sV.h, M["bc"])
                    t.wait_ge(sS.h, M["xin_done"][NIT - 1])
                for ih, h in enumerate((h0, h1)):
                    t.matmul(ps_y[0:65, ih * 512:(ih + 1) * 512],
                             vh_sl[gi % 2][:, (h // 4) * 65:(h // 4) * 65 + 65],
                             probs[gi % 2][:, ih * 512:(ih + 1) * 512],
                             start=(kp == 0), stop=(kp == K4 - 1))
                tinc(t)
            on("tensor", fta)
            av_done.append(sT.v)

        def emit_attn_pass_end(c, M, st, hp):
            m_av = st["av"][-1]
            for ih, h in enumerate((2 * hp, 2 * hp + 1)):
                m_mg0 = M["mg"][-1] if M["mg"] else None

                def fsc2(s, m=m_av, ih=ih, m_mg=m_mg0):
                    s.wait_ge(sT.h, m)
                    if m_mg is not None:
                        s.wait_ge(sV.h, m_mg)
                    s.copy(yT_sb[0:65, :], ps_y[0:65, ih * 512:(ih + 1) * 512])
                    sedge(s)
                    s.activation(linv[0:1, :], yT_sb[64:65, :], AF.Ln)
                    sedge(s)
                    s.activation(linv[0:1, :], linv[0:1, :], AF.Exp, scale=-1.0)
                    sinc(s)
                on("scalar", fsc2)
                m_ls = sS.v

                def fvl(v, m=m_ls):
                    v.wait_ge(sS.h, m)
                    v.tensor_scalar_mul(linv[0:1, :], linv[0:1, :], w_col[0:1, 0:1])
                    vedge(v)
                    v.tensor_copy(linv_bf[0:1, :], linv[0:1, :])
                    vinc(v)
                on("vector", fvl)
                m_li = sV.v

                def ftb(t, m=m_li, h=h, c=c):
                    t.wait_ge(sV.h, m)
                    if M["mg"]:
                        t.wait_ge(sV.h, M["mg"][-1])
                    t.matmul(ps_q[0:64, 0:512], ones1[0:1, 0:64],
                             linv_bf[0:1, :], start=True, stop=True)
                    tinc(t)
                on("tensor", ftb)
                m_rr = sT.v

                def fvg(v, m=m_rr, h=h):
                    v.wait_ge(sT.h, m)
                    base = (h % 2) * 64
                    v.tensor_tensor(
                        mergedT[base:base + 64, (h // 2) * CH:(h // 2) * CH + CH],
                        yT_sb[0:64, :], ps_q[0:64, 0:512], ALU.mult)
                    vinc(v)
                on("vector", fvg)
                M["mg"].append(sV.v)
            M.setdefault("yTc", [0] * 8)[hp] = sS.v
            M["yT_evac"][hp] = sV.v
            if hp == 7:
                M["attn_done_v"] = sV.v

        def emit_mamba_front(c, M):
            win_rd = []
            M["xssm_mm"] = []
            M["conv_carry"] = []
            M["xin_done"] = []
            M["xs_done"] = []
            for it in range(NIT):
                def fqw(q, it=it):
                    if it >= 2:
                        q.wait_ge(sT.h, M["xssm_mm"][it - 2])
                    win_rd.append(stWin.dma(
                        q, win_sl[it % 2][:, :].rearrange("p (dc o) -> p dc o", dc=NDC),
                        sc_win[:, :, 512 + it * 128:512 + it * 128 + 128].rearrange("dc p o -> p dc o")))
                on("sync", fqw)

                def ftx(t, it=it, c=c):
                    hh, vv = win_rd[it]
                    t.wait_ge(hh, vv)
                    if it == 0:
                        if c >= 1:
                            t.wait_ge(sV.h, M1[c - 1]["mo_evac"][-1])
                    else:
                        t.wait_ge(sV.h, M["conv_carry"][it - 1])
                        t.wait_ge(sS.h, M["xin_done"][it - 1])
                    for dc in range(NDC):
                        t.matmul(ps_e[:, :], win_sl[it % 2][:, dc * 128:(dc + 1) * 128],
                                 xnT_ch[:, dc * CH:(dc + 1) * CH],
                                 start=(dc == 0), stop=(dc == NDC - 1))
                    tinc(t)
                on("tensor", ftx)
                M["xssm_mm"].append(sT.v)
                m_mm = sT.v

                def fvc(v, m=m_mm, it=it):
                    v.wait_ge(sT.h, m)
                    v.tensor_copy(sp_tmp[:, 0:3], ps_e[:, 509:512])
                    v.tensor_copy(xin_bf[:, 0:3], conv_carry[:, it * 3:(it + 1) * 3])
                    vedge(v)
                    v.tensor_copy(conv_carry[:, it * 3:(it + 1) * 3], sp_tmp[:, 0:3])
                    vinc(v)
                on("vector", fvc)
                M["conv_carry"].append(sV.v)
                m_cs = sV.v

                def fsc(s, m=m_cs):
                    s.wait_ge(sV.h, m)
                    s.activation(xin_bf[:, 3:515], ps_e[:, :], AF.Copy)
                    sinc(s)
                on("scalar", fsc)
                M["xin_done"].append(sS.v)
                m_xin = sS.v

                def fvt(v, m=m_xin, it=it):
                    v.wait_ge(sS.h, m)
                    v.tensor_scalar_mul(cvt[:, :], xin_bf[:, 0:512],
                                        cwv[:, it * 4:it * 4 + 1])
                    for j in range(1, 4):
                        vedge(v)
                        v.scalar_tensor_tensor(cvt[:, :], xin_bf[:, j:512 + j],
                                               cwv[:, it * 4 + j:it * 4 + j + 1],
                                               cvt[:, :], ALU.mult, ALU.add)
                    vinc(v)
                on("vector", fvt)
                m_cv = sV.v

                def fss(s, m=m_cv, it=it):
                    s.wait_ge(sV.h, m)
                    s.activation(xs_bf[:, it * CH:(it + 1) * CH], cvt[:, :], AF.Silu,
                                 bias=cb_sb[:, it:it + 1])
                    sinc(s)
                on("scalar", fss)
                M["xs_done"].append(sS.v)

            def ftp(t):
                t.wait_ge(sS.h, M["xs_done"][NIT - 1])
                t.wait_ge(sV.h, M["conv_carry"][NIT - 1])
                t.wait_ge(sS.h, M["xin_done"][NIT - 1])
                for o0, o1, pp in [(0, 64, ps_e[0:64, :]), (64, 80, ps_q[0:16, 0:512]),
                                   (80, 96, ps_mo[0:16, 0:512])]:
                    for ic in range(NIT):
                        t.matmul(pp, wxpT_bf[:, ic * 96 + o0:ic * 96 + o1],
                                 xs_bf[:, ic * CH:(ic + 1) * CH],
                                 start=(ic == 0), stop=(ic == NIT - 1))
                tinc(t)
            on("tensor", ftp)
            m_xp = sT.v

            def fvp(v, m=m_xp):
                v.wait_ge(sT.h, m)
                v.tensor_copy(dtlowT_bf[0:64, :], ps_e[0:64, :])
                v.tensor_copy(B_bf[0:16, :], ps_q[0:16, 0:512])
                v.tensor_copy(C_bf[0:16, :], ps_mo[0:16, 0:512])
                v.tensor_scalar(stat[0:16, 0:1], w_col[0:16, 0:1], -1.0, 1.0,
                                ALU.mult, ALU.add)
                vedge(v)
                v.tensor_scalar_mul(C_bf[0:16, :], C_bf[0:16, :], stat[0:16, 0:1])
                vinc(v)
            on("vector", fvp)
            M["bc"] = sV.v

            rep_ev = []
            for n in range(NST):
                def ftr(t, n=n):
                    if n == 0:
                        t.wait_ge(sV.h, M["bc"])
                    else:
                        t.wait_ge(sS.h, rep_ev[n - 1])
                    t.matmul(ps_e[:, :], oneh[0:16, n * 128:(n + 1) * 128],
                             B_bf[0:16, :], start=True, stop=True)
                    t.matmul(ps_q[:, 0:512], oneh[0:16, n * 128:(n + 1) * 128],
                             C_bf[0:16, :], start=True, stop=True)
                    tinc(t)
                on("tensor", ftr)
                m_r = sT.v

                def fsr(s, m=m_r, n=n):
                    s.wait_ge(sT.h, m)
                    s.activation(Brep[:, n * CH:(n + 1) * CH], ps_e[:, :], AF.Copy)
                    s.activation(Crep[:, n * CH:(n + 1) * CH], ps_q[:, 0:512], AF.Copy)
                    sinc(s)
                on("scalar", fsr)
                rep_ev.append(sS.v)
            M["reps"] = sS.v

        def emit_scan_init(c, M):
            M["delta"] = []
            M["gmm"] = []
            M["silu_g"] = []
            M["gated"] = []
            M["g_rd"] = []
            return {"dA_m": {}, "scan_m": {}, "ro_m": {}}

        def emit_scan_head(c, M, st, it):
            dA_m, scan_m, ro_m = st["dA_m"], st["scan_m"], st["ro_m"]
            ps_dt = ps_mo[:, 512:1024]
            if True:
                def ftd(t, it=it):
                    if it == 0:
                        t.wait_ge(sS.h, M["reps"])
                        t.wait_ge(sS.h, M["q_evac_s"][1])
                    else:
                        t.wait_ge(sV.h, M["delta"][it - 1])
                    t.matmul(ps_dt, wdtT_bf[0:64, it * 128:(it + 1) * 128],
                             dtlowT_bf[0:64, :], start=True, stop=True)
                    tinc(t)
                on("tensor", ftd)
                m_dt = sT.v
                dpar = delta_b[it % 4]

                def fvd(v, m=m_dt, it=it, dpar=dpar):
                    v.wait_ge(sT.h, m)
                    if it >= 4:
                        v.wait_ge(sS.h, dA_m[(it - 4, NST - 1)])
                    v.tensor_scalar_add(dpar[:, :], ps_dt, bdt_sb[:, it:it + 1])
                    vinc(v)
                on("vector", fvd)
                m_z = sV.v

                def fsd(s, m=m_z, dpar=dpar, it=it):
                    s.wait_ge(sV.h, m)
                    if it > 0:
                        s.wait_ge(sV.h, M["silu_gv_l"][it - 1])
                    s.activation(sp_tmp[:, :], dpar[:, :], AF.Abs)
                    sedge(s)
                    s.activation(sp_tmp[:, :], sp_tmp[:, :], AF.Exp, scale=-1.0)
                    sedge(s)
                    s.activation(sp_tmp[:, :], sp_tmp[:, :], AF.Ln, bias=1.0)
                    sinc(s)
                on("scalar", fsd)
                m_sp = sS.v

                def fvz(v, m=m_sp, dpar=dpar, it=it):
                    v.wait_ge(sS.h, m)
                    v.scalar_tensor_tensor(dpar[:, :], dpar[:, :], 0.0, sp_tmp[:, :],
                                           ALU.max, ALU.add)
                    vedge(v)
                    v.tensor_tensor(du_b[:, :], dpar[:, :],
                                    xs_bf[:, it * CH:(it + 1) * CH], ALU.mult)
                    vinc(v)
                on("vector", fvz)
                m_du = sV.v
                M["delta"].append(sV.v)

                def fqg(q, it=it):
                    if it >= 2:
                        q.wait_ge(sT.h, M["gmm"][it - 2])
                    M["g_rd"].append(stWing.dma(
                        q, wing_sl[it % 2][:, :].rearrange("p (dc o) -> p dc o", dc=NDC),
                        sc_win[:, :, 2560 + it * 128:2560 + it * 128 + 128].rearrange("dc p o -> p dc o")))
                on("sync", fqg)

                def ftg(t, it=it):
                    hh, vv = M["g_rd"][it]
                    t.wait_ge(hh, vv)
                    if it == 0:
                        t.wait_ge(sS.h, M["reps"])
                        t.wait_ge(sV.h, M["bc"])
                    else:
                        t.wait_ge(sV.h, M["silu_gv_l"][it - 1])
                    for dc in range(NDC):
                        t.matmul(ps_mo[:, 0:512], wing_sl[it % 2][:, dc * 128:(dc + 1) * 128],
                                 xnT_ch[:, dc * CH:(dc + 1) * CH],
                                 start=(dc == 0), stop=(dc == NDC - 1))
                    tinc(t)
                on("tensor", ftg)
                M["gmm"].append(sT.v)
                m_g = sT.v

                def fsg(s, m=m_g, it=it):
                    s.wait_ge(sT.h, m)
                    s.wait_ge(sV.h, M["delta"][it])
                    if it > 3:
                        s.wait_ge(sV.h, M["gated"][it - 4])
                    s.activation(sp_tmp[:, :], ps_mo[:, 0:512], AF.Tanh, scale=0.5)
                    sinc(s)
                on("scalar", fsg)
                m_th = sS.v

                def fvsg(v, m=m_th, it=it):
                    v.wait_ge(sS.h, m)
                    v.tensor_scalar(cvt[:, :], sp_tmp[:, :], 0.5, 0.5, ALU.mult, ALU.add)
                    vedge(v)
                    v.tensor_tensor(silu_g[it % 4][:, :], ps_mo[:, 0:512], cvt[:, :],
                                    ALU.mult)
                    vinc(v)
                on("vector", fvsg)
                M["silu_g"].append(sS.v)
                M.setdefault("silu_gv_l", []).append(sV.v)
                st[("mdu", it)] = m_du
                st[("dpar", it)] = dpar

        def emit_scan_n(c, M, st, it, n):
            dA_m, scan_m, ro_m = st["dA_m"], st["scan_m"], st["ro_m"]
            m_du = st[("mdu", it)]
            dpar = st[("dpar", it)]
            if True:
                if True:
                    par = n % 4

                    def fsa(s, it=it, n=n, par=par, dpar=dpar, m_du=m_du, c=c):
                        s.wait_ge(sV.h, m_du)
                        if n >= 4:
                            s.wait_ge(sV.h, scan_m[(it, n - 4)])
                        elif it >= 1:
                            s.wait_ge(sV.h, scan_m[(it - 1, NST - 4 + n)])
                        elif c >= 1:
                            s.wait_ge(sV.h, M1[c - 1]["scan_last4"][n])
                        s.activation(dA_b[par][:, :], dpar[:, :], AF.Exp,
                                     scale=A_sb[:, it * NST + n:it * NST + n + 1])
                        sinc(s)
                    on("scalar", fsa)
                    dA_m[(it, n)] = sS.v
                    m_da = sS.v

                    def fvs(v, it=it, n=n, par=par, m_da=m_da):
                        v.wait_ge(sS.h, m_da)
                        v.tensor_tensor(dBu_b[n % 2][:, :], du_b[:, :],
                                        Brep[:, n * CH:(n + 1) * CH], ALU.mult)
                        vedge(v)
                        v.tensor_tensor_scan(
                            h_stk[:, n * CH:(n + 1) * CH],
                            dA_b[par][:, :], dBu_b[n % 2][:, :],
                            scan_carry[:, it * NST + n:it * NST + n + 1],
                            ALU.mult, ALU.add)
                        vedge(v)
                        v.tensor_copy(scan_carry[:, it * NST + n:it * NST + n + 1],
                                      h_stk[:, n * CH + CH - 1:n * CH + CH])
                        v.tensor_tensor(h_stk[:, n * CH:(n + 1) * CH],
                                        h_stk[:, n * CH:(n + 1) * CH],
                                        Crep[:, n * CH:(n + 1) * CH], ALU.mult)
                        vinc(v)
                    on("vector", fvs)
                    scan_m[(it, n)] = sV.v

        def emit_scan_tail(c, M, st, it):
            dA_m, scan_m, ro_m = st["dA_m"], st["scan_m"], st["ro_m"]
            if True:
                def fvtree(v, it=it):
                    for width in (8 * CH, 4 * CH, 2 * CH, CH):
                        v.tensor_tensor(h_stk[:, 0:width], h_stk[:, 0:width],
                                        h_stk[:, width:2 * width], ALU.add)
                        vedge(v)
                    v.scalar_tensor_tensor(cvt[:, :], xs_bf[:, it * CH:(it + 1) * CH],
                                           Dw_sb[:, it:it + 1], h_stk[:, 0:CH],
                                           ALU.mult, ALU.add)
                    vedge(v)
                    v.tensor_tensor(xs_bf[:, it * CH:(it + 1) * CH], cvt[:, :],
                                    silu_g[it % 4][:, :], ALU.mult)
                    vinc(v)
                on("vector", fvtree)
                M["gated"].append(sV.v)

        def emit_tail(c, M):
            mo_rd = []
            M["momm"] = []
            M["mo_evac"] = []
            for ot in range(NDC):
                def fqm(q, ot=ot):
                    if ot >= 2:
                        q.wait_ge(sT.h, M["momm"][ot - 2])
                    mo_rd.append(stWmo.dma(
                        q, wmo_sl[ot % 2][:, :].rearrange("p (ic o) -> p ic o", ic=NIT),
                        sc_wmo[:, :, ot * 128:(ot + 1) * 128].rearrange("ic p o -> p ic o")))
                on("sync", fqm)

                def ftm(t, ot=ot):
                    hh, vv = mo_rd[ot]
                    t.wait_ge(hh, vv)
                    if ot == 0:
                        t.wait_ge(sV.h, M["gated"][NIT - 1])
                    else:
                        t.wait_ge(sV.h, M["mo_evac"][ot - 1])
                    for ic in range(NIT):
                        t.matmul(ps_e[:, :], wmo_sl[ot % 2][:, ic * 128:(ic + 1) * 128],
                                 xs_bf[:, ic * CH:(ic + 1) * CH],
                                 start=(ic == 0), stop=(ic == NIT - 1))
                    tinc(t)
                on("tensor", ftm)
                M["momm"].append(sT.v)
                m_mo = sT.v

                def fvm(v, m=m_mo, ot=ot):
                    v.wait_ge(sT.h, m)
                    v.tensor_tensor(mergedT[:, ot * CH:(ot + 1) * CH],
                                    mergedT[:, ot * CH:(ot + 1) * CH],
                                    ps_e[:, :], ALU.add)
                    vinc(v)
                on("vector", fvm)
                M["mo_evac"].append(sV.v)
            M["merged"] = sV.v

            pr_rd = []
            M["prmm"] = []
            M["epi_wr"] = []
            prev_epi = M1[c - 1]["epi_wr"] if c >= 1 else None
            for pr in range(2):
                for og in range(2):
                    for dc in range(NDC):
                        i = pr * 16 + og * NDC + dc

                        def fqp(q, i=i, og=og, dc=dc, c=c):
                            if i >= 2:
                                q.wait_ge(sT.h, M["prmm"][i - 2])
                            else:
                                q.wait_ge(sT.h, M["a_sl_done"])
                            pr_rd.append(stWq.dma(q, wq_sl[i % 2][:, :],
                                                  sc_wpr[dc, :, og * 512:og * 512 + 512]))
                        on("sync", fqp)

                        def ftpj(t, i=i, og=og, dc=dc, pr=pr, c=c):
                            hh, vv = pr_rd[i]
                            t.wait_ge(hh, vv)
                            if dc == 0 and og == 0:
                                t.wait_ge(sV.h, M["merged"])
                                if pr == 0 and c >= 1:
                                    t.wait_ge(sV.h, M1[c - 1]["epi_v"][1])
                                elif pr == 1:
                                    t.wait_ge(sV.h, M["epi_v"][0])
                            for tj in range(2):
                                tt = 2 * pr + tj
                                pt = (ps_q if tj == 0 else ps_mo)[:, og * 512:og * 512 + 512]
                                t.matmul(pt,
                                         mergedT[:, dc * CH + tt * 128: dc * CH + tt * 128 + 128],
                                         wq_sl[i % 2][:, :],
                                         start=(dc == 0), stop=(dc == NDC - 1))
                            tinc(t)
                        on("tensor", ftpj)
                        M["prmm"].append(sT.v)
                m_pr = sT.v

                for tj in range(2):
                    tt = 2 * pr + tj

                    def fqx(q, tt=tt, pr=pr, c=c):
                        ei = len(M["epi_wr"])
                        if pr >= 1:
                            hh, vv = M["epi_wr"][tt - 2]
                            q.wait_ge(hh, vv)
                        elif c >= 1:
                            hh, vv = prev_epi[tt + 2]
                            q.wait_ge(hh, vv)
                        r0 = c * CH + tt * 128
                        M.setdefault("x_rd", []).append(
                            stEpi.dma(q, xbuf[tt % 2][:, :], x_h[r0:r0 + 128, :]))
                    on("sync", fqx)

                    def fve(v, tt=tt, tj=tj, m_pr=m_pr):
                        v.wait_ge(sT.h, m_pr)
                        hh, vv = M["x_rd"][tt]
                        v.wait_ge(hh, vv)
                        pt = ps_q if tj == 0 else ps_mo
                        v.tensor_tensor(out_st[tt % 2][:, :], xbuf[tt % 2][:, :],
                                        pt[:, :], ALU.add)
                        vinc(v)
                    on("vector", fve)
                    m_eo = sV.v

                    def fqo(q, tt=tt, c=c, m=m_eo):
                        q.wait_ge(sV.h, m)
                        r0 = c * CH + tt * 128
                        M["epi_wr"].append(
                            stOut.dma(q, out_h[r0:r0 + 128, :], out_st[tt % 2][:, :]))
                    on("sync", fqo)
                M.setdefault("epi_v", []).append(sV.v)
            M["ps_free"] = sV.v
            M["prmm_last"] = sT.v
            M["xnT_done"] = sT.v
            M["chunk_done"] = sV.v

            if "merged" in d_dbg:
                def fdm(q, c=c):
                    q.wait_ge(sV.h, M["merged"])
                    stDbg.dma(q, d_dbg["merged"][:, :, c * CH:(c + 1) * CH],
                              mergedT[:, :].rearrange("p (dc t) -> p dc t", dc=NDC))
                on("sync", fdm)
            if "gt" in d_dbg:
                def fdg(q, c=c):
                    q.wait_ge(sV.h, M["gated"][NIT - 1])
                    stDbg.dma(q, d_dbg["gt"][:, :, c * CH:(c + 1) * CH],
                              xs_bf[:, :].rearrange("p (it t) -> p it t", it=NIT))
                on("sync", fdg)
            if "xs" in d_dbg:
                def fdx(q, c=c):
                    q.wait_ge(sS.h, M["xs_done"][NIT - 1])
                    stDbg.dma(q, d_dbg["xs"][:, :, c * CH:(c + 1) * CH],
                              xs_bf[:, :].rearrange("p (it t) -> p it t", it=NIT))
                on("sync", fdx)
            if "bc" in d_dbg:
                def fdb(q, c=c):
                    q.wait_ge(sV.h, M["bc"])
                    stDbg.dma(q, d_dbg["bc"][0:16, c * CH:(c + 1) * CH], B_bf[0:16, :])
                    stDbg.dma(q, d_dbg["bc"][16:32, c * CH:(c + 1) * CH], C_bf[0:16, :])
                on("sync", fdb)
            if "qt" in d_dbg:
                def fdq(q, c=c):
                    q.wait_ge(sV.h, M["qrow"])
                    stDbg.dma(q, d_dbg["qt"][0:65, :, c * CH:(c + 1) * CH],
                              qT_sb[0:65, :].rearrange("p (h t) -> p h t", h=H))
                on("sync", fdq)

        def phase3():
            af.pop(); ab.pop()
            af.push(); ab.push()
            w1s = ab.take(NDC * IFF)
            w3s = ab.take(NDC * IFF)
            w2s = ab.take(NIT * DM)
            xn3 = [ab.take(NDC * CH) for _ in range(2)]
            hT = ab.take(NIT * CH)
            st3 = [af.take(DM) for _ in range(2)]
            cv3 = ab.take(2 * CH)

            last_chunk_v = M1[nch - 1]["chunk_done"]
            out_tot = stOut.totals()
            M3 = {}
            mm_w = {}
            evp = {0: 0, 1: 0}
            accp = {0: None, 1: None}
            xn_m = {}

            for e in range(NE):
                def fgw(q, e=e):
                    if e == 0:
                        q.wait_ge(sV.h, last_chunk_v)
                        for hh, vv in out_tot:
                            q.wait_ge(hh, vv)
                    else:
                        q.wait_ge(sT.h, mm_w[e - 1])
                    stW1.dma(q, w1s[:, :].rearrange("p (dc i) -> p dc i", dc=NDC),
                             w1_h[e].rearrange("(dc p) i -> p dc i", p=128))
                    stW3.dma(q, w3s[:, :].rearrange("p (dc i) -> p dc i", dc=NDC),
                             w3_h[e].rearrange("(dc p) i -> p dc i", p=128))
                    stW2.dma(q, w2s[:, :].rearrange("p (ic o) -> p ic o", ic=NIT),
                             w2_h[e].rearrange("(ic p) o -> p ic o", p=128))
                on("gpsimd", fgw)
                w_rdv = [(st.ss[0].h, st.ss[0].v) for st in (stW1, stW3, stW2)]

                for tc in range(nch):
                    def fq3(q, e=e, tc=tc):
                        key = (e, tc - 2) if tc >= 2 else ((e - 1, nch - 2 + tc) if e >= 1 else None)
                        if key is not None and key in mm_w:
                            q.wait_ge(sT.h, mm_w[key])
                        elif key is not None:
                            q.wait_ge(sT.h, mm_w[e - 1])
                        xn_m[(e, tc)] = stXn3.dma(
                            q, xn3[tc % 2][:, :].rearrange("p (dc t) -> p dc t", dc=NDC),
                            sc_xnT[:, :, tc * CH:(tc + 1) * CH].rearrange("dc p t -> p dc t"))
                    on("sync", fq3)

                    h_m = []
                    cvp = [cv3[:, 0:CH], cv3[:, CH:2 * CH]]
                    for ic in range(NIT):
                        pp = ps_q if ic % 2 == 0 else ps_y
                        def ft1(t, e=e, tc=tc, ic=ic, pp=pp):
                            if ic == 0:
                                hh, vv = xn_m[(e, tc)]
                                t.wait_ge(hh, vv)
                                for hh2, vv2 in w_rdv[0:2]:
                                    t.wait_ge(hh2, vv2)
                                if e == 0 and tc == 0:
                                    t.wait_ge(sV.h, last_chunk_v)
                                if evp[0] > 0:
                                    t.wait_ge(sV.h, evp[0])
                                if evp[1] > 0:
                                    t.wait_ge(sV.h, evp[1])
                            if ic >= 2:
                                t.wait_ge(sV.h, h_m[ic - 2])
                            for dc in range(NDC):
                                t.matmul(pp[:, 0:512],
                                         w1s[:, dc * IFF + ic * 128: dc * IFF + ic * 128 + 128],
                                         xn3[tc % 2][:, dc * CH:(dc + 1) * CH],
                                         start=(dc == 0), stop=(dc == NDC - 1))
                            for dc in range(NDC):
                                t.matmul(pp[:, 512:1024],
                                         w3s[:, dc * IFF + ic * 128: dc * IFF + ic * 128 + 128],
                                         xn3[tc % 2][:, dc * CH:(dc + 1) * CH],
                                         start=(dc == 0), stop=(dc == NDC - 1))
                            tinc(t)
                        on("tensor", ft1)
                        m1 = sT.v

                        def fs1(s, m=m1, ic=ic, pp=pp):
                            s.wait_ge(sT.h, m)
                            if ic >= 2:
                                s.wait_ge(sV.h, h_m[ic - 2])
                            s.activation(cvp[ic % 2], pp[:, 0:512], AF.Silu)
                            sinc(s)
                        on("scalar", fs1)
                        m_si = sS.v

                        def fv1(v, m=m_si, ic=ic, pp=pp):
                            v.wait_ge(sS.h, m)
                            v.tensor_tensor(hT[:, ic * CH:(ic + 1) * CH], cvp[ic % 2],
                                            pp[:, 512:1024], ALU.mult)
                            vinc(v)
                        on("vector", fv1)
                        h_m.append(sV.v)

                    for ts_ in range(NTT):
                        py = ps_q if ts_ % 2 == 0 else ps_mo
                        par = ts_ % 2

                        def ft2(t, e=e, tc=tc, ts_=ts_, py=py, par=par):
                            t.wait_ge(sV.h, h_m[NIT - 1])
                            hh, vv = w_rdv[2]
                            t.wait_ge(hh, vv)
                            if evp[par] > 0:
                                t.wait_ge(sV.h, evp[par])
                            for half in range(2):
                                for ic in range(NIT):
                                    t.matmul(py[:, half * 512:(half + 1) * 512],
                                             hT[:, ic * CH + ts_ * 128: ic * CH + ts_ * 128 + 128],
                                             w2s[:, ic * DM + half * 512: ic * DM + half * 512 + 512],
                                             start=(ic == 0), stop=(ic == NIT - 1))
                            tinc(t)
                        on("tensor", ft2)
                        m2 = sT.v

                        def fv2(v, m=m2, e=e, tc=tc, ts_=ts_, py=py, par=par):
                            v.wait_ge(sT.h, m)
                            if accp[par] is not None:
                                hh, vv = accp[par]
                                v.wait_ge(hh, vv)
                            ti = tc * NTT + ts_
                            v.tensor_scalar_mul(st3[par][:, :], py[:, :],
                                                c_coef[:, ti * NE + e: ti * NE + e + 1])
                            vinc(v)
                        on("vector", fv2)
                        evp[par] = sV.v
                        m_ev = sV.v

                        def fga(q, tc=tc, ts_=ts_, par=par, m=m_ev):
                            q.wait_ge(sV.h, m)
                            r0 = tc * CH + ts_ * 128
                            accp[par] = stAcc.dma(q, out_h[r0:r0 + 128, :],
                                                  st3[par][:, :], accum_op=ALU.add)
                        on("gpsimd", fga)
                    mm_w[(e, tc)] = sT.v
                    flush()
                mm_w[e] = sT.v

        phase_pre()
        wsT = phase0_weights()
        flush()
        phase0()
        if upto != "s1":
            for c in range(nch):
                chunk1(c)
                flush()
            if upto == "full":
                phase3()

        if "xn" in d_dbg or "c_coef" in d_dbg:
            def fdbg0(q):
                q.wait_ge(sV.h, sV.v)
                if "c_coef" in d_dbg:
                    stDbg.dma(q, d_dbg["c_coef"][:, 0:nch * NTT * NE],
                              c_coef[:, 0:nch * NTT * NE])
                if "xn" in d_dbg:
                    for hh, vv in stXnW.totals():
                        q.wait_ge(hh, vv)
                    stDbg.dma(q, d_dbg["xn"][:, :, 0:nch * CH],
                              sc_xnT[:, :, 0:nch * CH])
            on("sync", fdbg0)

        def ffin(q):
            q.wait_ge(sV.h, sV.v)
            q.wait_ge(sS.h, sS.v)
            q.wait_ge(sT.h, sT.v)
            for st_ in (stOut, stAcc, stDbg, stXnW, stKW, stWS):
                for hh, vv in st_.totals():
                    q.wait_ge(hh, vv)
        on("sync", ffin)
        flush()

    return nc, d_in, d_dbg



def _prep_inputs(kw):
    x = np.asarray(kw["L0_attn_x"])
    t = np.arange(S, dtype=np.float64)
    inv = 1.0 / (10000.0 ** (np.arange(0, ROPE_, 2, dtype=np.float64) / ROPE_))
    fr = np.outer(t, inv)
    cos_rep = np.tile(np.cos(fr), (1, H)).astype(ml_dtypes.bfloat16)
    sin_rep = np.tile(np.sin(fr), (1, H)).astype(ml_dtypes.bfloat16)

    ident = np.eye(128, dtype=ml_dtypes.bfloat16)
    ones1 = np.ones((1, 128), dtype=ml_dtypes.bfloat16)
    oneh = np.zeros((16, 16 * 128), dtype=ml_dtypes.bfloat16)
    for n in range(16):
        oneh[n, n * 128:(n + 1) * 128] = 1.0
    m4 = np.zeros((4, 128, CH), dtype=ml_dtypes.bfloat16)
    p = np.arange(128)[:, None]
    q = np.arange(CH)[None, :]
    for j in range(4):
        m4[j] = (q >= p + 128 * j).astype(ml_dtypes.bfloat16)

    def lay_pit(v, k):
        return np.ascontiguousarray(
            v.reshape(NIT, 128, k).transpose(1, 0, 2).reshape(128, NIT * k))

    com = dict(
        wcqT=np.ascontiguousarray(np.asarray(kw["L0_attn_W_cq"], np.float32).T),
        winT=np.ascontiguousarray(np.asarray(kw["L0_attn_W_in"], np.float32).T),
        wxpT=np.ascontiguousarray(np.asarray(kw["L0_attn_W_xproj"], np.float32).T),
        wdtT=np.ascontiguousarray(np.asarray(kw["L0_attn_W_dt"], np.float32).T),
        wmoT=np.ascontiguousarray(np.asarray(kw["L0_attn_W_mout"], np.float32).T),
        wprT=np.ascontiguousarray(np.asarray(kw["L0_attn_W_proj"], np.float32).T),
        alog=lay_pit(np.asarray(kw["L0_attn_A_log"], np.float32), NST),
        cw=lay_pit(np.asarray(kw["L0_attn_conv_w"], np.float32), 4),
        cb=lay_pit(np.asarray(kw["L0_attn_conv_b"], np.float32), 1),
        bdt=lay_pit(np.asarray(kw["L0_attn_b_dt"], np.float32), 1),
        dpar=lay_pit(np.asarray(kw["L0_attn_D_param"], np.float32), 1),
        grep=np.tile(np.asarray(kw["L0_attn_q_gain"], np.float32)[None, :], (128, 1)),
        alpha=np.tile(np.asarray(kw["L0_attn_merge_alpha"], np.float32)[None, :], (128, 1)),
        wr=np.ascontiguousarray(np.asarray(kw["L0_ffn_Wr"], np.float32)),
        br=np.tile(np.asarray(kw["L0_ffn_br"], np.float32)[None, :], (128, 1)),
        w1=np.ascontiguousarray(np.asarray(kw["L0_ffn_W1"], np.float32)),
        w3=np.ascontiguousarray(np.asarray(kw["L0_ffn_W3"], np.float32)),
        w2=np.ascontiguousarray(np.asarray(kw["L0_ffn_W2"], np.float32)),
        cosr=cos_rep, sinr=sin_rep,
        ident=ident, ones1=ones1, oneh=oneh, mask4=m4,
    )
    outs = []
    for b in range(8):
        d = dict(com)
        d["x"] = np.ascontiguousarray(x[b].astype(np.float32))
        outs.append(d)
    return outs


def kernel(**kw):
    in_maps = _prep_inputs(kw)
    nc, d_in, d_dbg = build(upto="full")
    res = run_bass_kernel_spmd(nc, in_maps, core_ids=list(range(8)))
    return np.stack([r["out"] for r in res.results], axis=0)


# version 11
# speedup vs baseline: 1.0259x; 1.0259x over previous
import contextlib
import numpy as np
import ml_dtypes

import concourse.bass as bass
import concourse.mybir as mybir
from concourse.bass_utils import run_bass_kernel_spmd

F32 = mybir.dt.float32
BF16 = mybir.dt.bfloat16
AF = mybir.ActivationFunctionType
ALU = mybir.AluOpType
AX = mybir.AxisListType

S = 4096
DM = 1024
INNER = 2048
H, KVH, HD = 16, 4, 64
NST = 16
DTR = 64
NE, IFF = 8, 2048
CH = 512
NCH = S // CH
NTT = CH // 128
NIT = INNER // 128
NDC = DM // 128
LN_EPS = 1e-5
RMS_EPS = float(np.finfo(np.float32).eps)
NKP = S // 128
ROPE_ = 32


class Ctr:
    def __init__(self, h):
        self.h = h
        self.v = 0


class DStream:
    def __init__(self, nc, es, name, nbuf):
        self.ss = [Ctr(es.enter_context(nc.semaphore(f"{name}{i}")))
                   for i in range(nbuf)]
        self.k = 0

    def dma(self, q, out, in_, **kw):
        s = self.ss[self.k % len(self.ss)]
        self.k += 1
        if s.v > 0:
            q.wait_ge(s.h, s.v)
        q.dma_start(out, in_, **kw).then_inc(s.h, 16)
        s.v += 16
        return (s.h, s.v)

    def totals(self):
        return [(s.h, s.v) for s in self.ss if s.v > 0]


class _Ar:
    def __init__(self, nc, es, name, n, dt):
        self.t = es.enter_context(nc.sbuf_tensor(name, [128, n], dt))
        self.off = 0
        self.cap = n
        self.marks = []

    def take(self, n):
        n = int(n)
        assert self.off + n <= self.cap, f"arena overflow {self.off}+{n}>{self.cap}"
        ap = self.t[:, self.off:self.off + n]
        self.off += n
        return ap

    def push(self):
        self.marks.append(self.off)

    def pop(self):
        self.off = self.marks.pop()


def build(upto="full", dbg=(), nch=NCH, race_check=False):
    nc = bass.Bass("TRN2", target_bir_lowering=False,
                   detect_race_conditions=race_check)
    dbg = set(dbg)

    d_in = {}
    def din(name, shape, dt=F32):
        d_in[name] = nc.dram_tensor(name, list(shape), dt, kind="ExternalInput").ap()
        return d_in[name]

    x_h = din("x", [S, DM])
    wcqT_h = din("wcqT", [DM, DM])
    winT_h = din("winT", [DM, 4608])
    wxpT_h = din("wxpT", [INNER, 96])
    wdtT_h = din("wdtT", [DTR, INNER])
    wmoT_h = din("wmoT", [INNER, DM])
    wprT_h = din("wprT", [DM, DM])
    alog_h = din("alog", [128, NIT * NST])
    cw_h = din("cw", [128, NIT * 4])
    cb_h = din("cb", [128, NIT])
    bdt_h = din("bdt", [128, NIT])
    dpar_h = din("dpar", [128, NIT])
    grep_h = din("grep", [128, H])
    alpha_h = din("alpha", [128, 1])
    wr_h = din("wr", [DM, NE])
    br_h = din("br", [128, NE])
    w1_h = din("w1", [NE, DM, IFF])
    w3_h = din("w3", [NE, DM, IFF])
    w2_h = din("w2", [NE, IFF, DM])
    cos_h = din("cosr", [S, 256], BF16)
    sin_h = din("sinr", [S, 256], BF16)
    ident_h = din("ident", [128, 128], BF16)
    ones1_h = din("ones1", [1, 128], BF16)
    oneh_h = din("oneh", [16, 16 * 128], BF16)
    mask4_h = din("mask4", [4, 128, CH], BF16)

    out_h = nc.dram_tensor("out", [S, DM], F32, kind="ExternalOutput").ap()

    d_dbg = {}
    def ddbg(name, shape, dt=F32):
        if name in dbg:
            d_dbg[name] = nc.dram_tensor("dbg_" + name, list(shape), dt,
                                         kind="ExternalOutput").ap()

    ddbg("xn", [NDC, 128, S], BF16)
    ddbg("c_coef", [128, (S // 128) * NE])
    ddbg("bc", [32, S], BF16)
    ddbg("gt", [128, NIT, S], BF16)
    ddbg("merged", [128, NDC, S], BF16)
    ddbg("qt", [128, H, S], BF16)
    ddbg("xs", [128, NIT, S], BF16)

    sc_xnT = nc.dram_tensor("sc_xnT", [NDC, 128, S], BF16, kind="Internal").ap()
    sc_kT = nc.dram_tensor("sc_kT", [NKP, 65, KVH, 128], BF16, kind="Internal").ap()
    sc_vh = nc.dram_tensor("sc_vh", [NKP, 128, KVH * 65], BF16, kind="Internal").ap()
    sc_wcq = nc.dram_tensor("sc_wcq", [NDC, 128, DM], BF16, kind="Internal").ap()
    sc_win = nc.dram_tensor("sc_win", [NDC, 128, 4608], BF16, kind="Internal").ap()
    sc_wpr = nc.dram_tensor("sc_wpr", [NDC, 128, DM], BF16, kind="Internal").ap()
    sc_wmo = nc.dram_tensor("sc_wmo", [NIT, 128, DM], BF16, kind="Internal").ap()

    es = contextlib.ExitStack()
    with es:
        afL = _Ar(nc, es, "aL32", 1984, F32)
        abL = _Ar(nc, es, "aLbf", 8704, BF16)
        af = _Ar(nc, es, "aR32", 10496, F32)
        ab = _Ar(nc, es, "aRbf", 72704, BF16)

        ident = abL.take(128)
        ones1 = abL.take(128)
        oneh = abL.take(16 * 128)
        mask4 = abL.take(4 * CH)
        wr_bf = abL.take(NDC * NE)
        wr_lo = abL.take(NDC * NE)
        wxpT_bf = abL.take(NIT * 96)
        wdtT_bf = abL.take(INNER)
        linv_bf = abL.take(CH)
        A_sb = afL.take(NIT * NST)
        cwv = afL.take(NIT * 4)
        cb_sb = afL.take(NIT)
        bdt_sb = afL.take(NIT)
        Dw_sb = afL.take(NIT)
        grep = afL.take(H)
        negg64 = afL.take(H)
        w_col = afL.take(1)
        br_rep = afL.take(NE)
        c_coef = afL.take((S // 128) * NE)
        scan_carry = afL.take(NIT * NST)
        conv_carry = afL.take(NIT * 3)
        stat = afL.take(16)
        stat2 = afL.take(16)
        linv = afL.take(CH)
        wr_f32 = afL.take(NDC * NE)

        af.push(); ab.push()
        xbuf = [af.take(DM) for _ in range(2)]
        junk = af.take(DM)
        q_sb = af.take(DM)
        kv_sb = af.take(256)
        yT_sb = af.take(CH)
        delta_b = [af.take(CH) for _ in range(4)]
        dA_b = [af.take(CH) for _ in range(2)]
        sp_tmp = af.take(CH)
        out_st = [af.take(DM) for _ in range(2)]

        xnT_ch = ab.take(NDC * CH)
        wq_sl = [ab.take(CH) for _ in range(2)]
        win_sl = [ab.take(NDC * 128) for _ in range(2)]
        wing_sl = [ab.take(NDC * 128) for _ in range(2)]
        wmo_sl = [ab.take(NIT * 128) for _ in range(2)]
        cosr = ab.take(NTT * 256)
        sinr = ab.take(NTT * 256)
        qT_sb = ab.take(H * CH)
        xn_bf = qT_sb[:, 0:DM]
        kT_ch = ab.take(KVH * CH)
        vh_st = ab.take(KVH * 65)
        kT_sl = [ab.take(KVH * 128) for _ in range(2)]
        vh_sl = [ab.take(KVH * 65) for _ in range(2)]
        probs = [ab.take(2 * CH) for _ in range(2)]
        mergedT = ab.take(NDC * CH)
        xin_bf = ab.take(CH + 3)
        cvt = ab.take(CH)
        xs_bf = ab.take(NIT * CH)
        dtlowT_bf = ab.take(CH)
        B_bf = ab.take(CH)
        C_bf = ab.take(CH)
        Brep = ab.take(NST * CH)
        Crep = ab.take(NST * CH)
        h_stk = ab.take(NST * CH)
        silu_g = [ab.take(CH) for _ in range(4)]
        du_b = ab.take(CH)
        dBu_b = [ab.take(CH) for _ in range(2)]
        wbounce = Brep
        xnloT_ch = h_stk[:, 0:NDC * CH]
        xnlo_t = Crep[:, 0:DM]

        ps_q = es.enter_context(nc.psum_tensor("ps_q", [128, 1024], F32))
        ps_mo = es.enter_context(nc.psum_tensor("ps_mo", [128, 1024], F32))
        ps_y = es.enter_context(nc.psum_tensor("ps_y", [128, 1024], F32))
        ps_e = ps_y[:, 0:512]
        ps_tt = es.enter_context(nc.psum_tensor("ps_tt", [128, 2048], BF16))
        ps_t = ps_tt[:, 0:128]
        ps_t2 = ps_tt[:, 1024:1152]

        sT = Ctr(es.enter_context(nc.semaphore("sT")))
        sV = Ctr(es.enter_context(nc.semaphore("sV")))
        sS = Ctr(es.enter_context(nc.semaphore("sS")))
        sG = Ctr(es.enter_context(nc.semaphore("sG")))

        stPre = DStream(nc, es, "pre", 2)
        stPreG = DStream(nc, es, "prg", 2)
        stX = DStream(nc, es, "x", 2)
        stXnW = DStream(nc, es, "xnw", 2)
        stXnR = DStream(nc, es, "xnr", 1)
        stWB = DStream(nc, es, "wb", 2)
        stWS = DStream(nc, es, "ws", 2)
        stWq = DStream(nc, es, "wq", 2)
        stWin = DStream(nc, es, "wi", 2)
        stWing = DStream(nc, es, "wg", 2)
        stWmo = DStream(nc, es, "wm", 2)
        stCos = DStream(nc, es, "cs", 2)
        stKW = DStream(nc, es, "kw", 2)
        stKR = DStream(nc, es, "kr", 2)
        stVR = DStream(nc, es, "vr", 2)
        stEpi = DStream(nc, es, "ep", 2)
        stOut = DStream(nc, es, "ou", 2)
        stDbg = DStream(nc, es, "db", 1)
        stW1 = DStream(nc, es, "w1s", 1)
        stW2 = DStream(nc, es, "w2s", 1)
        stW3 = DStream(nc, es, "w3s", 1)
        stXn3 = DStream(nc, es, "x3", 2)
        stAcc = DStream(nc, es, "ac", 2)

        blk = es.enter_context(nc.Block())

        class _Proxy:
            __slots__ = ("incs",)
            def __init__(self):
                self.incs = []

            def then_inc(self, sem, n):
                self.incs.append((sem, n))
                return self

        class _Rec:
            def __init__(self, calls):
                self.calls = calls
                self.last = None

            def __getattr__(self, n):
                def f(*a, **k):
                    p = _Proxy()
                    self.calls.append((n, a, k, p))
                    self.last = p
                    return p
                return f

        _onq = []

        def on(engname, fn):
            calls = []
            fn(_Rec(calls))
            if calls:
                _onq.append((engname, calls))

        def flush():
            if not _onq:
                return
            q = _onq[:]
            _onq.clear()
            for engname in ("sync", "gpsimd", "tensor", "scalar", "vector"):
                seq = [c for e, cs in q if e == engname for c in cs]
                if not seq:
                    continue

                def body(e, seq=seq):
                    for n, a, k, p in seq:
                        r = getattr(e, n)(*a, **k)
                        for sem, cnt in p.incs:
                            r.then_inc(sem, cnt)
                getattr(blk, engname)(body)

        def vinc(v):
            v.last.then_inc(sV.h, 1); sV.v += 1; return sV.v

        def sinc(s):
            s.last.then_inc(sS.h, 1); sS.v += 1; return sS.v

        def tinc(t):
            t.last.then_inc(sT.h, 1); sT.v += 1; return sT.v

        def sedge(s):
            s.last.then_inc(sS.h, 1); sS.v += 1
            s.wait_ge(sS.h, sS.v)

        def vedge(v):
            v.last.then_inc(sV.h, 1); sV.v += 1
            v.wait_ge(sV.h, sV.v)

        M1 = {}

        def phase_pre():
            def g(q):
                stPreG.dma(q, ident[:, :], ident_h)
                stPreG.dma(q, ones1[0:1, :], ones1_h)
                stPreG.dma(q, oneh[0:16, :].rearrange("a (b c) -> a b c", b=16),
                          oneh_h.rearrange("a (b c) -> a b c", b=16))
                stPreG.dma(q, mask4[:, :].rearrange("p (a c) -> p a c", a=4),
                          mask4_h.rearrange("a p c -> p a c"))
                stPreG.dma(q, wxpT_bf[:, :].rearrange("p (a e) -> p a e", a=NIT),
                          wxpT_h.rearrange("(a p) e -> p a e", p=128))
                stPreG.dma(q, wdtT_bf[0:64, :], wdtT_h)
            on("gpsimd", g)

            def q0(q):
                stPre.dma(q, A_sb[:, :], alog_h)
                stPre.dma(q, cwv[:, :], cw_h)
                stPre.dma(q, cb_sb[:, :], cb_h)
                stPre.dma(q, bdt_sb[:, :], bdt_h)
                stPre.dma(q, Dw_sb[:, :], dpar_h)
                stPre.dma(q, grep[:, :], grep_h)
                stPre.dma(q, w_col[:, :], alpha_h)
                stPre.dma(q, br_rep[:, :], br_h)
                stPre.dma(q, wr_f32[:, :].rearrange("p (a e) -> p a e", a=NDC),
                          wr_h.rearrange("(a p) e -> p a e", p=128))
            on("sync", q0)
            tot = stPre.totals() + stPreG.totals()

            def fs(s):
                for hh, vv in tot:
                    s.wait_ge(hh, vv)
                s.activation(A_sb[:, :], A_sb[:, :], AF.Exp)
                s.activation(w_col[:, :], w_col[:, :], AF.Exp, scale=-1.0)
                s.activation(negg64[:, :], grep[:, :], AF.Abs)
                sinc(s)
            on("scalar", fs)
            m = sS.v

            def fv(v, m=m):
                v.wait_ge(sS.h, m)
                v.tensor_scalar_mul(A_sb[:, :], A_sb[:, :], -1.0)
                v.tensor_scalar_mul(negg64[:, :], negg64[:, :], -64.0)
                v.tensor_scalar_add(w_col[:, :], w_col[:, :], 1.0)
                vedge(v)
                v.reciprocal(w_col[:, :], w_col[:, :])
                vedge(v)
                v.tensor_scalar(stat[:, 0:1], w_col[:, :], -1.0, 1.0, ALU.mult, ALU.add)
                vedge(v)
                v.tensor_scalar_mul(Dw_sb[:, :], Dw_sb[:, :], stat[:, 0:1])
                v.memset(scan_carry[:, :], 0.0)
                v.memset(conv_carry[:, :], 0.0)
                v.tensor_copy(wr_bf[:, :], wr_f32[:, :])
                vedge(v)
                v.tensor_tensor(wr_lo[:, :], wr_f32[:, :], wr_bf[:, :], ALU.subtract)
                vinc(v)
            on("vector", fv)

        def phase0_weights():
            jobs = []
            for p in range(NDC):
                jobs.append((wcqT_h[p * 128:(p + 1) * 128, :], sc_wcq[p, :, :], DM))
            for p in range(NDC):
                jobs.append((winT_h[p * 128:(p + 1) * 128, 0:2304],
                             sc_win[p, :, 0:2304], 2304))
                jobs.append((winT_h[p * 128:(p + 1) * 128, 2304:4608],
                             sc_win[p, :, 2304:4608], 2304))
            for p in range(NDC):
                jobs.append((wprT_h[p * 128:(p + 1) * 128, :], sc_wpr[p, :, :], DM))
            for p in range(NIT):
                jobs.append((wmoT_h[p * 128:(p + 1) * 128, :], sc_wmo[p, :, :], DM))

            ws_m = []
            early_tot = [None]
            for i, (src, dst, ncols) in enumerate(jobs):
                if i == 24 and early_tot[0] is None:
                    early_tot[0] = stWS.totals()
                bnc = wbounce[:, (i % 2) * 2304:(i % 2) * 2304 + ncols]

                def g(q, i=i, src=src, bnc=bnc):
                    if i >= 2:
                        hh, vv = ws_m[i - 2]
                        q.wait_ge(hh, vv)
                    stWB.dma(q, bnc[:, :], src)
                on("gpsimd", g)
                wb = stWB.ss[(stWB.k - 1) % 2]
                mwb = wb.v

                def q1(q, dst=dst, bnc=bnc, wb=wb, mwb=mwb):
                    q.wait_ge(wb.h, mwb)
                    ws_m.append(stWS.dma(q, dst, bnc[:, :]))
                on("sync", q1)
            return early_tot[0] or stWS.totals(), stWS.totals()

        p0 = {"x": {}, "ln": {}, "tp": {}, "evac": {}, "c": {}, "rt": {}, "wr": {}}

        def phase0():
            for c in range(nch):
                for tt in range(NTT):
                    ti = c * NTT + tt
                    xb = xbuf[tt % 2]

                    def fq(q, c=c, tt=tt, ti=ti):
                        if ti >= 2:
                            q.wait_ge(sV.h, p0["c"][ti - 2])
                        r0 = c * CH + tt * 128
                        p0["x"][ti] = stX.dma(q, xbuf[tt % 2][:, :], x_h[r0:r0 + 128, :])
                    on("sync", fq)

                    def fv(v, ti=ti, xb=xb):
                        hh, vv = p0["x"][ti]
                        v.wait_ge(hh, vv)
                        v.tensor_reduce(stat[:, 0:1], xb[:, :], AX.X, ALU.add, negate=True)
                        vedge(v)
                        v.tensor_scalar_mul(stat[:, 0:1], stat[:, 0:1], 1.0 / DM)
                        vinc(v)
                    on("vector", fv)
                    m_nm = sV.v

                    def fs(s, m=m_nm, xb=xb):
                        s.wait_ge(sV.h, m)
                        s.activation(junk[:, :], xb[:, :], AF.Square,
                                     bias=stat[:, 0:1], accum_out=stat2[:, 0:1])
                        sinc(s)
                    on("scalar", fs)
                    m_sq = sS.v

                    def fv2(v, m=m_sq):
                        v.wait_ge(sS.h, m)
                        v.tensor_scalar(stat2[:, 0:1], stat2[:, 0:1], 1.0 / DM, LN_EPS,
                                        ALU.mult, ALU.add)
                        vinc(v)
                    on("vector", fv2)
                    m_v2 = sV.v

                    def fs2(s, m=m_v2):
                        s.wait_ge(sV.h, m)
                        s.activation(stat2[:, 0:1], stat2[:, 0:1], AF.Ln)
                        sedge(s)
                        s.activation(stat2[:, 0:1], stat2[:, 0:1], AF.Exp, scale=-0.5)
                        sinc(s)
                    on("scalar", fs2)
                    m_rs = sS.v

                    def fv3(v, m=m_rs, ti=ti, xb=xb):
                        v.wait_ge(sS.h, m)
                        if ti >= 1:
                            v.wait_ge(sT.h, p0["tp"][ti - 1])
                        v.tensor_scalar(junk[:, :], xb[:, :], stat[:, 0:1],
                                        stat2[:, 0:1], ALU.add, ALU.mult)
                        vedge(v)
                        v.tensor_copy(xn_bf[:, :], junk[:, :])
                        vedge(v)
                        v.tensor_tensor(xnlo_t[:, :], junk[:, :], xn_bf[:, :],
                                        ALU.subtract)
                        vinc(v)
                    on("vector", fv3)
                    p0["ln"][ti] = sV.v
                    evac_base = sV.v

                    def ft(t, ti=ti, evac_base=evac_base):
                        t.wait_ge(sV.h, p0["ln"][ti])
                        for k in range(2 * NDC):
                            dc = k % NDC
                            srcb = xn_bf if k < NDC else xnlo_t
                            if k >= 2:
                                t.wait_ge(sV.h, evac_base + k - 1)
                            elif ti > 0:
                                t.wait_ge(sV.h, p0["evac"][ti - 1] - (1 - k))
                            t.transpose((ps_t if k % 2 == 0 else ps_t2)[:, :],
                                        srcb[:, dc * 128:(dc + 1) * 128], ident[:, :])
                            tinc(t)
                    on("tensor", ft)
                    p0["tp"][ti] = sT.v
                    m_tp0 = sT.v - 2 * NDC

                    def fv4(v, m0=m_tp0, tt=tt, c=c):
                        if tt == 0 and c >= 1:
                            hh, vv = p0["wr"][c - 1]
                            v.wait_ge(hh, vv)
                            v.wait_ge(sT.h, p0["rt"][c - 1])
                        for k in range(2 * NDC):
                            dc = k % NDC
                            dstb = xnT_ch if k < NDC else xnloT_ch
                            v.wait_ge(sT.h, m0 + k + 1)
                            v.tensor_copy(
                                dstb[:, dc * CH + tt * 128: dc * CH + tt * 128 + 128],
                                (ps_t if k % 2 == 0 else ps_t2)[:, :]).then_inc(sV.h, 1)
                            sV.v += 1
                    on("vector", fv4)
                    p0["evac"][ti] = sV.v

                    def ft2(t, tt=tt, ti=ti):
                        t.wait_ge(sV.h, p0["evac"][ti])
                        if ti >= 1:
                            t.wait_ge(sV.h, p0["c"][ti - 1])
                        for ji, (xT, wB) in enumerate(((xnT_ch, wr_bf),
                                                       (xnloT_ch, wr_bf),
                                                       (xnT_ch, wr_lo))):
                            for dc in range(NDC):
                                t.matmul(ps_e[:, 0:NE],
                                         xT[:, dc * CH + tt * 128: dc * CH + tt * 128 + 128],
                                         wB[:, dc * NE:(dc + 1) * NE],
                                         start=(ji == 0 and dc == 0),
                                         stop=(ji == 2 and dc == NDC - 1))
                        tinc(t)
                    on("tensor", ft2)
                    m_lg = sT.v
                    p0["rt"][c] = sT.v

                    def fv5(v, m=m_lg, ti=ti):
                        v.wait_ge(sT.h, m)
                        cc = c_coef[:, ti * NE:(ti + 1) * NE]
                        v.tensor_tensor(cc, ps_e[:, 0:NE], br_rep[:, :], ALU.add)
                        vedge(v)
                        v.tensor_reduce(stat[:, 1:2], cc, AX.X, ALU.max)
                        vedge(v)
                        v.tensor_single_scalar(stat[:, 8:16], cc, stat[:, 1:2], ALU.is_equal)
                        vedge(v)
                        v.scalar_tensor_tensor(junk[:, 0:NE], stat[:, 8:16], -1e30, cc,
                                               ALU.mult, ALU.add)
                        vedge(v)
                        v.tensor_reduce(stat[:, 2:3], junk[:, 0:NE], AX.X, ALU.max)
                        vedge(v)
                        v.tensor_tensor(stat[:, 3:4], stat[:, 2:3], stat[:, 1:2],
                                        ALU.subtract)
                        vinc(v)
                    on("vector", fv5)
                    m_r1 = sV.v

                    def fs3(s, m=m_r1):
                        s.wait_ge(sV.h, m)
                        s.activation(stat[:, 3:4], stat[:, 3:4], AF.Exp)
                        sinc(s)
                    on("scalar", fs3)
                    m_r2 = sS.v

                    def fv6(v, m=m_r2, ti=ti):
                        v.wait_ge(sS.h, m)
                        cc = c_coef[:, ti * NE:(ti + 1) * NE]
                        v.tensor_scalar_add(stat[:, 4:5], stat[:, 3:4], 1.0)
                        vedge(v)
                        v.reciprocal(stat[:, 4:5], stat[:, 4:5])
                        vedge(v)
                        v.tensor_tensor(stat[:, 5:6], stat[:, 3:4], stat[:, 4:5], ALU.mult)
                        v.tensor_single_scalar(junk[:, 0:NE], junk[:, 0:NE], stat[:, 2:3],
                                               ALU.is_equal)
                        v.tensor_scalar_mul(junk[:, 8:16], stat[:, 8:16], stat[:, 4:5])
                        vedge(v)
                        v.scalar_tensor_tensor(cc, junk[:, 0:NE], stat[:, 5:6],
                                               junk[:, 8:16], ALU.mult, ALU.add)
                        vinc(v)
                    on("vector", fv6)
                    p0["c"][ti] = sV.v

                def fq3(q, c=c):
                    q.wait_ge(sV.h, p0["evac"][c * NTT + NTT - 1])
                    for dc in range(NDC):
                        last = stXnW.dma(q, sc_xnT[dc, :, c * CH:(c + 1) * CH],
                                         xnT_ch[:, dc * CH:(dc + 1) * CH])
                    p0["wr"][c] = last
                on("sync", fq3)
                flush()

        def chunk1(c):
            K4 = NTT * (c + 1)
            M = {}
            M1[c] = M

            def fq(q, c=c):
                if c == 0:
                    for hh, vv in stXnW.totals():
                        q.wait_ge(hh, vv)
                    for hh, vv in wsT:
                        q.wait_ge(hh, vv)
                else:
                    q.wait_ge(sT.h, M1[c - 1]["xnT_done"])
                M["xnr"] = stXnR.dma(
                    q, xnT_ch[:, :].rearrange("p (dc t) -> p dc t", dc=NDC),
                    sc_xnT[:, :, c * CH:(c + 1) * CH].rearrange("dc p t -> p dc t"))
                if c >= 2:
                    q.wait_ge(sV.h, M1[c - 2]["rope_done"])
                stCos.dma(q, cosr[:, :].rearrange("p (tt f) -> p tt f", tt=NTT),
                          cos_h[c * CH:(c + 1) * CH, :].rearrange("(tt p) f -> p tt f", p=128))
                M["cos"] = stCos.dma(
                    q, sinr[:, :].rearrange("p (tt f) -> p tt f", tt=NTT),
                    sin_h[c * CH:(c + 1) * CH, :].rearrange("(tt p) f -> p tt f", p=128))
            on("sync", fq)
            m_cos = stCos.totals()

            kvq_sl = []
            rel_sl = []
            slabs = [("kv", dc, 0, sc_win[dc, :, 0:512]) for dc in range(NDC)] + \
                    [("q", dc, og, sc_wcq[dc, :, og * 512:og * 512 + 512])
                     for og in range(2) for dc in range(NDC)]
            for i, (kind, dc, og, src) in enumerate(slabs):
                def fqs(q, i=i, src=src, c=c):
                    if i >= 2:
                        q.wait_ge(sT.h, rel_sl[i - 2])
                    elif c >= 1:
                        q.wait_ge(sT.h, M1[c - 1]["prmm_last"])
                    kvq_sl.append(stWq.dma(q, wq_sl[i % 2][:, :], src))
                on("sync", fqs)
                grp = i // NDC

                def ftm(t, i=i, dc=dc, grp=grp, c=c):
                    hh, vv = kvq_sl[i]
                    t.wait_ge(hh, vv)
                    if dc == 0:
                        hh2, vv2 = M["xnr"]
                        t.wait_ge(hh2, vv2)
                        if grp == 0:
                            if c >= 1:
                                t.wait_ge(sV.h, M1[c - 1]["ps_free"])
                            else:
                                t.wait_ge(sV.h, p0["c"][nch * NTT - 1])
                        elif grp == 1:
                            t.wait_ge(sS.h, M["kv_evac_s"])
                        else:
                            t.wait_ge(sS.h, M["q_evac_s"][0])
                    for tt in range(NTT):
                        pt = (ps_q if tt < 2 else ps_mo)[:, (tt % 2) * 512:(tt % 2) * 512 + 512]
                        t.matmul(pt, xnT_ch[:, dc * CH + tt * 128: dc * CH + tt * 128 + 128],
                                 wq_sl[i % 2][:, :], start=(dc == 0), stop=(dc == NDC - 1))
                    tinc(t)
                on("tensor", ftm)
                rel_sl.append(sT.v)
                if grp == 0 and dc == NDC - 1:
                    M["kv_mm"] = sT.v
                    emit_kv_evac(c, M, m_cos)
                if grp == 1 and dc == NDC - 1:
                    M["q_mm0"] = sT.v
                    emit_q_evac(c, M, og=0)
                if grp == 2 and dc == NDC - 1:
                    M["q_mm1"] = sT.v
                    emit_q_evac(c, M, og=1)
            M["a_sl_done"] = rel_sl[-1]

            emit_qrow64(c, M)
            emit_mamba_front(c, M)
            att_state = emit_attention_init(c, M)
            scn_state = emit_scan_init(c, M)
            emit_scan_head(c, M, scn_state, 0)
            emit_scan_head(c, M, scn_state, 1)
            for hp in range(8):
                ki, ni = 0, 0
                heads_done = hp == 7
                while ki < K4 or ni < 32:
                    if not heads_done and ni >= 16:
                        emit_scan_head(c, M, scn_state, 2 * hp + 2)
                        emit_scan_head(c, M, scn_state, 2 * hp + 3)
                        heads_done = True
                    if ki < K4 and (ni * K4 >= ki * 32 or ni >= 32):
                        emit_attn_kp(c, M, att_state, hp, ki, K4)
                        ki += 1
                    else:
                        it, n = (2 * hp + ni // 16, ni % 16)
                        emit_scan_n(c, M, scn_state, it, n)
                        ni += 1
                if not heads_done:
                    emit_scan_head(c, M, scn_state, 2 * hp + 2)
                    emit_scan_head(c, M, scn_state, 2 * hp + 3)
                emit_attn_pass_end(c, M, att_state, hp)
                emit_scan_tail(c, M, scn_state, 2 * hp)
                emit_scan_tail(c, M, scn_state, 2 * hp + 1)
            M["attn_done"] = M["attn_done_v"]
            emit_tail(c, M)

        def emit_kv_evac(c, M, m_cos):
            M["ktp_evac"] = []
            for tt in range(NTT):
                pt = (ps_q if tt < 2 else ps_mo)[:, (tt % 2) * 512:(tt % 2) * 512 + 512]

                def fsv(s, pt=pt, tt=tt, c=c):
                    s.wait_ge(sT.h, M["kv_mm"])
                    for hh, vv in m_cos:
                        s.wait_ge(hh, vv)
                    if tt == 0 and c >= 1:
                        for hh, vv in M1[c - 1]["kT_wr"]:
                            s.wait_ge(hh, vv)
                    if tt >= 1:
                        hh, vv = M["vh_wr"][tt - 1]
                        s.wait_ge(hh, vv)
                        s.wait_ge(sV.h, M["kv_rope"][tt - 1])
                    s.copy(kv_sb[:, :], pt[:, 0:256])
                    for kv in range(KVH):
                        s.activation(vh_st[:, kv * 65:kv * 65 + 64],
                                     pt[:, 256 + kv * 64:256 + kv * 64 + 64], AF.Copy)
                    sinc(s)
                on("scalar", fsv)
                m_k = sS.v

                def fvv(v, m=m_k):
                    v.wait_ge(sS.h, m)
                    for kv in range(KVH):
                        v.memset(vh_st[:, kv * 65 + 64:kv * 65 + 65], 1.0)
                    v.tensor_tensor(junk[:, 0:256], kv_sb[:, :], kv_sb[:, :], ALU.mult)
                    vedge(v)
                    v.tensor_reduce(stat[:, 0:4],
                                    junk[:, 0:256].rearrange("p (h d) -> p h d", h=KVH),
                                    AX.X, ALU.add)
                    vedge(v)
                    v.tensor_scalar(stat[:, 0:4], stat[:, 0:4], 1.0 / HD, RMS_EPS,
                                    ALU.mult, ALU.add)
                    vinc(v)
                on("vector", fvv)
                m_ms = sV.v

                def fsr(s, m=m_ms):
                    s.wait_ge(sV.h, m)
                    s.activation(stat[:, 0:4], stat[:, 0:4], AF.Ln)
                    sedge(s)
                    s.activation(stat[:, 0:4], stat[:, 0:4], AF.Exp, scale=-0.5)
                    sinc(s)
                on("scalar", fsr)
                m_rs = sS.v

                def fvr(v, m=m_rs, tt=tt):
                    v.wait_ge(sS.h, m)
                    for kv in range(KVH):
                        v.tensor_scalar_mul(kv_sb[:, kv * 64:(kv + 1) * 64],
                                            kv_sb[:, kv * 64:(kv + 1) * 64],
                                            stat[:, kv:kv + 1])
                    kvv = kv_sb[:, :].rearrange("p (h d) -> p h d", h=KVH)
                    cv = cosr[:, tt * 256:(tt + 1) * 256].rearrange("p (h d) -> p h d", h=H)[:, 0:KVH, :]
                    sv = sinr[:, tt * 256:(tt + 1) * 256].rearrange("p (h d) -> p h d", h=H)[:, 0:KVH, :]
                    a = kvv[:, :, 0:16]; b = kvv[:, :, 16:32]
                    vedge(v)
                    j3 = lambda o: junk[:, o:o + 64].rearrange("p (h d) -> p h d", h=KVH)
                    v.tensor_tensor(j3(0), a, cv, ALU.mult)
                    v.tensor_tensor(j3(64), b, sv, ALU.mult)
                    v.tensor_tensor(j3(128), b, cv, ALU.mult)
                    v.tensor_tensor(j3(192), a, sv, ALU.mult)
                    vedge(v)
                    v.tensor_tensor(a, j3(0), j3(64), ALU.add)
                    v.tensor_tensor(b, j3(128), j3(192), ALU.subtract)
                    vedge(v)
                    if tt >= 1:
                        v.wait_ge(sT.h, M["ktp_t"][tt - 1])
                    v.tensor_copy(cvt[:, 0:256], kv_sb[:, :])
                    vinc(v)
                on("vector", fvr)
                M.setdefault("kv_rope", []).append(sV.v)
                m_kro = sV.v
                evb = sV.v

                def ftk(t, m=m_kro, tt=tt, evb=evb, c=c):
                    t.wait_ge(sV.h, m)
                    for kv in range(KVH):
                        if kv >= 2:
                            t.wait_ge(sV.h, evb + kv - 1)
                        elif tt > 0:
                            t.wait_ge(sV.h, M["ktp_evac"][tt - 1] - (1 - kv))
                        elif c >= 1:
                            t.wait_ge(sV.h, M1[c - 1]["qtp_evac"][-1] - (1 - kv))
                            t.wait_ge(sV.h, M1[c - 1]["mg"][-1])
                        t.transpose((ps_t if kv % 2 == 0 else ps_t2)[0:64, 0:128],
                                    cvt[:, kv * 64:(kv + 1) * 64], ident[:, :])
                        tinc(t)
                on("tensor", ftk)
                M.setdefault("ktp_t", []).append(sT.v)
                m_tk0 = sT.v - KVH

                def fvk(v, m0=m_tk0, tt=tt):
                    for kv in range(KVH):
                        v.wait_ge(sT.h, m0 + kv + 1)
                        v.tensor_copy(
                            kT_ch[0:64, kv * CH + tt * 128: kv * CH + tt * 128 + 128],
                            (ps_t if kv % 2 == 0 else ps_t2)[0:64, 0:128]).then_inc(sV.h, 1)
                        sV.v += 1
                    M["ktp_evac"].append(sV.v)
                on("vector", fvk)
                m_vst = sV.v

                def fqv(q, tt=tt, c=c, m=m_vst):
                    q.wait_ge(sV.h, m)
                    M.setdefault("vh_wr", []).append(
                        stKW.dma(q, sc_vh[c * NTT + tt, :, :], vh_st[:, :]))
                on("sync", fqv)
            M["kv_evac_s"] = sS.v

            def fvo(v):
                v.memset(kT_ch[64:65, :], 1.0)
                vinc(v)
            on("vector", fvo)
            m_ones = sV.v

            def fqk(q, c=c, m=m_ones):
                q.wait_ge(sV.h, m)
                for tt in range(NTT):
                    stKW.dma(q, sc_kT[c * NTT + tt, :, :, :],
                             kT_ch[0:65, :].rearrange("p (kv t) -> p kv t", kv=KVH)[:, :, tt * 128:(tt + 1) * 128])
            on("sync", fqk)
            M["kT_wr"] = stKW.totals()

        def emit_q_evac(c, M, og):
            M.setdefault("q_evac_s", {})
            M.setdefault("qtp_evac", [0])
            M.setdefault("q_rope", [])
            M.setdefault("qtp_t", [])
            mm_m = M["q_mm0"] if og == 0 else M["q_mm1"]
            for tt in range(NTT):
                pt = (ps_q if tt < 2 else ps_mo)[:, (tt % 2) * 512:(tt % 2) * 512 + 512]

                def fse(s, pt=pt, og=og, tt=tt, mm_m=mm_m):
                    s.wait_ge(sT.h, mm_m)
                    if M["q_rope"]:
                        s.wait_ge(sV.h, M["q_rope"][-1])
                    s.copy(q_sb[:, og * 512:og * 512 + 512], pt[:, :])
                    sinc(s)
                on("scalar", fse)
                m_c = sS.v

                def fvq(v, m=m_c, og=og):
                    v.wait_ge(sS.h, m)
                    qs = q_sb[:, og * 512:og * 512 + 512]
                    v.tensor_tensor(junk[:, 0:512], qs, qs, ALU.mult)
                    vedge(v)
                    v.tensor_reduce(stat[:, 0:8],
                                    junk[:, 0:512].rearrange("p (h d) -> p h d", h=8),
                                    AX.X, ALU.add)
                    vedge(v)
                    v.tensor_scalar(stat[:, 0:8], stat[:, 0:8], 1.0 / HD, RMS_EPS,
                                    ALU.mult, ALU.add)
                    vinc(v)
                on("vector", fvq)
                m_ms = sV.v

                def fsr(s, m=m_ms):
                    s.wait_ge(sV.h, m)
                    s.activation(stat[:, 0:8], stat[:, 0:8], AF.Ln)
                    sedge(s)
                    s.activation(stat[:, 0:8], stat[:, 0:8], AF.Exp, scale=-0.5)
                    sinc(s)
                on("scalar", fsr)
                m_rs = sS.v

                def fvr(v, m=m_rs, og=og, tt=tt):
                    v.wait_ge(sS.h, m)
                    v.tensor_tensor(stat[:, 0:8], stat[:, 0:8],
                                    grep[:, og * 8:og * 8 + 8], ALU.mult)
                    qs = q_sb[:, og * 512:og * 512 + 512]
                    for h8 in range(8):
                        v.tensor_scalar_mul(qs[:, h8 * 64:(h8 + 1) * 64],
                                            qs[:, h8 * 64:(h8 + 1) * 64],
                                            stat[:, h8:h8 + 1])
                    qv3 = qs.rearrange("p (h d) -> p h d", h=8)
                    cv = cosr[:, tt * 256:(tt + 1) * 256].rearrange("p (h d) -> p h d", h=H)[:, 0:8, :]
                    sv = sinr[:, tt * 256:(tt + 1) * 256].rearrange("p (h d) -> p h d", h=H)[:, 0:8, :]
                    a = qv3[:, :, 0:16]; b = qv3[:, :, 16:32]
                    vedge(v)
                    j3 = lambda o: junk[:, o:o + 128].rearrange("p (h d) -> p h d", h=8)
                    v.tensor_tensor(j3(0), a, cv, ALU.mult)
                    v.tensor_tensor(j3(128), b, sv, ALU.mult)
                    v.tensor_tensor(j3(256), b, cv, ALU.mult)
                    v.tensor_tensor(j3(384), a, sv, ALU.mult)
                    vedge(v)
                    v.tensor_tensor(a, j3(0), j3(128), ALU.add)
                    v.tensor_tensor(b, j3(256), j3(384), ALU.subtract)
                    vedge(v)
                    if M["qtp_t"]:
                        v.wait_ge(sT.h, M["qtp_t"][-1])
                    else:
                        v.wait_ge(sT.h, M["ktp_t"][-1])
                    v.tensor_copy(cvt[:, 0:512], qs)
                    vinc(v)
                on("vector", fvr)
                M["q_rope"].append(sV.v)
                m_ro = sV.v
                evb = sV.v

                def ftq(t, m=m_ro, evb=evb, og=og, tt=tt):
                    t.wait_ge(sV.h, m)
                    for h8 in range(8):
                        if h8 >= 2:
                            t.wait_ge(sV.h, evb + h8 - 1)
                        else:
                            prev = M["qtp_evac"][-1] if (og > 0 or tt > 0) else M["ktp_evac"][-1]
                            t.wait_ge(sV.h, prev - (1 - h8))
                        t.transpose((ps_t if h8 % 2 == 0 else ps_t2)[0:64, 0:128],
                                    cvt[:, h8 * 64:(h8 + 1) * 64], ident[:, :])
                        tinc(t)
                on("tensor", ftq)
                M["qtp_t"].append(sT.v)
                m_tq0 = sT.v - 8

                def fvk(v, m0=m_tq0, og=og, tt=tt):
                    for h8 in range(8):
                        h = og * 8 + h8
                        v.wait_ge(sT.h, m0 + h8 + 1)
                        v.tensor_copy(
                            qT_sb[0:64, h * CH + tt * 128: h * CH + tt * 128 + 128],
                            (ps_t if h8 % 2 == 0 else ps_t2)[0:64, 0:128]).then_inc(sV.h, 1)
                        sV.v += 1
                    M["qtp_evac"].append(sV.v)
                on("vector", fvk)
            M["q_evac_s"][og] = sS.v

        def emit_qrow64(c, M):
            def fv(v):
                v.memset(qT_sb[64:65, :], 1.0)
                vedge(v)
                for h in range(H):
                    v.tensor_scalar_mul(qT_sb[64:65, h * CH:(h + 1) * CH],
                                        qT_sb[64:65, h * CH:(h + 1) * CH],
                                        negg64[64:65, h:h + 1])
                vinc(v)
            on("vector", fv)
            M["qrow"] = sV.v
            M["rope_done"] = sV.v

        def emit_attention_init(c, M):
            M["mg"] = []
            M["yT_evac"] = [0] * 8
            return {"kr": [], "vr": [], "av": [], "exp": [], "gi": 0}

        def emit_attn_kp(c, M, st, hp, kp, K4):
            kr, vr, av_done, exp_done = st["kr"], st["vr"], st["av"], st["exp"]
            h0, h1 = 2 * hp, 2 * hp + 1
            gi = st["gi"]; st["gi"] += 1

            def fqs(q, gi=gi, kp=kp, c=c):
                if gi == 0:
                    for hh, vv in stKW.totals():
                        q.wait_ge(hh, vv)
                    q.wait_ge(sV.h, M["qrow"])
                if gi >= 2:
                    q.wait_ge(sT.h, av_done[gi - 2])
                kr.append(stKR.dma(q, kT_sl[gi % 2][0:65, :],
                                   sc_kT[kp, :, :, :].rearrange("p kv t -> p (kv t)")))
                vr.append(stVR.dma(q, vh_sl[gi % 2][:, :], sc_vh[kp, :, :]))
            on("sync", fqs)

            def fts(t, gi=gi, h0=h0, h1=h1, kp=kp,
                    m_pe=(exp_done[-1] if exp_done else None),
                    m_mg=(M["mg"][-1] if (kp == 0 and M["mg"]) else None)):
                hh, vv = kr[gi]
                t.wait_ge(hh, vv)
                if m_mg is not None:
                    t.wait_ge(sV.h, m_mg)
                if gi == 0:
                    t.wait_ge(sS.h, M["q_evac_s"][1])
                    t.wait_ge(sS.h, M["reps"])
                    t.wait_ge(sV.h, M["bc"])
                if m_pe is not None:
                    t.wait_ge(sS.h, m_pe)
                for ih, h in enumerate((h0, h1)):
                    t.matmul(ps_q[:, ih * 512:(ih + 1) * 512],
                             kT_sl[gi % 2][0:65, (h // 4) * 128:(h // 4) * 128 + 128],
                             qT_sb[0:65, h * CH:(h + 1) * CH],
                             start=True, stop=True)
                tinc(t)
            on("tensor", fts)
            m_sc = sT.v

            def fse(s, m=m_sc, gi=gi):
                s.wait_ge(sT.h, m)
                if gi >= 2:
                    s.wait_ge(sT.h, av_done[gi - 2])
                s.activation(probs[gi % 2][:, :], ps_q[:, :], AF.Exp, scale=0.125)
                sinc(s)
            on("scalar", fse)
            exp_done.append(sS.v)
            m_exp = sS.v

            m_mask = None
            if kp >= 4 * c:
                j = kp - 4 * c
                def fgm(g, m=m_exp, gi=gi, j=j, c=c):
                    g.wait_ge(sS.h, m)
                    for ih in range(2):
                        g.affine_select(probs[gi % 2][:, ih * 512:(ih + 1) * 512],
                                        probs[gi % 2][:, ih * 512:(ih + 1) * 512],
                                        [[1, 512]], ALU.is_ge, 0.0,
                                        base=-128 * j, channel_multiplier=-1)
                    g.last.then_inc(sG.h, 1)
                on("gpsimd", fgm)
                sG.v += 1
                m_mask = sG.v

            def fta(t, gi=gi, kp=kp, h0=h0, h1=h1, hp=hp, c=c,
                    m_exp=m_exp, m_mask=m_mask, K4=K4):
                hh, vv = vr[gi]
                t.wait_ge(hh, vv)
                if m_mask is None:
                    t.wait_ge(sS.h, m_exp)
                else:
                    t.wait_ge(sG.h, m_mask)
                if kp == 0 and hp >= 1:
                    t.wait_ge(sS.h, M["yTc"][hp - 1])
                elif kp == 0:
                    t.wait_ge(sV.h, M["bc"])
                    t.wait_ge(sS.h, M["xin_done"][NIT - 1])
                for ih, h in enumerate((h0, h1)):
                    t.matmul(ps_y[0:65, ih * 512:(ih + 1) * 512],
                             vh_sl[gi % 2][:, (h // 4) * 65:(h // 4) * 65 + 65],
                             probs[gi % 2][:, ih * 512:(ih + 1) * 512],
                             start=(kp == 0), stop=(kp == K4 - 1))
                tinc(t)
            on("tensor", fta)
            av_done.append(sT.v)

        def emit_attn_pass_end(c, M, st, hp):
            m_av = st["av"][-1]
            for ih, h in enumerate((2 * hp, 2 * hp + 1)):
                m_mg0 = M["mg"][-1] if M["mg"] else None

                def fsc2(s, m=m_av, ih=ih, m_mg=m_mg0):
                    s.wait_ge(sT.h, m)
                    if m_mg is not None:
                        s.wait_ge(sV.h, m_mg)
                    s.copy(yT_sb[0:65, :], ps_y[0:65, ih * 512:(ih + 1) * 512])
                    sedge(s)
                    s.activation(linv[0:1, :], yT_sb[64:65, :], AF.Ln)
                    sedge(s)
                    s.activation(linv[0:1, :], linv[0:1, :], AF.Exp, scale=-1.0)
                    sinc(s)
                on("scalar", fsc2)
                m_ls = sS.v

                def fvl(v, m=m_ls):
                    v.wait_ge(sS.h, m)
                    v.tensor_scalar_mul(linv[0:1, :], linv[0:1, :], w_col[0:1, 0:1])
                    vedge(v)
                    v.tensor_copy(linv_bf[0:1, :], linv[0:1, :])
                    vinc(v)
                on("vector", fvl)
                m_li = sV.v

                def ftb(t, m=m_li, h=h, c=c):
                    t.wait_ge(sV.h, m)
                    if M["mg"]:
                        t.wait_ge(sV.h, M["mg"][-1])
                    t.matmul(ps_q[0:64, 0:512], ones1[0:1, 0:64],
                             linv_bf[0:1, :], start=True, stop=True)
                    tinc(t)
                on("tensor", ftb)
                m_rr = sT.v

                def fvg(v, m=m_rr, h=h):
                    v.wait_ge(sT.h, m)
                    base = (h % 2) * 64
                    v.tensor_tensor(
                        mergedT[base:base + 64, (h // 2) * CH:(h // 2) * CH + CH],
                        yT_sb[0:64, :], ps_q[0:64, 0:512], ALU.mult)
                    vinc(v)
                on("vector", fvg)
                M["mg"].append(sV.v)
            M.setdefault("yTc", [0] * 8)[hp] = sS.v
            M["yT_evac"][hp] = sV.v
            if hp == 7:
                M["attn_done_v"] = sV.v

        def emit_mamba_front(c, M):
            win_rd = []
            M["xssm_mm"] = []
            M["conv_carry"] = []
            M["xin_done"] = []
            M["xs_done"] = []
            for it in range(NIT):
                def fqw(q, it=it):
                    if it >= 2:
                        q.wait_ge(sT.h, M["xssm_mm"][it - 2])
                    win_rd.append(stWin.dma(
                        q, win_sl[it % 2][:, :].rearrange("p (dc o) -> p dc o", dc=NDC),
                        sc_win[:, :, 512 + it * 128:512 + it * 128 + 128].rearrange("dc p o -> p dc o")))
                on("sync", fqw)

                def ftx(t, it=it, c=c):
                    hh, vv = win_rd[it]
                    t.wait_ge(hh, vv)
                    if it == 0:
                        if c >= 1:
                            t.wait_ge(sV.h, M1[c - 1]["mo_evac"][-1])
                    else:
                        t.wait_ge(sV.h, M["conv_carry"][it - 1])
                        t.wait_ge(sS.h, M["xin_done"][it - 1])
                    for dc in range(NDC):
                        t.matmul(ps_e[:, :], win_sl[it % 2][:, dc * 128:(dc + 1) * 128],
                                 xnT_ch[:, dc * CH:(dc + 1) * CH],
                                 start=(dc == 0), stop=(dc == NDC - 1))
                    tinc(t)
                on("tensor", ftx)
                M["xssm_mm"].append(sT.v)
                m_mm = sT.v

                def fvc(v, m=m_mm, it=it):
                    v.wait_ge(sT.h, m)
                    v.tensor_copy(sp_tmp[:, 0:3], ps_e[:, 509:512])
                    v.tensor_copy(xin_bf[:, 0:3], conv_carry[:, it * 3:(it + 1) * 3])
                    vedge(v)
                    v.tensor_copy(conv_carry[:, it * 3:(it + 1) * 3], sp_tmp[:, 0:3])
                    vinc(v)
                on("vector", fvc)
                M["conv_carry"].append(sV.v)
                m_cs = sV.v

                def fsc(s, m=m_cs):
                    s.wait_ge(sV.h, m)
                    s.activation(xin_bf[:, 3:515], ps_e[:, :], AF.Copy)
                    sinc(s)
                on("scalar", fsc)
                M["xin_done"].append(sS.v)
                m_xin = sS.v

                def fvt(v, m=m_xin, it=it):
                    v.wait_ge(sS.h, m)
                    v.tensor_scalar_mul(cvt[:, :], xin_bf[:, 0:512],
                                        cwv[:, it * 4:it * 4 + 1])
                    for j in range(1, 4):
                        vedge(v)
                        v.scalar_tensor_tensor(cvt[:, :], xin_bf[:, j:512 + j],
                                               cwv[:, it * 4 + j:it * 4 + j + 1],
                                               cvt[:, :], ALU.mult, ALU.add)
                    vinc(v)
                on("vector", fvt)
                m_cv = sV.v

                def fss(s, m=m_cv, it=it):
                    s.wait_ge(sV.h, m)
                    s.activation(xs_bf[:, it * CH:(it + 1) * CH], cvt[:, :], AF.Silu,
                                 bias=cb_sb[:, it:it + 1])
                    sinc(s)
                on("scalar", fss)
                M["xs_done"].append(sS.v)

            def ftp(t):
                t.wait_ge(sS.h, M["xs_done"][NIT - 1])
                t.wait_ge(sV.h, M["conv_carry"][NIT - 1])
                t.wait_ge(sS.h, M["xin_done"][NIT - 1])
                for o0, o1, pp in [(0, 64, ps_e[0:64, :]), (64, 80, ps_q[0:16, 0:512]),
                                   (80, 96, ps_mo[0:16, 0:512])]:
                    for ic in range(NIT):
                        t.matmul(pp, wxpT_bf[:, ic * 96 + o0:ic * 96 + o1],
                                 xs_bf[:, ic * CH:(ic + 1) * CH],
                                 start=(ic == 0), stop=(ic == NIT - 1))
                tinc(t)
            on("tensor", ftp)
            m_xp = sT.v

            def fvp(v, m=m_xp):
                v.wait_ge(sT.h, m)
                v.tensor_copy(dtlowT_bf[0:64, :], ps_e[0:64, :])
                v.tensor_copy(B_bf[0:16, :], ps_q[0:16, 0:512])
                v.tensor_copy(C_bf[0:16, :], ps_mo[0:16, 0:512])
                v.tensor_scalar(stat[0:16, 0:1], w_col[0:16, 0:1], -1.0, 1.0,
                                ALU.mult, ALU.add)
                vedge(v)
                v.tensor_scalar_mul(C_bf[0:16, :], C_bf[0:16, :], stat[0:16, 0:1])
                vinc(v)
            on("vector", fvp)
            M["bc"] = sV.v

            rep_ev = []
            for n in range(NST):
                def ftr(t, n=n):
                    if n == 0:
                        t.wait_ge(sV.h, M["bc"])
                    else:
                        t.wait_ge(sS.h, rep_ev[n - 1])
                    t.matmul(ps_e[:, :], oneh[0:16, n * 128:(n + 1) * 128],
                             B_bf[0:16, :], start=True, stop=True)
                    t.matmul(ps_q[:, 0:512], oneh[0:16, n * 128:(n + 1) * 128],
                             C_bf[0:16, :], start=True, stop=True)
                    tinc(t)
                on("tensor", ftr)
                m_r = sT.v

                def fsr(s, m=m_r, n=n):
                    s.wait_ge(sT.h, m)
                    s.activation(Brep[:, n * CH:(n + 1) * CH], ps_e[:, :], AF.Copy)
                    s.activation(Crep[:, n * CH:(n + 1) * CH], ps_q[:, 0:512], AF.Copy)
                    sinc(s)
                on("scalar", fsr)
                rep_ev.append(sS.v)
            M["reps"] = sS.v

        def emit_scan_init(c, M):
            M["delta"] = []
            M["gmm"] = []
            M["silu_g"] = []
            M["gated"] = []
            M["g_rd"] = []
            return {"dA_m": {}, "scan_m": {}, "ro_m": {}}

        def emit_scan_head(c, M, st, it):
            dA_m, scan_m, ro_m = st["dA_m"], st["scan_m"], st["ro_m"]
            ps_dt = ps_mo[:, 512:1024]
            if True:
                def ftd(t, it=it):
                    if it == 0:
                        t.wait_ge(sS.h, M["reps"])
                        t.wait_ge(sS.h, M["q_evac_s"][1])
                    else:
                        t.wait_ge(sV.h, M["delta"][it - 1])
                    t.matmul(ps_dt, wdtT_bf[0:64, it * 128:(it + 1) * 128],
                             dtlowT_bf[0:64, :], start=True, stop=True)
                    tinc(t)
                on("tensor", ftd)
                m_dt = sT.v
                dpar = delta_b[it % 4]

                def fvd(v, m=m_dt, it=it, dpar=dpar):
                    v.wait_ge(sT.h, m)
                    if it >= 4:
                        v.wait_ge(sS.h, dA_m[(it - 4, NST - 1)])
                    v.tensor_scalar_add(dpar[:, :], ps_dt, bdt_sb[:, it:it + 1])
                    vinc(v)
                on("vector", fvd)
                m_z = sV.v

                def fsd(s, m=m_z, dpar=dpar, it=it):
                    s.wait_ge(sV.h, m)
                    if it > 0:
                        s.wait_ge(sV.h, M["silu_gv_l"][it - 1])
                    s.activation(sp_tmp[:, :], dpar[:, :], AF.Abs)
                    sedge(s)
                    s.activation(sp_tmp[:, :], sp_tmp[:, :], AF.Exp, scale=-1.0)
                    sedge(s)
                    s.activation(sp_tmp[:, :], sp_tmp[:, :], AF.Ln, bias=1.0)
                    sinc(s)
                on("scalar", fsd)
                m_sp = sS.v

                def fvz(v, m=m_sp, dpar=dpar, it=it):
                    v.wait_ge(sS.h, m)
                    v.scalar_tensor_tensor(dpar[:, :], dpar[:, :], 0.0, sp_tmp[:, :],
                                           ALU.max, ALU.add)
                    vedge(v)
                    v.tensor_tensor(du_b[:, :], dpar[:, :],
                                    xs_bf[:, it * CH:(it + 1) * CH], ALU.mult)
                    vinc(v)
                on("vector", fvz)
                m_du = sV.v
                M["delta"].append(sV.v)

                def fqg(q, it=it):
                    if it >= 2:
                        q.wait_ge(sT.h, M["gmm"][it - 2])
                    M["g_rd"].append(stWing.dma(
                        q, wing_sl[it % 2][:, :].rearrange("p (dc o) -> p dc o", dc=NDC),
                        sc_win[:, :, 2560 + it * 128:2560 + it * 128 + 128].rearrange("dc p o -> p dc o")))
                on("sync", fqg)

                def ftg(t, it=it):
                    hh, vv = M["g_rd"][it]
                    t.wait_ge(hh, vv)
                    if it == 0:
                        t.wait_ge(sS.h, M["reps"])
                        t.wait_ge(sV.h, M["bc"])
                    else:
                        t.wait_ge(sV.h, M["silu_gv_l"][it - 1])
                    for dc in range(NDC):
                        t.matmul(ps_mo[:, 0:512], wing_sl[it % 2][:, dc * 128:(dc + 1) * 128],
                                 xnT_ch[:, dc * CH:(dc + 1) * CH],
                                 start=(dc == 0), stop=(dc == NDC - 1))
                    tinc(t)
                on("tensor", ftg)
                M["gmm"].append(sT.v)
                m_g = sT.v

                def fsg(s, m=m_g, it=it):
                    s.wait_ge(sT.h, m)
                    s.wait_ge(sV.h, M["delta"][it])
                    if it > 3:
                        s.wait_ge(sV.h, M["gated"][it - 4])
                    s.activation(sp_tmp[:, :], ps_mo[:, 0:512], AF.Tanh, scale=0.5)
                    sinc(s)
                on("scalar", fsg)
                m_th = sS.v

                def fvsg(v, m=m_th, it=it):
                    v.wait_ge(sS.h, m)
                    v.tensor_scalar(cvt[:, :], sp_tmp[:, :], 0.5, 0.5, ALU.mult, ALU.add)
                    vedge(v)
                    v.tensor_tensor(silu_g[it % 4][:, :], ps_mo[:, 0:512], cvt[:, :],
                                    ALU.mult)
                    vinc(v)
                on("vector", fvsg)
                M["silu_g"].append(sS.v)
                M.setdefault("silu_gv_l", []).append(sV.v)
                st[("mdu", it)] = m_du
                st[("dpar", it)] = dpar

        def emit_scan_n(c, M, st, it, n):
            dA_m, scan_m, ro_m = st["dA_m"], st["scan_m"], st["ro_m"]
            m_du = st[("mdu", it)]
            dpar = st[("dpar", it)]
            if True:
                if True:
                    par = n % 2

                    def fsa(s, it=it, n=n, par=par, dpar=dpar, m_du=m_du):
                        s.wait_ge(sV.h, m_du)
                        if n >= 2:
                            s.wait_ge(sV.h, scan_m[(it, n - 2)])
                        elif it >= 1:
                            s.wait_ge(sV.h, scan_m[(it - 1, NST - 2 + n)])
                        s.activation(dA_b[par][:, :], dpar[:, :], AF.Exp,
                                     scale=A_sb[:, it * NST + n:it * NST + n + 1])
                        sinc(s)
                    on("scalar", fsa)
                    dA_m[(it, n)] = sS.v
                    m_da = sS.v

                    def fvs(v, it=it, n=n, par=par, m_da=m_da):
                        v.wait_ge(sS.h, m_da)
                        v.tensor_tensor(dBu_b[par][:, :], du_b[:, :],
                                        Brep[:, n * CH:(n + 1) * CH], ALU.mult)
                        vedge(v)
                        v.tensor_tensor_scan(
                            h_stk[:, n * CH:(n + 1) * CH],
                            dA_b[par][:, :], dBu_b[par][:, :],
                            scan_carry[:, it * NST + n:it * NST + n + 1],
                            ALU.mult, ALU.add)
                        vedge(v)
                        v.tensor_copy(scan_carry[:, it * NST + n:it * NST + n + 1],
                                      h_stk[:, n * CH + CH - 1:n * CH + CH])
                        v.tensor_tensor(h_stk[:, n * CH:(n + 1) * CH],
                                        h_stk[:, n * CH:(n + 1) * CH],
                                        Crep[:, n * CH:(n + 1) * CH], ALU.mult)
                        vinc(v)
                    on("vector", fvs)
                    scan_m[(it, n)] = sV.v

        def emit_scan_tail(c, M, st, it):
            dA_m, scan_m, ro_m = st["dA_m"], st["scan_m"], st["ro_m"]
            if True:
                def fvtree(v, it=it):
                    for width in (8 * CH, 4 * CH, 2 * CH, CH):
                        v.tensor_tensor(h_stk[:, 0:width], h_stk[:, 0:width],
                                        h_stk[:, width:2 * width], ALU.add)
                        vedge(v)
                    v.scalar_tensor_tensor(cvt[:, :], xs_bf[:, it * CH:(it + 1) * CH],
                                           Dw_sb[:, it:it + 1], h_stk[:, 0:CH],
                                           ALU.mult, ALU.add)
                    vedge(v)
                    v.tensor_tensor(xs_bf[:, it * CH:(it + 1) * CH], cvt[:, :],
                                    silu_g[it % 4][:, :], ALU.mult)
                    vinc(v)
                on("vector", fvtree)
                M["gated"].append(sV.v)

        def emit_tail(c, M):
            mo_rd = []
            M["momm"] = []
            M["mo_evac"] = []
            for ot in range(NDC):
                def fqm(q, ot=ot, c=c):
                    if c == 0 and ot == 0:
                        for hh, vv in wsT_all:
                            q.wait_ge(hh, vv)
                    if ot >= 2:
                        q.wait_ge(sT.h, M["momm"][ot - 2])
                    mo_rd.append(stWmo.dma(
                        q, wmo_sl[ot % 2][:, :].rearrange("p (ic o) -> p ic o", ic=NIT),
                        sc_wmo[:, :, ot * 128:(ot + 1) * 128].rearrange("ic p o -> p ic o")))
                on("sync", fqm)

                def ftm(t, ot=ot):
                    hh, vv = mo_rd[ot]
                    t.wait_ge(hh, vv)
                    if ot == 0:
                        t.wait_ge(sV.h, M["gated"][NIT - 1])
                    else:
                        t.wait_ge(sV.h, M["mo_evac"][ot - 1])
                    for ic in range(NIT):
                        t.matmul(ps_e[:, :], wmo_sl[ot % 2][:, ic * 128:(ic + 1) * 128],
                                 xs_bf[:, ic * CH:(ic + 1) * CH],
                                 start=(ic == 0), stop=(ic == NIT - 1))
                    tinc(t)
                on("tensor", ftm)
                M["momm"].append(sT.v)
                m_mo = sT.v

                def fvm(v, m=m_mo, ot=ot):
                    v.wait_ge(sT.h, m)
                    v.tensor_tensor(mergedT[:, ot * CH:(ot + 1) * CH],
                                    mergedT[:, ot * CH:(ot + 1) * CH],
                                    ps_e[:, :], ALU.add)
                    vinc(v)
                on("vector", fvm)
                M["mo_evac"].append(sV.v)
            M["merged"] = sV.v

            pr_rd = []
            M["prmm"] = []
            M["epi_wr"] = []
            prev_epi = M1[c - 1]["epi_wr"] if c >= 1 else None
            for pr in range(2):
                for og in range(2):
                    for dc in range(NDC):
                        i = pr * 16 + og * NDC + dc

                        def fqp(q, i=i, og=og, dc=dc, c=c):
                            if c == 0 and i == 0:
                                for hh, vv in wsT_all:
                                    q.wait_ge(hh, vv)
                            if i >= 2:
                                q.wait_ge(sT.h, M["prmm"][i - 2])
                            else:
                                q.wait_ge(sT.h, M["a_sl_done"])
                            pr_rd.append(stWq.dma(q, wq_sl[i % 2][:, :],
                                                  sc_wpr[dc, :, og * 512:og * 512 + 512]))
                        on("sync", fqp)

                        def ftpj(t, i=i, og=og, dc=dc, pr=pr, c=c):
                            hh, vv = pr_rd[i]
                            t.wait_ge(hh, vv)
                            if dc == 0 and og == 0:
                                t.wait_ge(sV.h, M["merged"])
                                if pr == 0 and c >= 1:
                                    t.wait_ge(sV.h, M1[c - 1]["epi_v"][1])
                                elif pr == 1:
                                    t.wait_ge(sV.h, M["epi_v"][0])
                            for tj in range(2):
                                tt = 2 * pr + tj
                                pt = (ps_q if tj == 0 else ps_mo)[:, og * 512:og * 512 + 512]
                                t.matmul(pt,
                                         mergedT[:, dc * CH + tt * 128: dc * CH + tt * 128 + 128],
                                         wq_sl[i % 2][:, :],
                                         start=(dc == 0), stop=(dc == NDC - 1))
                            tinc(t)
                        on("tensor", ftpj)
                        M["prmm"].append(sT.v)
                m_pr = sT.v

                for tj in range(2):
                    tt = 2 * pr + tj

                    def fqx(q, tt=tt, pr=pr, c=c):
                        ei = len(M["epi_wr"])
                        if pr >= 1:
                            hh, vv = M["epi_wr"][tt - 2]
                            q.wait_ge(hh, vv)
                        elif c >= 1:
                            hh, vv = prev_epi[tt + 2]
                            q.wait_ge(hh, vv)
                        r0 = c * CH + tt * 128
                        M.setdefault("x_rd", []).append(
                            stEpi.dma(q, xbuf[tt % 2][:, :], x_h[r0:r0 + 128, :]))
                    on("sync", fqx)

                    def fve(v, tt=tt, tj=tj, m_pr=m_pr):
                        v.wait_ge(sT.h, m_pr)
                        hh, vv = M["x_rd"][tt]
                        v.wait_ge(hh, vv)
                        pt = ps_q if tj == 0 else ps_mo
                        v.tensor_tensor(out_st[tt % 2][:, :], xbuf[tt % 2][:, :],
                                        pt[:, :], ALU.add)
                        vinc(v)
                    on("vector", fve)
                    m_eo = sV.v

                    def fqo(q, tt=tt, c=c, m=m_eo):
                        q.wait_ge(sV.h, m)
                        r0 = c * CH + tt * 128
                        M["epi_wr"].append(
                            stOut.dma(q, out_h[r0:r0 + 128, :], out_st[tt % 2][:, :]))
                    on("sync", fqo)
                M.setdefault("epi_v", []).append(sV.v)
            M["ps_free"] = sV.v
            M["prmm_last"] = sT.v
            M["xnT_done"] = sT.v
            M["chunk_done"] = sV.v

            if "merged" in d_dbg:
                def fdm(q, c=c):
                    q.wait_ge(sV.h, M["merged"])
                    stDbg.dma(q, d_dbg["merged"][:, :, c * CH:(c + 1) * CH],
                              mergedT[:, :].rearrange("p (dc t) -> p dc t", dc=NDC))
                on("sync", fdm)
            if "gt" in d_dbg:
                def fdg(q, c=c):
                    q.wait_ge(sV.h, M["gated"][NIT - 1])
                    stDbg.dma(q, d_dbg["gt"][:, :, c * CH:(c + 1) * CH],
                              xs_bf[:, :].rearrange("p (it t) -> p it t", it=NIT))
                on("sync", fdg)
            if "xs" in d_dbg:
                def fdx(q, c=c):
                    q.wait_ge(sS.h, M["xs_done"][NIT - 1])
                    stDbg.dma(q, d_dbg["xs"][:, :, c * CH:(c + 1) * CH],
                              xs_bf[:, :].rearrange("p (it t) -> p it t", it=NIT))
                on("sync", fdx)
            if "bc" in d_dbg:
                def fdb(q, c=c):
                    q.wait_ge(sV.h, M["bc"])
                    stDbg.dma(q, d_dbg["bc"][0:16, c * CH:(c + 1) * CH], B_bf[0:16, :])
                    stDbg.dma(q, d_dbg["bc"][16:32, c * CH:(c + 1) * CH], C_bf[0:16, :])
                on("sync", fdb)
            if "qt" in d_dbg:
                def fdq(q, c=c):
                    q.wait_ge(sV.h, M["qrow"])
                    stDbg.dma(q, d_dbg["qt"][0:65, :, c * CH:(c + 1) * CH],
                              qT_sb[0:65, :].rearrange("p (h t) -> p h t", h=H))
                on("sync", fdq)

        def phase3():
            af.pop(); ab.pop()
            af.push(); ab.push()
            w1s = ab.take(NDC * IFF)
            w3s = ab.take(NDC * IFF)
            w2s = ab.take(NIT * DM)
            xn3 = [ab.take(NDC * CH) for _ in range(2)]
            hT = ab.take(NIT * CH)
            st3 = [af.take(DM) for _ in range(2)]
            cv3 = ab.take(2 * CH)

            last_chunk_v = M1[nch - 1]["chunk_done"]
            out_tot = stOut.totals()
            M3 = {}
            mm_w = {}
            evp = {0: 0, 1: 0}
            accp = {0: None, 1: None}
            xn_m = {}

            for e in range(NE):
                def fgw(q, e=e):
                    if e == 0:
                        q.wait_ge(sV.h, last_chunk_v)
                        for hh, vv in out_tot:
                            q.wait_ge(hh, vv)
                    else:
                        q.wait_ge(sT.h, mm_w[e - 1])
                    stW1.dma(q, w1s[:, :].rearrange("p (dc i) -> p dc i", dc=NDC),
                             w1_h[e].rearrange("(dc p) i -> p dc i", p=128))
                    stW3.dma(q, w3s[:, :].rearrange("p (dc i) -> p dc i", dc=NDC),
                             w3_h[e].rearrange("(dc p) i -> p dc i", p=128))
                    stW2.dma(q, w2s[:, :].rearrange("p (ic o) -> p ic o", ic=NIT),
                             w2_h[e].rearrange("(ic p) o -> p ic o", p=128))
                on("gpsimd", fgw)
                w_rdv = [(st.ss[0].h, st.ss[0].v) for st in (stW1, stW3, stW2)]

                for tc in range(nch):
                    def fq3(q, e=e, tc=tc):
                        key = (e, tc - 2) if tc >= 2 else ((e - 1, nch - 2 + tc) if e >= 1 else None)
                        if key is not None and key in mm_w:
                            q.wait_ge(sT.h, mm_w[key])
                        elif key is not None:
                            q.wait_ge(sT.h, mm_w[e - 1])
                        xn_m[(e, tc)] = stXn3.dma(
                            q, xn3[tc % 2][:, :].rearrange("p (dc t) -> p dc t", dc=NDC),
                            sc_xnT[:, :, tc * CH:(tc + 1) * CH].rearrange("dc p t -> p dc t"))
                    on("sync", fq3)

                    h_m = []
                    cvp = [cv3[:, 0:CH], cv3[:, CH:2 * CH]]
                    for ic in range(NIT):
                        pp = ps_q if ic % 2 == 0 else ps_y
                        def ft1(t, e=e, tc=tc, ic=ic, pp=pp):
                            if ic == 0:
                                hh, vv = xn_m[(e, tc)]
                                t.wait_ge(hh, vv)
                                for hh2, vv2 in w_rdv[0:2]:
                                    t.wait_ge(hh2, vv2)
                                if e == 0 and tc == 0:
                                    t.wait_ge(sV.h, last_chunk_v)
                                if evp[0] > 0:
                                    t.wait_ge(sV.h, evp[0])
                                if evp[1] > 0:
                                    t.wait_ge(sV.h, evp[1])
                            if ic >= 2:
                                t.wait_ge(sV.h, h_m[ic - 2])
                            for dc in range(NDC):
                                t.matmul(pp[:, 0:512],
                                         w1s[:, dc * IFF + ic * 128: dc * IFF + ic * 128 + 128],
                                         xn3[tc % 2][:, dc * CH:(dc + 1) * CH],
                                         start=(dc == 0), stop=(dc == NDC - 1))
                            for dc in range(NDC):
                                t.matmul(pp[:, 512:1024],
                                         w3s[:, dc * IFF + ic * 128: dc * IFF + ic * 128 + 128],
                                         xn3[tc % 2][:, dc * CH:(dc + 1) * CH],
                                         start=(dc == 0), stop=(dc == NDC - 1))
                            tinc(t)
                        on("tensor", ft1)
                        m1 = sT.v

                        def fs1(s, m=m1, ic=ic, pp=pp):
                            s.wait_ge(sT.h, m)
                            if ic >= 2:
                                s.wait_ge(sV.h, h_m[ic - 2])
                            s.activation(cvp[ic % 2], pp[:, 0:512], AF.Silu)
                            sinc(s)
                        on("scalar", fs1)
                        m_si = sS.v

                        def fv1(v, m=m_si, ic=ic, pp=pp):
                            v.wait_ge(sS.h, m)
                            v.tensor_tensor(hT[:, ic * CH:(ic + 1) * CH], cvp[ic % 2],
                                            pp[:, 512:1024], ALU.mult)
                            vinc(v)
                        on("vector", fv1)
                        h_m.append(sV.v)

                    for ts_ in range(NTT):
                        py = ps_q if ts_ % 2 == 0 else ps_mo
                        par = ts_ % 2

                        def ft2(t, e=e, tc=tc, ts_=ts_, py=py, par=par):
                            t.wait_ge(sV.h, h_m[NIT - 1])
                            hh, vv = w_rdv[2]
                            t.wait_ge(hh, vv)
                            if evp[par] > 0:
                                t.wait_ge(sV.h, evp[par])
                            for half in range(2):
                                for ic in range(NIT):
                                    t.matmul(py[:, half * 512:(half + 1) * 512],
                                             hT[:, ic * CH + ts_ * 128: ic * CH + ts_ * 128 + 128],
                                             w2s[:, ic * DM + half * 512: ic * DM + half * 512 + 512],
                                             start=(ic == 0), stop=(ic == NIT - 1))
                            tinc(t)
                        on("tensor", ft2)
                        m2 = sT.v

                        def fv2(v, m=m2, e=e, tc=tc, ts_=ts_, py=py, par=par):
                            v.wait_ge(sT.h, m)
                            if accp[par] is not None:
                                hh, vv = accp[par]
                                v.wait_ge(hh, vv)
                            ti = tc * NTT + ts_
                            v.tensor_scalar_mul(st3[par][:, :], py[:, :],
                                                c_coef[:, ti * NE + e: ti * NE + e + 1])
                            vinc(v)
                        on("vector", fv2)
                        evp[par] = sV.v
                        m_ev = sV.v

                        def fga(q, tc=tc, ts_=ts_, par=par, m=m_ev):
                            q.wait_ge(sV.h, m)
                            r0 = tc * CH + ts_ * 128
                            accp[par] = stAcc.dma(q, out_h[r0:r0 + 128, :],
                                                  st3[par][:, :], accum_op=ALU.add)
                        on("gpsimd", fga)
                    mm_w[(e, tc)] = sT.v
                    flush()
                mm_w[e] = sT.v

        phase_pre()
        wsT, wsT_all = phase0_weights()
        flush()
        phase0()
        if upto != "s1":
            for c in range(nch):
                chunk1(c)
                flush()
            if upto == "full":
                phase3()

        if "xn" in d_dbg or "c_coef" in d_dbg:
            def fdbg0(q):
                q.wait_ge(sV.h, sV.v)
                if "c_coef" in d_dbg:
                    stDbg.dma(q, d_dbg["c_coef"][:, 0:nch * NTT * NE],
                              c_coef[:, 0:nch * NTT * NE])
                if "xn" in d_dbg:
                    for hh, vv in stXnW.totals():
                        q.wait_ge(hh, vv)
                    stDbg.dma(q, d_dbg["xn"][:, :, 0:nch * CH],
                              sc_xnT[:, :, 0:nch * CH])
            on("sync", fdbg0)

        def ffin(q):
            q.wait_ge(sV.h, sV.v)
            q.wait_ge(sS.h, sS.v)
            q.wait_ge(sT.h, sT.v)
            for st_ in (stOut, stAcc, stDbg, stXnW, stKW, stWS):
                for hh, vv in st_.totals():
                    q.wait_ge(hh, vv)
        on("sync", ffin)
        flush()

    return nc, d_in, d_dbg



def _prep_inputs(kw):
    x = np.asarray(kw["L0_attn_x"])
    t = np.arange(S, dtype=np.float64)
    inv = 1.0 / (10000.0 ** (np.arange(0, ROPE_, 2, dtype=np.float64) / ROPE_))
    fr = np.outer(t, inv)
    cos_rep = np.tile(np.cos(fr), (1, H)).astype(ml_dtypes.bfloat16)
    sin_rep = np.tile(np.sin(fr), (1, H)).astype(ml_dtypes.bfloat16)

    ident = np.eye(128, dtype=ml_dtypes.bfloat16)
    ones1 = np.ones((1, 128), dtype=ml_dtypes.bfloat16)
    oneh = np.zeros((16, 16 * 128), dtype=ml_dtypes.bfloat16)
    for n in range(16):
        oneh[n, n * 128:(n + 1) * 128] = 1.0
    m4 = np.zeros((4, 128, CH), dtype=ml_dtypes.bfloat16)
    p = np.arange(128)[:, None]
    q = np.arange(CH)[None, :]
    for j in range(4):
        m4[j] = (q >= p + 128 * j).astype(ml_dtypes.bfloat16)

    def lay_pit(v, k):
        return np.ascontiguousarray(
            v.reshape(NIT, 128, k).transpose(1, 0, 2).reshape(128, NIT * k))

    com = dict(
        wcqT=np.ascontiguousarray(np.asarray(kw["L0_attn_W_cq"], np.float32).T),
        winT=np.ascontiguousarray(np.asarray(kw["L0_attn_W_in"], np.float32).T),
        wxpT=np.ascontiguousarray(np.asarray(kw["L0_attn_W_xproj"], np.float32).T),
        wdtT=np.ascontiguousarray(np.asarray(kw["L0_attn_W_dt"], np.float32).T),
        wmoT=np.ascontiguousarray(np.asarray(kw["L0_attn_W_mout"], np.float32).T),
        wprT=np.ascontiguousarray(np.asarray(kw["L0_attn_W_proj"], np.float32).T),
        alog=lay_pit(np.asarray(kw["L0_attn_A_log"], np.float32), NST),
        cw=lay_pit(np.asarray(kw["L0_attn_conv_w"], np.float32), 4),
        cb=lay_pit(np.asarray(kw["L0_attn_conv_b"], np.float32), 1),
        bdt=lay_pit(np.asarray(kw["L0_attn_b_dt"], np.float32), 1),
        dpar=lay_pit(np.asarray(kw["L0_attn_D_param"], np.float32), 1),
        grep=np.tile(np.asarray(kw["L0_attn_q_gain"], np.float32)[None, :], (128, 1)),
        alpha=np.tile(np.asarray(kw["L0_attn_merge_alpha"], np.float32)[None, :], (128, 1)),
        wr=np.ascontiguousarray(np.asarray(kw["L0_ffn_Wr"], np.float32)),
        br=np.tile(np.asarray(kw["L0_ffn_br"], np.float32)[None, :], (128, 1)),
        w1=np.ascontiguousarray(np.asarray(kw["L0_ffn_W1"], np.float32)),
        w3=np.ascontiguousarray(np.asarray(kw["L0_ffn_W3"], np.float32)),
        w2=np.ascontiguousarray(np.asarray(kw["L0_ffn_W2"], np.float32)),
        cosr=cos_rep, sinr=sin_rep,
        ident=ident, ones1=ones1, oneh=oneh, mask4=m4,
    )
    outs = []
    for b in range(8):
        d = dict(com)
        d["x"] = np.ascontiguousarray(x[b].astype(np.float32))
        outs.append(d)
    return outs


def kernel(**kw):
    in_maps = _prep_inputs(kw)
    nc, d_in, d_dbg = build(upto="full")
    res = run_bass_kernel_spmd(nc, in_maps, core_ids=list(range(8)))
    return np.stack([r["out"] for r in res.results], axis=0)


# version 12
# speedup vs baseline: 1.0312x; 1.0051x over previous
import contextlib
import numpy as np
import ml_dtypes

import concourse.bass as bass
import concourse.mybir as mybir
from concourse.bass_utils import run_bass_kernel_spmd

F32 = mybir.dt.float32
BF16 = mybir.dt.bfloat16
AF = mybir.ActivationFunctionType
ALU = mybir.AluOpType
AX = mybir.AxisListType

S = 4096
DM = 1024
INNER = 2048
H, KVH, HD = 16, 4, 64
NST = 16
DTR = 64
NE, IFF = 8, 2048
CH = 512
NCH = S // CH
NTT = CH // 128
NIT = INNER // 128
NDC = DM // 128
LN_EPS = 1e-5
RMS_EPS = float(np.finfo(np.float32).eps)
NKP = S // 128
ROPE_ = 32


class Ctr:
    def __init__(self, h):
        self.h = h
        self.v = 0


class DStream:
    def __init__(self, nc, es, name, nbuf):
        self.ss = [Ctr(es.enter_context(nc.semaphore(f"{name}{i}")))
                   for i in range(nbuf)]
        self.k = 0

    def dma(self, q, out, in_, **kw):
        s = self.ss[self.k % len(self.ss)]
        self.k += 1
        if s.v > 0:
            q.wait_ge(s.h, s.v)
        q.dma_start(out, in_, **kw).then_inc(s.h, 16)
        s.v += 16
        return (s.h, s.v)

    def totals(self):
        return [(s.h, s.v) for s in self.ss if s.v > 0]


class _Ar:
    def __init__(self, nc, es, name, n, dt):
        self.t = es.enter_context(nc.sbuf_tensor(name, [128, n], dt))
        self.off = 0
        self.cap = n
        self.marks = []

    def take(self, n):
        n = int(n)
        assert self.off + n <= self.cap, f"arena overflow {self.off}+{n}>{self.cap}"
        ap = self.t[:, self.off:self.off + n]
        self.off += n
        return ap

    def push(self):
        self.marks.append(self.off)

    def pop(self):
        self.off = self.marks.pop()


def build(upto="full", dbg=(), nch=NCH, race_check=False):
    nc = bass.Bass("TRN2", target_bir_lowering=False,
                   detect_race_conditions=race_check)
    dbg = set(dbg)

    d_in = {}
    def din(name, shape, dt=F32):
        d_in[name] = nc.dram_tensor(name, list(shape), dt, kind="ExternalInput").ap()
        return d_in[name]

    x_h = din("x", [S, DM])
    wcqT_h = din("wcqT", [DM, DM])
    winT_h = din("winT", [DM, 4608])
    wxpT_h = din("wxpT", [INNER, 96])
    wdtT_h = din("wdtT", [DTR, INNER])
    wmoT_h = din("wmoT", [INNER, DM])
    wprT_h = din("wprT", [DM, DM])
    alog_h = din("alog", [128, NIT * NST])
    cw_h = din("cw", [128, NIT * 4])
    cb_h = din("cb", [128, NIT])
    bdt_h = din("bdt", [128, NIT])
    dpar_h = din("dpar", [128, NIT])
    grep_h = din("grep", [128, H])
    alpha_h = din("alpha", [128, 1])
    wr_h = din("wr", [DM, NE])
    br_h = din("br", [128, NE])
    w1_h = din("w1", [NE, DM, IFF])
    w3_h = din("w3", [NE, DM, IFF])
    w2_h = din("w2", [NE, IFF, DM])
    cos_h = din("cosr", [S, 256], BF16)
    sin_h = din("sinr", [S, 256], BF16)
    ident_h = din("ident", [128, 128], BF16)
    ones1_h = din("ones1", [1, 128], BF16)
    oneh_h = din("oneh", [16, 16 * 128], BF16)
    mask4_h = din("mask4", [4, 128, CH], BF16)

    out_h = nc.dram_tensor("out", [S, DM], F32, kind="ExternalOutput").ap()

    d_dbg = {}
    def ddbg(name, shape, dt=F32):
        if name in dbg:
            d_dbg[name] = nc.dram_tensor("dbg_" + name, list(shape), dt,
                                         kind="ExternalOutput").ap()

    ddbg("xn", [NDC, 128, S], BF16)
    ddbg("c_coef", [128, (S // 128) * NE])
    ddbg("bc", [32, S], BF16)
    ddbg("gt", [128, NIT, S], BF16)
    ddbg("merged", [128, NDC, S], BF16)
    ddbg("qt", [128, H, S], BF16)
    ddbg("xs", [128, NIT, S], BF16)

    sc_xnT = nc.dram_tensor("sc_xnT", [NDC, 128, S], BF16, kind="Internal").ap()
    sc_kT = nc.dram_tensor("sc_kT", [NKP, 65, KVH, 128], BF16, kind="Internal").ap()
    sc_vh = nc.dram_tensor("sc_vh", [NKP, 128, KVH * 65], BF16, kind="Internal").ap()
    sc_wcq = nc.dram_tensor("sc_wcq", [NDC, 128, DM], BF16, kind="Internal").ap()
    sc_win = nc.dram_tensor("sc_win", [NDC, 128, 4608], BF16, kind="Internal").ap()
    sc_wpr = nc.dram_tensor("sc_wpr", [NDC, 128, DM], BF16, kind="Internal").ap()
    sc_wmo = nc.dram_tensor("sc_wmo", [NIT, 128, DM], BF16, kind="Internal").ap()

    es = contextlib.ExitStack()
    with es:
        afL = _Ar(nc, es, "aL32", 1984, F32)
        abL = _Ar(nc, es, "aLbf", 8704, BF16)
        af = _Ar(nc, es, "aR32", 10496, F32)
        ab = _Ar(nc, es, "aRbf", 72704, BF16)

        ident = abL.take(128)
        ones1 = abL.take(128)
        oneh = abL.take(16 * 128)
        mask4 = abL.take(4 * CH)
        wr_bf = abL.take(NDC * NE)
        wr_lo = abL.take(NDC * NE)
        wxpT_bf = abL.take(NIT * 96)
        wdtT_bf = abL.take(INNER)
        linv_bf = abL.take(CH)
        A_sb = afL.take(NIT * NST)
        cwv = afL.take(NIT * 4)
        cb_sb = afL.take(NIT)
        bdt_sb = afL.take(NIT)
        Dw_sb = afL.take(NIT)
        grep = afL.take(H)
        negg64 = afL.take(H)
        w_col = afL.take(1)
        br_rep = afL.take(NE)
        c_coef = afL.take((S // 128) * NE)
        scan_carry = afL.take(NIT * NST)
        conv_carry = afL.take(NIT * 3)
        stat = afL.take(16)
        stat2 = afL.take(16)
        linv = afL.take(CH)
        wr_f32 = afL.take(NDC * NE)

        af.push(); ab.push()
        xbuf = [af.take(DM) for _ in range(2)]
        junk = af.take(DM)
        q_sb = af.take(DM)
        kv_sb = af.take(256)
        yT_sb = af.take(CH)
        delta_b = [af.take(CH) for _ in range(4)]
        dA_b = [af.take(CH) for _ in range(2)]
        sp_tmp = af.take(CH)
        out_st = [af.take(DM) for _ in range(2)]

        xnT_ch = ab.take(NDC * CH)
        wq_sl = [ab.take(CH) for _ in range(2)]
        win_sl = [ab.take(NDC * 128) for _ in range(2)]
        wing_sl = [ab.take(NDC * 128) for _ in range(2)]
        wmo_sl = [ab.take(NIT * 128) for _ in range(2)]
        cosr = ab.take(NTT * 256)
        sinr = ab.take(NTT * 256)
        qT_sb = ab.take(H * CH)
        xn_bf = qT_sb[:, 0:DM]
        kT_ch = ab.take(KVH * CH)
        vh_st = ab.take(KVH * 65)
        kT_sl = [ab.take(KVH * 128) for _ in range(2)]
        vh_sl = [ab.take(KVH * 65) for _ in range(2)]
        probs = [ab.take(2 * CH) for _ in range(2)]
        mergedT = ab.take(NDC * CH)
        xin_bf = ab.take(CH + 3)
        cvt = ab.take(CH)
        xs_bf = ab.take(NIT * CH)
        dtlowT_bf = ab.take(CH)
        B_bf = ab.take(CH)
        C_bf = ab.take(CH)
        Brep = ab.take(NST * CH)
        Crep = ab.take(NST * CH)
        h_stk = ab.take(NST * CH)
        silu_g = [ab.take(CH) for _ in range(4)]
        du_b = ab.take(CH)
        dBu_b = [ab.take(CH) for _ in range(2)]
        wbounce = Brep
        xnloT_ch = h_stk[:, 0:NDC * CH]
        xnlo_t = Crep[:, 0:DM]

        ps_q = es.enter_context(nc.psum_tensor("ps_q", [128, 1024], F32))
        ps_mo = es.enter_context(nc.psum_tensor("ps_mo", [128, 1024], F32))
        ps_y = es.enter_context(nc.psum_tensor("ps_y", [128, 1024], F32))
        ps_e = ps_y[:, 0:512]
        ps_tt = es.enter_context(nc.psum_tensor("ps_tt", [128, 2048], BF16))
        ps_t = ps_tt[:, 0:128]
        ps_t2 = ps_tt[:, 1024:1152]

        sT = Ctr(es.enter_context(nc.semaphore("sT")))
        sV = Ctr(es.enter_context(nc.semaphore("sV")))
        sS = Ctr(es.enter_context(nc.semaphore("sS")))
        sG = Ctr(es.enter_context(nc.semaphore("sG")))

        stPre = DStream(nc, es, "pre", 2)
        stPreG = DStream(nc, es, "prg", 2)
        stX = DStream(nc, es, "x", 2)
        stXnW = DStream(nc, es, "xnw", 2)
        stXnR = DStream(nc, es, "xnr", 1)
        stWB = DStream(nc, es, "wb", 2)
        stWS = DStream(nc, es, "ws", 2)
        stWq = DStream(nc, es, "wq", 2)
        stWin = DStream(nc, es, "wi", 2)
        stWing = DStream(nc, es, "wg", 2)
        stWmo = DStream(nc, es, "wm", 2)
        stCos = DStream(nc, es, "cs", 2)
        stKW = DStream(nc, es, "kw", 2)
        stKR = DStream(nc, es, "kr", 2)
        stVR = DStream(nc, es, "vr", 2)
        stEpi = DStream(nc, es, "ep", 2)
        stOut = DStream(nc, es, "ou", 2)
        stDbg = DStream(nc, es, "db", 1)
        stW1 = DStream(nc, es, "w1s", 1)
        stW2 = DStream(nc, es, "w2s", 1)
        stW3 = DStream(nc, es, "w3s", 1)
        stXn3 = DStream(nc, es, "x3", 2)
        stAcc = DStream(nc, es, "ac", 2)

        blk = es.enter_context(nc.Block())

        class _Proxy:
            __slots__ = ("incs",)
            def __init__(self):
                self.incs = []

            def then_inc(self, sem, n):
                self.incs.append((sem, n))
                return self

        class _Rec:
            def __init__(self, calls):
                self.calls = calls
                self.last = None

            def __getattr__(self, n):
                def f(*a, **k):
                    p = _Proxy()
                    self.calls.append((n, a, k, p))
                    self.last = p
                    return p
                return f

        _onq = []

        def on(engname, fn):
            calls = []
            fn(_Rec(calls))
            if calls:
                _onq.append((engname, calls))

        def flush():
            if not _onq:
                return
            q = _onq[:]
            _onq.clear()
            for engname in ("sync", "gpsimd", "tensor", "scalar", "vector"):
                seq = [c for e, cs in q if e == engname for c in cs]
                if not seq:
                    continue

                def body(e, seq=seq):
                    for n, a, k, p in seq:
                        r = getattr(e, n)(*a, **k)
                        for sem, cnt in p.incs:
                            r.then_inc(sem, cnt)
                getattr(blk, engname)(body)

        def vinc(v):
            v.last.then_inc(sV.h, 1); sV.v += 1; return sV.v

        def sinc(s):
            s.last.then_inc(sS.h, 1); sS.v += 1; return sS.v

        def tinc(t):
            t.last.then_inc(sT.h, 1); sT.v += 1; return sT.v

        def sedge(s):
            s.last.then_inc(sS.h, 1); sS.v += 1
            s.wait_ge(sS.h, sS.v)

        def vedge(v):
            v.last.then_inc(sV.h, 1); sV.v += 1
            v.wait_ge(sV.h, sV.v)

        M1 = {}

        def phase_pre():
            def g(q):
                stPreG.dma(q, ident[:, :], ident_h)
                stPreG.dma(q, ones1[0:1, :], ones1_h)
                stPreG.dma(q, oneh[0:16, :].rearrange("a (b c) -> a b c", b=16),
                          oneh_h.rearrange("a (b c) -> a b c", b=16))
                stPreG.dma(q, mask4[:, :].rearrange("p (a c) -> p a c", a=4),
                          mask4_h.rearrange("a p c -> p a c"))
                stPreG.dma(q, wxpT_bf[:, :].rearrange("p (a e) -> p a e", a=NIT),
                          wxpT_h.rearrange("(a p) e -> p a e", p=128))
                stPreG.dma(q, wdtT_bf[0:64, :], wdtT_h)
            on("gpsimd", g)

            def q0(q):
                stPre.dma(q, A_sb[:, :], alog_h)
                stPre.dma(q, cwv[:, :], cw_h)
                stPre.dma(q, cb_sb[:, :], cb_h)
                stPre.dma(q, bdt_sb[:, :], bdt_h)
                stPre.dma(q, Dw_sb[:, :], dpar_h)
                stPre.dma(q, grep[:, :], grep_h)
                stPre.dma(q, w_col[:, :], alpha_h)
                stPre.dma(q, br_rep[:, :], br_h)
                stPre.dma(q, wr_f32[:, :].rearrange("p (a e) -> p a e", a=NDC),
                          wr_h.rearrange("(a p) e -> p a e", p=128))
            on("sync", q0)
            tot = stPre.totals() + stPreG.totals()

            def fs(s):
                for hh, vv in tot:
                    s.wait_ge(hh, vv)
                s.activation(A_sb[:, :], A_sb[:, :], AF.Exp)
                s.activation(w_col[:, :], w_col[:, :], AF.Exp, scale=-1.0)
                s.activation(negg64[:, :], grep[:, :], AF.Abs)
                sinc(s)
            on("scalar", fs)
            m = sS.v

            def fv(v, m=m):
                v.wait_ge(sS.h, m)
                v.tensor_scalar_mul(A_sb[:, :], A_sb[:, :], -1.0)
                v.tensor_scalar_mul(negg64[:, :], negg64[:, :], -64.0)
                v.tensor_scalar_add(w_col[:, :], w_col[:, :], 1.0)
                vedge(v)
                v.reciprocal(w_col[:, :], w_col[:, :])
                vedge(v)
                v.tensor_scalar(stat[:, 0:1], w_col[:, :], -1.0, 1.0, ALU.mult, ALU.add)
                vedge(v)
                v.tensor_scalar_mul(Dw_sb[:, :], Dw_sb[:, :], stat[:, 0:1])
                v.memset(scan_carry[:, :], 0.0)
                v.memset(conv_carry[:, :], 0.0)
                v.tensor_copy(wr_bf[:, :], wr_f32[:, :])
                vedge(v)
                v.tensor_tensor(wr_lo[:, :], wr_f32[:, :], wr_bf[:, :], ALU.subtract)
                vinc(v)
            on("vector", fv)

        def phase0_weights():
            jobs = []
            for p in range(NDC):
                jobs.append((wcqT_h[p * 128:(p + 1) * 128, :], sc_wcq[p, :, :], DM))
            for p in range(NDC):
                jobs.append((winT_h[p * 128:(p + 1) * 128, 0:2304],
                             sc_win[p, :, 0:2304], 2304))
                jobs.append((winT_h[p * 128:(p + 1) * 128, 2304:4608],
                             sc_win[p, :, 2304:4608], 2304))
            for p in range(NDC):
                jobs.append((wprT_h[p * 128:(p + 1) * 128, :], sc_wpr[p, :, :], DM))
            for p in range(NIT):
                jobs.append((wmoT_h[p * 128:(p + 1) * 128, :], sc_wmo[p, :, :], DM))

            ws_m = []
            for i, (src, dst, ncols) in enumerate(jobs):
                bnc = wbounce[:, (i % 2) * 2304:(i % 2) * 2304 + ncols]

                def g(q, i=i, src=src, bnc=bnc):
                    if i >= 2:
                        hh, vv = ws_m[i - 2]
                        q.wait_ge(hh, vv)
                    stWB.dma(q, bnc[:, :], src)
                on("gpsimd", g)
                wb = stWB.ss[(stWB.k - 1) % 2]
                mwb = wb.v

                def q1(q, dst=dst, bnc=bnc, wb=wb, mwb=mwb):
                    q.wait_ge(wb.h, mwb)
                    ws_m.append(stWS.dma(q, dst, bnc[:, :]))
                on("sync", q1)
            return stWS.totals()

        p0 = {"x": {}, "ln": {}, "tp": {}, "evac": {}, "c": {}, "rt": {}, "wr": {}}

        def phase0():
            for c in range(nch):
                for tt in range(NTT):
                    ti = c * NTT + tt
                    xb = xbuf[tt % 2]

                    def fq(q, c=c, tt=tt, ti=ti):
                        if ti >= 2:
                            q.wait_ge(sV.h, p0["c"][ti - 2])
                        r0 = c * CH + tt * 128
                        p0["x"][ti] = stX.dma(q, xbuf[tt % 2][:, :], x_h[r0:r0 + 128, :])
                    on("sync", fq)

                    def fv(v, ti=ti, xb=xb):
                        hh, vv = p0["x"][ti]
                        v.wait_ge(hh, vv)
                        v.tensor_reduce(stat[:, 0:1], xb[:, :], AX.X, ALU.add, negate=True)
                        vedge(v)
                        v.tensor_scalar_mul(stat[:, 0:1], stat[:, 0:1], 1.0 / DM)
                        vinc(v)
                    on("vector", fv)
                    m_nm = sV.v

                    def fs(s, m=m_nm, xb=xb):
                        s.wait_ge(sV.h, m)
                        s.activation(junk[:, :], xb[:, :], AF.Square,
                                     bias=stat[:, 0:1], accum_out=stat2[:, 0:1])
                        sinc(s)
                    on("scalar", fs)
                    m_sq = sS.v

                    def fv2(v, m=m_sq):
                        v.wait_ge(sS.h, m)
                        v.tensor_scalar(stat2[:, 0:1], stat2[:, 0:1], 1.0 / DM, LN_EPS,
                                        ALU.mult, ALU.add)
                        vinc(v)
                    on("vector", fv2)
                    m_v2 = sV.v

                    def fs2(s, m=m_v2):
                        s.wait_ge(sV.h, m)
                        s.activation(stat2[:, 0:1], stat2[:, 0:1], AF.Ln)
                        sedge(s)
                        s.activation(stat2[:, 0:1], stat2[:, 0:1], AF.Exp, scale=-0.5)
                        sinc(s)
                    on("scalar", fs2)
                    m_rs = sS.v

                    def fv3(v, m=m_rs, ti=ti, xb=xb):
                        v.wait_ge(sS.h, m)
                        if ti >= 1:
                            v.wait_ge(sT.h, p0["tp"][ti - 1])
                        v.tensor_scalar(junk[:, :], xb[:, :], stat[:, 0:1],
                                        stat2[:, 0:1], ALU.add, ALU.mult)
                        vedge(v)
                        v.tensor_copy(xn_bf[:, :], junk[:, :])
                        vedge(v)
                        v.tensor_tensor(xnlo_t[:, :], junk[:, :], xn_bf[:, :],
                                        ALU.subtract)
                        vinc(v)
                    on("vector", fv3)
                    p0["ln"][ti] = sV.v
                    evac_base = sV.v

                    def ft(t, ti=ti, evac_base=evac_base):
                        t.wait_ge(sV.h, p0["ln"][ti])
                        for k in range(2 * NDC):
                            dc = k % NDC
                            srcb = xn_bf if k < NDC else xnlo_t
                            if k >= 2:
                                t.wait_ge(sV.h, evac_base + k - 1)
                            elif ti > 0:
                                t.wait_ge(sV.h, p0["evac"][ti - 1] - (1 - k))
                            t.transpose((ps_t if k % 2 == 0 else ps_t2)[:, :],
                                        srcb[:, dc * 128:(dc + 1) * 128], ident[:, :])
                            tinc(t)
                    on("tensor", ft)
                    p0["tp"][ti] = sT.v
                    m_tp0 = sT.v - 2 * NDC

                    def fv4(v, m0=m_tp0, tt=tt, c=c):
                        if tt == 0 and c >= 1:
                            hh, vv = p0["wr"][c - 1]
                            v.wait_ge(hh, vv)
                            v.wait_ge(sT.h, p0["rt"][c - 1])
                        for k in range(2 * NDC):
                            dc = k % NDC
                            dstb = xnT_ch if k < NDC else xnloT_ch
                            v.wait_ge(sT.h, m0 + k + 1)
                            v.tensor_copy(
                                dstb[:, dc * CH + tt * 128: dc * CH + tt * 128 + 128],
                                (ps_t if k % 2 == 0 else ps_t2)[:, :]).then_inc(sV.h, 1)
                            sV.v += 1
                    on("vector", fv4)
                    p0["evac"][ti] = sV.v

                    def ft2(t, tt=tt, ti=ti):
                        t.wait_ge(sV.h, p0["evac"][ti])
                        if ti >= 1:
                            t.wait_ge(sV.h, p0["c"][ti - 1])
                        for ji, (xT, wB) in enumerate(((xnT_ch, wr_bf),
                                                       (xnloT_ch, wr_bf),
                                                       (xnT_ch, wr_lo))):
                            for dc in range(NDC):
                                t.matmul(ps_e[:, 0:NE],
                                         xT[:, dc * CH + tt * 128: dc * CH + tt * 128 + 128],
                                         wB[:, dc * NE:(dc + 1) * NE],
                                         start=(ji == 0 and dc == 0),
                                         stop=(ji == 2 and dc == NDC - 1))
                        tinc(t)
                    on("tensor", ft2)
                    m_lg = sT.v
                    p0["rt"][c] = sT.v

                    def fv5(v, m=m_lg, ti=ti):
                        v.wait_ge(sT.h, m)
                        cc = c_coef[:, ti * NE:(ti + 1) * NE]
                        v.tensor_tensor(cc, ps_e[:, 0:NE], br_rep[:, :], ALU.add)
                        vedge(v)
                        v.tensor_reduce(stat[:, 1:2], cc, AX.X, ALU.max)
                        vedge(v)
                        v.tensor_single_scalar(stat[:, 8:16], cc, stat[:, 1:2], ALU.is_equal)
                        vedge(v)
                        v.scalar_tensor_tensor(junk[:, 0:NE], stat[:, 8:16], -1e30, cc,
                                               ALU.mult, ALU.add)
                        vedge(v)
                        v.tensor_reduce(stat[:, 2:3], junk[:, 0:NE], AX.X, ALU.max)
                        vedge(v)
                        v.tensor_tensor(stat[:, 3:4], stat[:, 2:3], stat[:, 1:2],
                                        ALU.subtract)
                        vinc(v)
                    on("vector", fv5)
                    m_r1 = sV.v

                    def fs3(s, m=m_r1):
                        s.wait_ge(sV.h, m)
                        s.activation(stat[:, 3:4], stat[:, 3:4], AF.Exp)
                        sinc(s)
                    on("scalar", fs3)
                    m_r2 = sS.v

                    def fv6(v, m=m_r2, ti=ti):
                        v.wait_ge(sS.h, m)
                        cc = c_coef[:, ti * NE:(ti + 1) * NE]
                        v.tensor_scalar_add(stat[:, 4:5], stat[:, 3:4], 1.0)
                        vedge(v)
                        v.reciprocal(stat[:, 4:5], stat[:, 4:5])
                        vedge(v)
                        v.tensor_tensor(stat[:, 5:6], stat[:, 3:4], stat[:, 4:5], ALU.mult)
                        v.tensor_single_scalar(junk[:, 0:NE], junk[:, 0:NE], stat[:, 2:3],
                                               ALU.is_equal)
                        v.tensor_scalar_mul(junk[:, 8:16], stat[:, 8:16], stat[:, 4:5])
                        vedge(v)
                        v.scalar_tensor_tensor(cc, junk[:, 0:NE], stat[:, 5:6],
                                               junk[:, 8:16], ALU.mult, ALU.add)
                        vinc(v)
                    on("vector", fv6)
                    p0["c"][ti] = sV.v

                def fq3(q, c=c):
                    q.wait_ge(sV.h, p0["evac"][c * NTT + NTT - 1])
                    for dc in range(NDC):
                        last = stXnW.dma(q, sc_xnT[dc, :, c * CH:(c + 1) * CH],
                                         xnT_ch[:, dc * CH:(dc + 1) * CH])
                    p0["wr"][c] = last
                on("sync", fq3)
                flush()

        def chunk1(c):
            K4 = NTT * (c + 1)
            M = {}
            M1[c] = M

            def fq(q, c=c):
                if c == 0:
                    for hh, vv in stXnW.totals():
                        q.wait_ge(hh, vv)
                    for hh, vv in wsT:
                        q.wait_ge(hh, vv)
                else:
                    q.wait_ge(sT.h, M1[c - 1]["xnT_done"])
                M["xnr"] = stXnR.dma(
                    q, xnT_ch[:, :].rearrange("p (dc t) -> p dc t", dc=NDC),
                    sc_xnT[:, :, c * CH:(c + 1) * CH].rearrange("dc p t -> p dc t"))
                if c >= 2:
                    q.wait_ge(sV.h, M1[c - 2]["rope_done"])
                stCos.dma(q, cosr[:, :].rearrange("p (tt f) -> p tt f", tt=NTT),
                          cos_h[c * CH:(c + 1) * CH, :].rearrange("(tt p) f -> p tt f", p=128))
                M["cos"] = stCos.dma(
                    q, sinr[:, :].rearrange("p (tt f) -> p tt f", tt=NTT),
                    sin_h[c * CH:(c + 1) * CH, :].rearrange("(tt p) f -> p tt f", p=128))
            on("sync", fq)
            m_cos = stCos.totals()

            kvq_sl = []
            rel_sl = []
            slabs = [("kv", dc, 0, sc_win[dc, :, 0:512]) for dc in range(NDC)] + \
                    [("q", dc, og, sc_wcq[dc, :, og * 512:og * 512 + 512])
                     for og in range(2) for dc in range(NDC)]
            for i, (kind, dc, og, src) in enumerate(slabs):
                def fqs(q, i=i, src=src, c=c):
                    if i >= 2:
                        q.wait_ge(sT.h, rel_sl[i - 2])
                    elif c >= 1:
                        q.wait_ge(sT.h, M1[c - 1]["prmm_last"])
                    kvq_sl.append(stWq.dma(q, wq_sl[i % 2][:, :], src))
                on("sync", fqs)
                grp = i // NDC

                def ftm(t, i=i, dc=dc, grp=grp, c=c):
                    hh, vv = kvq_sl[i]
                    t.wait_ge(hh, vv)
                    if dc == 0:
                        hh2, vv2 = M["xnr"]
                        t.wait_ge(hh2, vv2)
                        if grp == 0:
                            if c >= 1:
                                t.wait_ge(sV.h, M1[c - 1]["ps_free"])
                            else:
                                t.wait_ge(sV.h, p0["c"][nch * NTT - 1])
                        elif grp == 1:
                            t.wait_ge(sS.h, M["kv_evac_s"])
                        else:
                            t.wait_ge(sS.h, M["q_evac_s"][0])
                    for tt in range(NTT):
                        pt = (ps_q if tt < 2 else ps_mo)[:, (tt % 2) * 512:(tt % 2) * 512 + 512]
                        t.matmul(pt, xnT_ch[:, dc * CH + tt * 128: dc * CH + tt * 128 + 128],
                                 wq_sl[i % 2][:, :], start=(dc == 0), stop=(dc == NDC - 1))
                    tinc(t)
                on("tensor", ftm)
                rel_sl.append(sT.v)
                if grp == 0 and dc == NDC - 1:
                    M["kv_mm"] = sT.v
                    emit_kv_evac(c, M, m_cos)
                if grp == 1 and dc == NDC - 1:
                    M["q_mm0"] = sT.v
                    emit_q_evac(c, M, og=0)
                if grp == 2 and dc == NDC - 1:
                    M["q_mm1"] = sT.v
                    emit_q_evac(c, M, og=1)
            M["a_sl_done"] = rel_sl[-1]

            emit_qrow64(c, M)
            emit_mamba_front(c, M)
            att_state = emit_attention_init(c, M)
            scn_state = emit_scan_init(c, M)
            emit_scan_head(c, M, scn_state, 0)
            emit_scan_head(c, M, scn_state, 1)
            for hp in range(8):
                ki, ni = 0, 0
                heads_done = hp == 7
                while ki < K4 or ni < 32:
                    if not heads_done and ni >= 16:
                        emit_scan_head(c, M, scn_state, 2 * hp + 2)
                        emit_scan_head(c, M, scn_state, 2 * hp + 3)
                        heads_done = True
                    if ki < K4 and (ni * K4 >= ki * 32 or ni >= 32):
                        emit_attn_kp(c, M, att_state, hp, ki, K4)
                        ki += 1
                    else:
                        it, n = (2 * hp + ni // 16, ni % 16)
                        emit_scan_n(c, M, scn_state, it, n)
                        ni += 1
                if not heads_done:
                    emit_scan_head(c, M, scn_state, 2 * hp + 2)
                    emit_scan_head(c, M, scn_state, 2 * hp + 3)
                emit_attn_pass_end(c, M, att_state, hp)
                emit_scan_tail(c, M, scn_state, 2 * hp)
                emit_scan_tail(c, M, scn_state, 2 * hp + 1)
            M["attn_done"] = M["attn_done_v"]
            emit_tail(c, M)

        def emit_kv_evac(c, M, m_cos):
            M["ktp_evac"] = []
            for tt in range(NTT):
                pt = (ps_q if tt < 2 else ps_mo)[:, (tt % 2) * 512:(tt % 2) * 512 + 512]

                def fsv(s, pt=pt, tt=tt, c=c):
                    s.wait_ge(sT.h, M["kv_mm"])
                    for hh, vv in m_cos:
                        s.wait_ge(hh, vv)
                    if tt == 0 and c >= 1:
                        for hh, vv in M1[c - 1]["kT_wr"]:
                            s.wait_ge(hh, vv)
                    if tt >= 1:
                        hh, vv = M["vh_wr"][tt - 1]
                        s.wait_ge(hh, vv)
                        s.wait_ge(sV.h, M["kv_rope"][tt - 1])
                    s.copy(kv_sb[:, :], pt[:, 0:256])
                    for kv in range(KVH):
                        s.activation(vh_st[:, kv * 65:kv * 65 + 64],
                                     pt[:, 256 + kv * 64:256 + kv * 64 + 64], AF.Copy)
                    sinc(s)
                on("scalar", fsv)
                m_k = sS.v

                def fvv(v, m=m_k):
                    v.wait_ge(sS.h, m)
                    for kv in range(KVH):
                        v.memset(vh_st[:, kv * 65 + 64:kv * 65 + 65], 1.0)
                    v.tensor_tensor(junk[:, 0:256], kv_sb[:, :], kv_sb[:, :], ALU.mult)
                    vedge(v)
                    v.tensor_reduce(stat[:, 0:4],
                                    junk[:, 0:256].rearrange("p (h d) -> p h d", h=KVH),
                                    AX.X, ALU.add)
                    vedge(v)
                    v.tensor_scalar(stat[:, 0:4], stat[:, 0:4], 1.0 / HD, RMS_EPS,
                                    ALU.mult, ALU.add)
                    vinc(v)
                on("vector", fvv)
                m_ms = sV.v

                def fsr(s, m=m_ms):
                    s.wait_ge(sV.h, m)
                    s.activation(stat[:, 0:4], stat[:, 0:4], AF.Ln)
                    sedge(s)
                    s.activation(stat[:, 0:4], stat[:, 0:4], AF.Exp, scale=-0.5)
                    sinc(s)
                on("scalar", fsr)
                m_rs = sS.v

                def fvr(v, m=m_rs, tt=tt):
                    v.wait_ge(sS.h, m)
                    for kv in range(KVH):
                        v.tensor_scalar_mul(kv_sb[:, kv * 64:(kv + 1) * 64],
                                            kv_sb[:, kv * 64:(kv + 1) * 64],
                                            stat[:, kv:kv + 1])
                    kvv = kv_sb[:, :].rearrange("p (h d) -> p h d", h=KVH)
                    cv = cosr[:, tt * 256:(tt + 1) * 256].rearrange("p (h d) -> p h d", h=H)[:, 0:KVH, :]
                    sv = sinr[:, tt * 256:(tt + 1) * 256].rearrange("p (h d) -> p h d", h=H)[:, 0:KVH, :]
                    a = kvv[:, :, 0:16]; b = kvv[:, :, 16:32]
                    vedge(v)
                    j3 = lambda o: junk[:, o:o + 64].rearrange("p (h d) -> p h d", h=KVH)
                    v.tensor_tensor(j3(0), a, cv, ALU.mult)
                    v.tensor_tensor(j3(64), b, sv, ALU.mult)
                    v.tensor_tensor(j3(128), b, cv, ALU.mult)
                    v.tensor_tensor(j3(192), a, sv, ALU.mult)
                    vedge(v)
                    v.tensor_tensor(a, j3(0), j3(64), ALU.add)
                    v.tensor_tensor(b, j3(128), j3(192), ALU.subtract)
                    vedge(v)
                    if tt >= 1:
                        v.wait_ge(sT.h, M["ktp_t"][tt - 1])
                    v.tensor_copy(cvt[:, 0:256], kv_sb[:, :])
                    vinc(v)
                on("vector", fvr)
                M.setdefault("kv_rope", []).append(sV.v)
                m_kro = sV.v
                evb = sV.v

                def ftk(t, m=m_kro, tt=tt, evb=evb, c=c):
                    t.wait_ge(sV.h, m)
                    for kv in range(KVH):
                        if kv >= 2:
                            t.wait_ge(sV.h, evb + kv - 1)
                        elif tt > 0:
                            t.wait_ge(sV.h, M["ktp_evac"][tt - 1] - (1 - kv))
                        elif c >= 1:
                            t.wait_ge(sV.h, M1[c - 1]["qtp_evac"][-1] - (1 - kv))
                            t.wait_ge(sV.h, M1[c - 1]["mg"][-1])
                        t.transpose((ps_t if kv % 2 == 0 else ps_t2)[0:64, 0:128],
                                    cvt[:, kv * 64:(kv + 1) * 64], ident[:, :])
                        tinc(t)
                on("tensor", ftk)
                M.setdefault("ktp_t", []).append(sT.v)
                m_tk0 = sT.v - KVH

                def fvk(v, m0=m_tk0, tt=tt):
                    for kv in range(KVH):
                        v.wait_ge(sT.h, m0 + kv + 1)
                        v.tensor_copy(
                            kT_ch[0:64, kv * CH + tt * 128: kv * CH + tt * 128 + 128],
                            (ps_t if kv % 2 == 0 else ps_t2)[0:64, 0:128]).then_inc(sV.h, 1)
                        sV.v += 1
                    M["ktp_evac"].append(sV.v)
                on("vector", fvk)
                m_vst = sV.v

                def fqv(q, tt=tt, c=c, m=m_vst):
                    q.wait_ge(sV.h, m)
                    M.setdefault("vh_wr", []).append(
                        stKW.dma(q, sc_vh[c * NTT + tt, :, :], vh_st[:, :]))
                on("sync", fqv)
            M["kv_evac_s"] = sS.v

            def fvo(v):
                v.memset(kT_ch[64:65, :], 1.0)
                vinc(v)
            on("vector", fvo)
            m_ones = sV.v

            def fqk(q, c=c, m=m_ones):
                q.wait_ge(sV.h, m)
                for tt in range(NTT):
                    stKW.dma(q, sc_kT[c * NTT + tt, :, :, :],
                             kT_ch[0:65, :].rearrange("p (kv t) -> p kv t", kv=KVH)[:, :, tt * 128:(tt + 1) * 128])
            on("sync", fqk)
            M["kT_wr"] = stKW.totals()

        def emit_q_evac(c, M, og):
            M.setdefault("q_evac_s", {})
            M.setdefault("qtp_evac", [0])
            M.setdefault("q_rope", [])
            M.setdefault("qtp_t", [])
            mm_m = M["q_mm0"] if og == 0 else M["q_mm1"]
            for tt in range(NTT):
                pt = (ps_q if tt < 2 else ps_mo)[:, (tt % 2) * 512:(tt % 2) * 512 + 512]

                def fse(s, pt=pt, og=og, tt=tt, mm_m=mm_m):
                    s.wait_ge(sT.h, mm_m)
                    if M["q_rope"]:
                        s.wait_ge(sV.h, M["q_rope"][-1])
                    s.copy(q_sb[:, og * 512:og * 512 + 512], pt[:, :])
                    sinc(s)
                on("scalar", fse)
                m_c = sS.v

                def fvq(v, m=m_c, og=og):
                    v.wait_ge(sS.h, m)
                    qs = q_sb[:, og * 512:og * 512 + 512]
                    v.tensor_tensor(junk[:, 0:512], qs, qs, ALU.mult)
                    vedge(v)
                    v.tensor_reduce(stat[:, 0:8],
                                    junk[:, 0:512].rearrange("p (h d) -> p h d", h=8),
                                    AX.X, ALU.add)
                    vedge(v)
                    v.tensor_scalar(stat[:, 0:8], stat[:, 0:8], 1.0 / HD, RMS_EPS,
                                    ALU.mult, ALU.add)
                    vinc(v)
                on("vector", fvq)
                m_ms = sV.v

                def fsr(s, m=m_ms):
                    s.wait_ge(sV.h, m)
                    s.activation(stat[:, 0:8], stat[:, 0:8], AF.Ln)
                    sedge(s)
                    s.activation(stat[:, 0:8], stat[:, 0:8], AF.Exp, scale=-0.5)
                    sinc(s)
                on("scalar", fsr)
                m_rs = sS.v

                def fvr(v, m=m_rs, og=og, tt=tt):
                    v.wait_ge(sS.h, m)
                    v.tensor_tensor(stat[:, 0:8], stat[:, 0:8],
                                    grep[:, og * 8:og * 8 + 8], ALU.mult)
                    qs = q_sb[:, og * 512:og * 512 + 512]
                    for h8 in range(8):
                        v.tensor_scalar_mul(qs[:, h8 * 64:(h8 + 1) * 64],
                                            qs[:, h8 * 64:(h8 + 1) * 64],
                                            stat[:, h8:h8 + 1])
                    qv3 = qs.rearrange("p (h d) -> p h d", h=8)
                    cv = cosr[:, tt * 256:(tt + 1) * 256].rearrange("p (h d) -> p h d", h=H)[:, 0:8, :]
                    sv = sinr[:, tt * 256:(tt + 1) * 256].rearrange("p (h d) -> p h d", h=H)[:, 0:8, :]
                    a = qv3[:, :, 0:16]; b = qv3[:, :, 16:32]
                    vedge(v)
                    j3 = lambda o: junk[:, o:o + 128].rearrange("p (h d) -> p h d", h=8)
                    v.tensor_tensor(j3(0), a, cv, ALU.mult)
                    v.tensor_tensor(j3(128), b, sv, ALU.mult)
                    v.tensor_tensor(j3(256), b, cv, ALU.mult)
                    v.tensor_tensor(j3(384), a, sv, ALU.mult)
                    vedge(v)
                    v.tensor_tensor(a, j3(0), j3(128), ALU.add)
                    v.tensor_tensor(b, j3(256), j3(384), ALU.subtract)
                    vedge(v)
                    if M["qtp_t"]:
                        v.wait_ge(sT.h, M["qtp_t"][-1])
                    else:
                        v.wait_ge(sT.h, M["ktp_t"][-1])
                    v.tensor_copy(cvt[:, 0:512], qs)
                    vinc(v)
                on("vector", fvr)
                M["q_rope"].append(sV.v)
                m_ro = sV.v
                evb = sV.v

                def ftq(t, m=m_ro, evb=evb, og=og, tt=tt):
                    t.wait_ge(sV.h, m)
                    for h8 in range(8):
                        if h8 >= 2:
                            t.wait_ge(sV.h, evb + h8 - 1)
                        else:
                            prev = M["qtp_evac"][-1] if (og > 0 or tt > 0) else M["ktp_evac"][-1]
                            t.wait_ge(sV.h, prev - (1 - h8))
                        t.transpose((ps_t if h8 % 2 == 0 else ps_t2)[0:64, 0:128],
                                    cvt[:, h8 * 64:(h8 + 1) * 64], ident[:, :])
                        tinc(t)
                on("tensor", ftq)
                M["qtp_t"].append(sT.v)
                m_tq0 = sT.v - 8

                def fvk(v, m0=m_tq0, og=og, tt=tt):
                    for h8 in range(8):
                        h = og * 8 + h8
                        v.wait_ge(sT.h, m0 + h8 + 1)
                        v.tensor_copy(
                            qT_sb[0:64, h * CH + tt * 128: h * CH + tt * 128 + 128],
                            (ps_t if h8 % 2 == 0 else ps_t2)[0:64, 0:128]).then_inc(sV.h, 1)
                        sV.v += 1
                    M["qtp_evac"].append(sV.v)
                on("vector", fvk)
            M["q_evac_s"][og] = sS.v

        def emit_qrow64(c, M):
            def fv(v):
                v.memset(qT_sb[64:65, :], 1.0)
                vedge(v)
                for h in range(H):
                    v.tensor_scalar_mul(qT_sb[64:65, h * CH:(h + 1) * CH],
                                        qT_sb[64:65, h * CH:(h + 1) * CH],
                                        negg64[64:65, h:h + 1])
                vinc(v)
            on("vector", fv)
            M["qrow"] = sV.v
            M["rope_done"] = sV.v

        def emit_attention_init(c, M):
            M["mg"] = []
            M["yT_evac"] = [0] * 8
            return {"kr": [], "vr": [], "av": [], "exp": [], "gi": 0}

        def emit_attn_kp(c, M, st, hp, kp, K4):
            kr, vr, av_done, exp_done = st["kr"], st["vr"], st["av"], st["exp"]
            h0, h1 = 2 * hp, 2 * hp + 1
            gi = st["gi"]; st["gi"] += 1

            def fqs(q, gi=gi, kp=kp, c=c):
                if gi == 0:
                    for hh, vv in stKW.totals():
                        q.wait_ge(hh, vv)
                    q.wait_ge(sV.h, M["qrow"])
                if gi >= 2:
                    q.wait_ge(sT.h, av_done[gi - 2])
                kr.append(stKR.dma(q, kT_sl[gi % 2][0:65, :],
                                   sc_kT[kp, :, :, :].rearrange("p kv t -> p (kv t)")))
                vr.append(stVR.dma(q, vh_sl[gi % 2][:, :], sc_vh[kp, :, :]))
            on("sync", fqs)

            def fts(t, gi=gi, h0=h0, h1=h1, kp=kp,
                    m_pe=(exp_done[-1] if exp_done else None),
                    m_mg=(M["mg"][-1] if (kp == 0 and M["mg"]) else None)):
                hh, vv = kr[gi]
                t.wait_ge(hh, vv)
                if m_mg is not None:
                    t.wait_ge(sV.h, m_mg)
                if gi == 0:
                    t.wait_ge(sS.h, M["q_evac_s"][1])
                    t.wait_ge(sS.h, M["reps"])
                    t.wait_ge(sV.h, M["bc"])
                if m_pe is not None:
                    t.wait_ge(sS.h, m_pe)
                for ih, h in enumerate((h0, h1)):
                    t.matmul(ps_q[:, ih * 512:(ih + 1) * 512],
                             kT_sl[gi % 2][0:65, (h // 4) * 128:(h // 4) * 128 + 128],
                             qT_sb[0:65, h * CH:(h + 1) * CH],
                             start=True, stop=True)
                tinc(t)
            on("tensor", fts)
            m_sc = sT.v

            def fse(s, m=m_sc, gi=gi):
                s.wait_ge(sT.h, m)
                if gi >= 2:
                    s.wait_ge(sT.h, av_done[gi - 2])
                s.activation(probs[gi % 2][:, :], ps_q[:, :], AF.Exp, scale=0.125)
                sinc(s)
            on("scalar", fse)
            exp_done.append(sS.v)
            m_exp = sS.v

            m_mask = None
            if kp >= 4 * c:
                j = kp - 4 * c
                def fgm(g, m=m_exp, gi=gi, j=j, c=c):
                    g.wait_ge(sS.h, m)
                    for ih in range(2):
                        g.affine_select(probs[gi % 2][:, ih * 512:(ih + 1) * 512],
                                        probs[gi % 2][:, ih * 512:(ih + 1) * 512],
                                        [[1, 512]], ALU.is_ge, 0.0,
                                        base=-128 * j, channel_multiplier=-1)
                    g.last.then_inc(sG.h, 1)
                on("gpsimd", fgm)
                sG.v += 1
                m_mask = sG.v

            def fta(t, gi=gi, kp=kp, h0=h0, h1=h1, hp=hp, c=c,
                    m_exp=m_exp, m_mask=m_mask, K4=K4):
                hh, vv = vr[gi]
                t.wait_ge(hh, vv)
                if m_mask is None:
                    t.wait_ge(sS.h, m_exp)
                else:
                    t.wait_ge(sG.h, m_mask)
                if kp == 0 and hp >= 1:
                    t.wait_ge(sS.h, M["yTc"][hp - 1])
                elif kp == 0:
                    t.wait_ge(sV.h, M["bc"])
                    t.wait_ge(sS.h, M["xin_done"][NIT - 1])
                for ih, h in enumerate((h0, h1)):
                    t.matmul(ps_y[0:65, ih * 512:(ih + 1) * 512],
                             vh_sl[gi % 2][:, (h // 4) * 65:(h // 4) * 65 + 65],
                             probs[gi % 2][:, ih * 512:(ih + 1) * 512],
                             start=(kp == 0), stop=(kp == K4 - 1))
                tinc(t)
            on("tensor", fta)
            av_done.append(sT.v)

        def emit_attn_pass_end(c, M, st, hp):
            m_av = st["av"][-1]
            for ih, h in enumerate((2 * hp, 2 * hp + 1)):
                m_mg0 = M["mg"][-1] if M["mg"] else None

                def fsc2(s, m=m_av, ih=ih, m_mg=m_mg0):
                    s.wait_ge(sT.h, m)
                    if m_mg is not None:
                        s.wait_ge(sV.h, m_mg)
                    s.copy(yT_sb[0:65, :], ps_y[0:65, ih * 512:(ih + 1) * 512])
                    sedge(s)
                    s.activation(linv[0:1, :], yT_sb[64:65, :], AF.Ln)
                    sedge(s)
                    s.activation(linv[0:1, :], linv[0:1, :], AF.Exp, scale=-1.0)
                    sinc(s)
                on("scalar", fsc2)
                m_ls = sS.v

                def fvl(v, m=m_ls):
                    v.wait_ge(sS.h, m)
                    v.tensor_scalar_mul(linv[0:1, :], linv[0:1, :], w_col[0:1, 0:1])
                    vedge(v)
                    v.tensor_copy(linv_bf[0:1, :], linv[0:1, :])
                    vinc(v)
                on("vector", fvl)
                m_li = sV.v

                def ftb(t, m=m_li, h=h, c=c):
                    t.wait_ge(sV.h, m)
                    if M["mg"]:
                        t.wait_ge(sV.h, M["mg"][-1])
                    t.matmul(ps_q[0:64, 0:512], ones1[0:1, 0:64],
                             linv_bf[0:1, :], start=True, stop=True)
                    tinc(t)
                on("tensor", ftb)
                m_rr = sT.v

                def fvg(v, m=m_rr, h=h):
                    v.wait_ge(sT.h, m)
                    base = (h % 2) * 64
                    v.tensor_tensor(
                        mergedT[base:base + 64, (h // 2) * CH:(h // 2) * CH + CH],
                        yT_sb[0:64, :], ps_q[0:64, 0:512], ALU.mult)
                    vinc(v)
                on("vector", fvg)
                M["mg"].append(sV.v)
            M.setdefault("yTc", [0] * 8)[hp] = sS.v
            M["yT_evac"][hp] = sV.v
            if hp == 7:
                M["attn_done_v"] = sV.v

        def emit_mamba_front(c, M):
            win_rd = []
            M["xssm_mm"] = []
            M["conv_carry"] = []
            M["xin_done"] = []
            M["xs_done"] = []
            for it in range(NIT):
                def fqw(q, it=it):
                    if it >= 2:
                        q.wait_ge(sT.h, M["xssm_mm"][it - 2])
                    win_rd.append(stWin.dma(
                        q, win_sl[it % 2][:, :].rearrange("p (dc o) -> p dc o", dc=NDC),
                        sc_win[:, :, 512 + it * 128:512 + it * 128 + 128].rearrange("dc p o -> p dc o")))
                on("sync", fqw)

                def ftx(t, it=it, c=c):
                    hh, vv = win_rd[it]
                    t.wait_ge(hh, vv)
                    if it == 0:
                        if c >= 1:
                            t.wait_ge(sV.h, M1[c - 1]["mo_evac"][-1])
                    else:
                        t.wait_ge(sV.h, M["conv_carry"][it - 1])
                        t.wait_ge(sS.h, M["xin_done"][it - 1])
                    for dc in range(NDC):
                        t.matmul(ps_e[:, :], win_sl[it % 2][:, dc * 128:(dc + 1) * 128],
                                 xnT_ch[:, dc * CH:(dc + 1) * CH],
                                 start=(dc == 0), stop=(dc == NDC - 1))
                    tinc(t)
                on("tensor", ftx)
                M["xssm_mm"].append(sT.v)
                m_mm = sT.v

                def fvc(v, m=m_mm, it=it):
                    v.wait_ge(sT.h, m)
                    v.tensor_copy(sp_tmp[:, 0:3], ps_e[:, 509:512])
                    v.tensor_copy(xin_bf[:, 0:3], conv_carry[:, it * 3:(it + 1) * 3])
                    vedge(v)
                    v.tensor_copy(conv_carry[:, it * 3:(it + 1) * 3], sp_tmp[:, 0:3])
                    vinc(v)
                on("vector", fvc)
                M["conv_carry"].append(sV.v)
                m_cs = sV.v

                def fsc(s, m=m_cs):
                    s.wait_ge(sV.h, m)
                    s.activation(xin_bf[:, 3:515], ps_e[:, :], AF.Copy)
                    sinc(s)
                on("scalar", fsc)
                M["xin_done"].append(sS.v)
                m_xin = sS.v

                def fvt(v, m=m_xin, it=it):
                    v.wait_ge(sS.h, m)
                    v.tensor_scalar_mul(cvt[:, :], xin_bf[:, 0:512],
                                        cwv[:, it * 4:it * 4 + 1])
                    for j in range(1, 4):
                        vedge(v)
                        v.scalar_tensor_tensor(cvt[:, :], xin_bf[:, j:512 + j],
                                               cwv[:, it * 4 + j:it * 4 + j + 1],
                                               cvt[:, :], ALU.mult, ALU.add)
                    vinc(v)
                on("vector", fvt)
                m_cv = sV.v

                def fss(s, m=m_cv, it=it):
                    s.wait_ge(sV.h, m)
                    s.activation(xs_bf[:, it * CH:(it + 1) * CH], cvt[:, :], AF.Silu,
                                 bias=cb_sb[:, it:it + 1])
                    sinc(s)
                on("scalar", fss)
                M["xs_done"].append(sS.v)

            def ftp(t):
                t.wait_ge(sS.h, M["xs_done"][NIT - 1])
                t.wait_ge(sV.h, M["conv_carry"][NIT - 1])
                t.wait_ge(sS.h, M["xin_done"][NIT - 1])
                for o0, o1, pp in [(0, 64, ps_e[0:64, :]), (64, 80, ps_q[0:16, 0:512]),
                                   (80, 96, ps_mo[0:16, 0:512])]:
                    for ic in range(NIT):
                        t.matmul(pp, wxpT_bf[:, ic * 96 + o0:ic * 96 + o1],
                                 xs_bf[:, ic * CH:(ic + 1) * CH],
                                 start=(ic == 0), stop=(ic == NIT - 1))
                tinc(t)
            on("tensor", ftp)
            m_xp = sT.v

            def fvp(v, m=m_xp):
                v.wait_ge(sT.h, m)
                v.tensor_copy(dtlowT_bf[0:64, :], ps_e[0:64, :])
                v.tensor_copy(B_bf[0:16, :], ps_q[0:16, 0:512])
                v.tensor_copy(C_bf[0:16, :], ps_mo[0:16, 0:512])
                v.tensor_scalar(stat[0:16, 0:1], w_col[0:16, 0:1], -1.0, 1.0,
                                ALU.mult, ALU.add)
                vedge(v)
                v.tensor_scalar_mul(C_bf[0:16, :], C_bf[0:16, :], stat[0:16, 0:1])
                vinc(v)
            on("vector", fvp)
            M["bc"] = sV.v

            rep_ev = []
            for n in range(NST):
                def ftr(t, n=n):
                    if n == 0:
                        t.wait_ge(sV.h, M["bc"])
                    else:
                        t.wait_ge(sS.h, rep_ev[n - 1])
                    t.matmul(ps_e[:, :], oneh[0:16, n * 128:(n + 1) * 128],
                             B_bf[0:16, :], start=True, stop=True)
                    t.matmul(ps_q[:, 0:512], oneh[0:16, n * 128:(n + 1) * 128],
                             C_bf[0:16, :], start=True, stop=True)
                    tinc(t)
                on("tensor", ftr)
                m_r = sT.v

                def fsr(s, m=m_r, n=n):
                    s.wait_ge(sT.h, m)
                    s.activation(Brep[:, n * CH:(n + 1) * CH], ps_e[:, :], AF.Copy)
                    s.activation(Crep[:, n * CH:(n + 1) * CH], ps_q[:, 0:512], AF.Copy)
                    sinc(s)
                on("scalar", fsr)
                rep_ev.append(sS.v)
            M["reps"] = sS.v

        def emit_scan_init(c, M):
            M["delta"] = []
            M["gmm"] = []
            M["silu_g"] = []
            M["gated"] = []
            M["g_rd"] = []
            return {"dA_m": {}, "scan_m": {}, "ro_m": {}}

        def emit_scan_head(c, M, st, it):
            dA_m, scan_m, ro_m = st["dA_m"], st["scan_m"], st["ro_m"]
            ps_dt = ps_mo[:, 512:1024]
            if True:
                def ftd(t, it=it):
                    if it == 0:
                        t.wait_ge(sS.h, M["reps"])
                        t.wait_ge(sS.h, M["q_evac_s"][1])
                    else:
                        t.wait_ge(sV.h, M["delta"][it - 1])
                    t.matmul(ps_dt, wdtT_bf[0:64, it * 128:(it + 1) * 128],
                             dtlowT_bf[0:64, :], start=True, stop=True)
                    tinc(t)
                on("tensor", ftd)
                m_dt = sT.v
                dpar = delta_b[it % 4]

                def fvd(v, m=m_dt, it=it, dpar=dpar):
                    v.wait_ge(sT.h, m)
                    if it >= 4:
                        v.wait_ge(sS.h, dA_m[(it - 4, NST - 1)])
                    v.tensor_scalar_add(dpar[:, :], ps_dt, bdt_sb[:, it:it + 1])
                    vinc(v)
                on("vector", fvd)
                m_z = sV.v

                def fsd(s, m=m_z, dpar=dpar, it=it):
                    s.wait_ge(sV.h, m)
                    if it > 0:
                        s.wait_ge(sV.h, M["silu_gv_l"][it - 1])
                    s.activation(sp_tmp[:, :], dpar[:, :], AF.Abs)
                    sedge(s)
                    s.activation(sp_tmp[:, :], sp_tmp[:, :], AF.Exp, scale=-1.0)
                    sedge(s)
                    s.activation(sp_tmp[:, :], sp_tmp[:, :], AF.Ln, bias=1.0)
                    sinc(s)
                on("scalar", fsd)
                m_sp = sS.v

                def fvz(v, m=m_sp, dpar=dpar, it=it):
                    v.wait_ge(sS.h, m)
                    v.scalar_tensor_tensor(dpar[:, :], dpar[:, :], 0.0, sp_tmp[:, :],
                                           ALU.max, ALU.add)
                    vedge(v)
                    v.tensor_tensor(du_b[:, :], dpar[:, :],
                                    xs_bf[:, it * CH:(it + 1) * CH], ALU.mult)
                    vinc(v)
                on("vector", fvz)
                m_du = sV.v
                M["delta"].append(sV.v)

                def fqg(q, it=it):
                    if it >= 2:
                        q.wait_ge(sT.h, M["gmm"][it - 2])
                    M["g_rd"].append(stWing.dma(
                        q, wing_sl[it % 2][:, :].rearrange("p (dc o) -> p dc o", dc=NDC),
                        sc_win[:, :, 2560 + it * 128:2560 + it * 128 + 128].rearrange("dc p o -> p dc o")))
                on("sync", fqg)

                def ftg(t, it=it):
                    hh, vv = M["g_rd"][it]
                    t.wait_ge(hh, vv)
                    if it == 0:
                        t.wait_ge(sS.h, M["reps"])
                        t.wait_ge(sV.h, M["bc"])
                    else:
                        t.wait_ge(sV.h, M["silu_gv_l"][it - 1])
                    for dc in range(NDC):
                        t.matmul(ps_mo[:, 0:512], wing_sl[it % 2][:, dc * 128:(dc + 1) * 128],
                                 xnT_ch[:, dc * CH:(dc + 1) * CH],
                                 start=(dc == 0), stop=(dc == NDC - 1))
                    tinc(t)
                on("tensor", ftg)
                M["gmm"].append(sT.v)
                m_g = sT.v

                def fsg(s, m=m_g, it=it):
                    s.wait_ge(sT.h, m)
                    s.wait_ge(sV.h, M["delta"][it])
                    if it > 3:
                        s.wait_ge(sV.h, M["gated"][it - 4])
                    s.activation(sp_tmp[:, :], ps_mo[:, 0:512], AF.Tanh, scale=0.5)
                    sinc(s)
                on("scalar", fsg)
                m_th = sS.v

                def fvsg(v, m=m_th, it=it):
                    v.wait_ge(sS.h, m)
                    v.tensor_scalar(cvt[:, :], sp_tmp[:, :], 0.5, 0.5, ALU.mult, ALU.add)
                    vedge(v)
                    v.tensor_tensor(silu_g[it % 4][:, :], ps_mo[:, 0:512], cvt[:, :],
                                    ALU.mult)
                    vinc(v)
                on("vector", fvsg)
                M["silu_g"].append(sS.v)
                M.setdefault("silu_gv_l", []).append(sV.v)
                st[("mdu", it)] = m_du
                st[("dpar", it)] = dpar

        def emit_scan_n(c, M, st, it, n):
            dA_m, scan_m, ro_m = st["dA_m"], st["scan_m"], st["ro_m"]
            m_du = st[("mdu", it)]
            dpar = st[("dpar", it)]
            par = n % 2

            def fsa(s, it=it, n=n, par=par, dpar=dpar, m_du=m_du):
                s.wait_ge(sV.h, m_du)
                if n >= 2:
                    s.wait_ge(sV.h, scan_m[(it, n - 2)])
                elif it >= 1:
                    s.wait_ge(sV.h, scan_m[(it - 1, NST - 2 + n)])
                s.activation(dA_b[par][:, :], dpar[:, :], AF.Exp,
                             scale=A_sb[:, it * NST + n:it * NST + n + 1])
                sinc(s)
            on("scalar", fsa)
            dA_m[(it, n)] = sS.v
            if par == 0:
                st["pend"] = (n, sS.v)
                return
            n0, m_da0 = st.pop("pend")
            m_da1 = sS.v

            def fvs2(v, it=it, n0=n0, n1=n, m_da0=m_da0, m_da1=m_da1):
                v.wait_ge(sS.h, m_da0)
                v.tensor_tensor(dBu_b[0][:, :], du_b[:, :],
                                Brep[:, n0 * CH:(n0 + 1) * CH], ALU.mult)
                e_b0 = vinc(v)
                v.wait_ge(sS.h, m_da1)
                v.tensor_tensor(dBu_b[1][:, :], du_b[:, :],
                                Brep[:, n1 * CH:(n1 + 1) * CH], ALU.mult)
                e_b1 = vinc(v)
                v.wait_ge(sV.h, e_b0)
                v.tensor_tensor_scan(
                    h_stk[:, n0 * CH:(n0 + 1) * CH],
                    dA_b[0][:, :], dBu_b[0][:, :],
                    scan_carry[:, it * NST + n0:it * NST + n0 + 1],
                    ALU.mult, ALU.add)
                scan_m[(it, n0)] = vinc(v)
                v.wait_ge(sV.h, e_b1)
                v.tensor_tensor_scan(
                    h_stk[:, n1 * CH:(n1 + 1) * CH],
                    dA_b[1][:, :], dBu_b[1][:, :],
                    scan_carry[:, it * NST + n1:it * NST + n1 + 1],
                    ALU.mult, ALU.add)
                scan_m[(it, n1)] = vinc(v)
                v.wait_ge(sV.h, scan_m[(it, n0)])
                v.tensor_copy(scan_carry[:, it * NST + n0:it * NST + n0 + 1],
                              h_stk[:, n0 * CH + CH - 1:n0 * CH + CH])
                v.tensor_tensor(h_stk[:, n0 * CH:(n0 + 1) * CH],
                                h_stk[:, n0 * CH:(n0 + 1) * CH],
                                Crep[:, n0 * CH:(n0 + 1) * CH], ALU.mult)
                v.wait_ge(sV.h, scan_m[(it, n1)])
                v.tensor_copy(scan_carry[:, it * NST + n1:it * NST + n1 + 1],
                              h_stk[:, n1 * CH + CH - 1:n1 * CH + CH])
                v.tensor_tensor(h_stk[:, n1 * CH:(n1 + 1) * CH],
                                h_stk[:, n1 * CH:(n1 + 1) * CH],
                                Crep[:, n1 * CH:(n1 + 1) * CH], ALU.mult)
                vinc(v)
            on("vector", fvs2)

        def emit_scan_tail(c, M, st, it):
            dA_m, scan_m, ro_m = st["dA_m"], st["scan_m"], st["ro_m"]
            if True:
                def fvtree(v, it=it):
                    for width in (8 * CH, 4 * CH, 2 * CH, CH):
                        v.tensor_tensor(h_stk[:, 0:width], h_stk[:, 0:width],
                                        h_stk[:, width:2 * width], ALU.add)
                        vedge(v)
                    v.scalar_tensor_tensor(cvt[:, :], xs_bf[:, it * CH:(it + 1) * CH],
                                           Dw_sb[:, it:it + 1], h_stk[:, 0:CH],
                                           ALU.mult, ALU.add)
                    vedge(v)
                    v.tensor_tensor(xs_bf[:, it * CH:(it + 1) * CH], cvt[:, :],
                                    silu_g[it % 4][:, :], ALU.mult)
                    vinc(v)
                on("vector", fvtree)
                M["gated"].append(sV.v)

        def emit_tail(c, M):
            mo_rd = []
            M["momm"] = []
            M["mo_evac"] = []
            for ot in range(NDC):
                def fqm(q, ot=ot):
                    if ot >= 2:
                        q.wait_ge(sT.h, M["momm"][ot - 2])
                    mo_rd.append(stWmo.dma(
                        q, wmo_sl[ot % 2][:, :].rearrange("p (ic o) -> p ic o", ic=NIT),
                        sc_wmo[:, :, ot * 128:(ot + 1) * 128].rearrange("ic p o -> p ic o")))
                on("sync", fqm)

                def ftm(t, ot=ot):
                    hh, vv = mo_rd[ot]
                    t.wait_ge(hh, vv)
                    if ot == 0:
                        t.wait_ge(sV.h, M["gated"][NIT - 1])
                    else:
                        t.wait_ge(sV.h, M["mo_evac"][ot - 1])
                    for ic in range(NIT):
                        t.matmul(ps_e[:, :], wmo_sl[ot % 2][:, ic * 128:(ic + 1) * 128],
                                 xs_bf[:, ic * CH:(ic + 1) * CH],
                                 start=(ic == 0), stop=(ic == NIT - 1))
                    tinc(t)
                on("tensor", ftm)
                M["momm"].append(sT.v)
                m_mo = sT.v

                def fvm(v, m=m_mo, ot=ot):
                    v.wait_ge(sT.h, m)
                    v.tensor_tensor(mergedT[:, ot * CH:(ot + 1) * CH],
                                    mergedT[:, ot * CH:(ot + 1) * CH],
                                    ps_e[:, :], ALU.add)
                    vinc(v)
                on("vector", fvm)
                M["mo_evac"].append(sV.v)
            M["merged"] = sV.v

            pr_rd = []
            M["prmm"] = []
            M["epi_wr"] = []
            prev_epi = M1[c - 1]["epi_wr"] if c >= 1 else None
            for pr in range(2):
                for og in range(2):
                    for dc in range(NDC):
                        i = pr * 16 + og * NDC + dc

                        def fqp(q, i=i, og=og, dc=dc, c=c):
                            if i >= 2:
                                q.wait_ge(sT.h, M["prmm"][i - 2])
                            else:
                                q.wait_ge(sT.h, M["a_sl_done"])
                            pr_rd.append(stWq.dma(q, wq_sl[i % 2][:, :],
                                                  sc_wpr[dc, :, og * 512:og * 512 + 512]))
                        on("sync", fqp)

                        def ftpj(t, i=i, og=og, dc=dc, pr=pr, c=c):
                            hh, vv = pr_rd[i]
                            t.wait_ge(hh, vv)
                            if dc == 0 and og == 0:
                                t.wait_ge(sV.h, M["merged"])
                                if pr == 0 and c >= 1:
                                    t.wait_ge(sV.h, M1[c - 1]["epi_v"][1])
                                elif pr == 1:
                                    t.wait_ge(sV.h, M["epi_v"][0])
                            for tj in range(2):
                                tt = 2 * pr + tj
                                pt = (ps_q if tj == 0 else ps_mo)[:, og * 512:og * 512 + 512]
                                t.matmul(pt,
                                         mergedT[:, dc * CH + tt * 128: dc * CH + tt * 128 + 128],
                                         wq_sl[i % 2][:, :],
                                         start=(dc == 0), stop=(dc == NDC - 1))
                            tinc(t)
                        on("tensor", ftpj)
                        M["prmm"].append(sT.v)
                m_pr = sT.v

                for tj in range(2):
                    tt = 2 * pr + tj

                    def fqx(q, tt=tt, pr=pr, c=c):
                        ei = len(M["epi_wr"])
                        if pr >= 1:
                            hh, vv = M["epi_wr"][tt - 2]
                            q.wait_ge(hh, vv)
                        elif c >= 1:
                            hh, vv = prev_epi[tt + 2]
                            q.wait_ge(hh, vv)
                        r0 = c * CH + tt * 128
                        M.setdefault("x_rd", []).append(
                            stEpi.dma(q, xbuf[tt % 2][:, :], x_h[r0:r0 + 128, :]))
                    on("sync", fqx)

                    def fve(v, tt=tt, tj=tj, m_pr=m_pr):
                        v.wait_ge(sT.h, m_pr)
                        hh, vv = M["x_rd"][tt]
                        v.wait_ge(hh, vv)
                        pt = ps_q if tj == 0 else ps_mo
                        v.tensor_tensor(out_st[tt % 2][:, :], xbuf[tt % 2][:, :],
                                        pt[:, :], ALU.add)
                        vinc(v)
                    on("vector", fve)
                    m_eo = sV.v

                    def fqo(q, tt=tt, c=c, m=m_eo):
                        q.wait_ge(sV.h, m)
                        r0 = c * CH + tt * 128
                        M["epi_wr"].append(
                            stOut.dma(q, out_h[r0:r0 + 128, :], out_st[tt % 2][:, :]))
                    on("sync", fqo)
                M.setdefault("epi_v", []).append(sV.v)
            M["ps_free"] = sV.v
            M["prmm_last"] = sT.v
            M["xnT_done"] = sT.v
            M["chunk_done"] = sV.v

            if "merged" in d_dbg:
                def fdm(q, c=c):
                    q.wait_ge(sV.h, M["merged"])
                    stDbg.dma(q, d_dbg["merged"][:, :, c * CH:(c + 1) * CH],
                              mergedT[:, :].rearrange("p (dc t) -> p dc t", dc=NDC))
                on("sync", fdm)
            if "gt" in d_dbg:
                def fdg(q, c=c):
                    q.wait_ge(sV.h, M["gated"][NIT - 1])
                    stDbg.dma(q, d_dbg["gt"][:, :, c * CH:(c + 1) * CH],
                              xs_bf[:, :].rearrange("p (it t) -> p it t", it=NIT))
                on("sync", fdg)
            if "xs" in d_dbg:
                def fdx(q, c=c):
                    q.wait_ge(sS.h, M["xs_done"][NIT - 1])
                    stDbg.dma(q, d_dbg["xs"][:, :, c * CH:(c + 1) * CH],
                              xs_bf[:, :].rearrange("p (it t) -> p it t", it=NIT))
                on("sync", fdx)
            if "bc" in d_dbg:
                def fdb(q, c=c):
                    q.wait_ge(sV.h, M["bc"])
                    stDbg.dma(q, d_dbg["bc"][0:16, c * CH:(c + 1) * CH], B_bf[0:16, :])
                    stDbg.dma(q, d_dbg["bc"][16:32, c * CH:(c + 1) * CH], C_bf[0:16, :])
                on("sync", fdb)
            if "qt" in d_dbg:
                def fdq(q, c=c):
                    q.wait_ge(sV.h, M["qrow"])
                    stDbg.dma(q, d_dbg["qt"][0:65, :, c * CH:(c + 1) * CH],
                              qT_sb[0:65, :].rearrange("p (h t) -> p h t", h=H))
                on("sync", fdq)

        def phase3():
            af.pop(); ab.pop()
            af.push(); ab.push()
            w1s = ab.take(NDC * IFF)
            w3s = ab.take(NDC * IFF)
            w2s = ab.take(NIT * DM)
            xn3 = [ab.take(NDC * CH) for _ in range(2)]
            hT = ab.take(NIT * CH)
            st3 = [af.take(DM) for _ in range(2)]
            cv3 = ab.take(2 * CH)

            last_chunk_v = M1[nch - 1]["chunk_done"]
            out_tot = stOut.totals()
            M3 = {}
            mm_w = {}
            evp = {0: 0, 1: 0}
            accp = {0: None, 1: None}
            xn_m = {}

            for e in range(NE):
                def fgw(q, e=e):
                    if e == 0:
                        q.wait_ge(sV.h, last_chunk_v)
                        for hh, vv in out_tot:
                            q.wait_ge(hh, vv)
                    else:
                        q.wait_ge(sT.h, mm_w[e - 1])
                    stW1.dma(q, w1s[:, :].rearrange("p (dc i) -> p dc i", dc=NDC),
                             w1_h[e].rearrange("(dc p) i -> p dc i", p=128))
                    stW3.dma(q, w3s[:, :].rearrange("p (dc i) -> p dc i", dc=NDC),
                             w3_h[e].rearrange("(dc p) i -> p dc i", p=128))
                    stW2.dma(q, w2s[:, :].rearrange("p (ic o) -> p ic o", ic=NIT),
                             w2_h[e].rearrange("(ic p) o -> p ic o", p=128))
                on("gpsimd", fgw)
                w_rdv = [(st.ss[0].h, st.ss[0].v) for st in (stW1, stW3, stW2)]

                for tc in range(nch):
                    def fq3(q, e=e, tc=tc):
                        key = (e, tc - 2) if tc >= 2 else ((e - 1, nch - 2 + tc) if e >= 1 else None)
                        if key is not None and key in mm_w:
                            q.wait_ge(sT.h, mm_w[key])
                        elif key is not None:
                            q.wait_ge(sT.h, mm_w[e - 1])
                        xn_m[(e, tc)] = stXn3.dma(
                            q, xn3[tc % 2][:, :].rearrange("p (dc t) -> p dc t", dc=NDC),
                            sc_xnT[:, :, tc * CH:(tc + 1) * CH].rearrange("dc p t -> p dc t"))
                    on("sync", fq3)

                    h_m = []
                    cvp = [cv3[:, 0:CH], cv3[:, CH:2 * CH]]
                    for ic in range(NIT):
                        pp = ps_q if ic % 2 == 0 else ps_y
                        def ft1(t, e=e, tc=tc, ic=ic, pp=pp):
                            if ic == 0:
                                hh, vv = xn_m[(e, tc)]
                                t.wait_ge(hh, vv)
                                for hh2, vv2 in w_rdv[0:2]:
                                    t.wait_ge(hh2, vv2)
                                if e == 0 and tc == 0:
                                    t.wait_ge(sV.h, last_chunk_v)
                                if evp[0] > 0:
                                    t.wait_ge(sV.h, evp[0])
                                if evp[1] > 0:
                                    t.wait_ge(sV.h, evp[1])
                            if ic >= 2:
                                t.wait_ge(sV.h, h_m[ic - 2])
                            for dc in range(NDC):
                                t.matmul(pp[:, 0:512],
                                         w1s[:, dc * IFF + ic * 128: dc * IFF + ic * 128 + 128],
                                         xn3[tc % 2][:, dc * CH:(dc + 1) * CH],
                                         start=(dc == 0), stop=(dc == NDC - 1))
                            for dc in range(NDC):
                                t.matmul(pp[:, 512:1024],
                                         w3s[:, dc * IFF + ic * 128: dc * IFF + ic * 128 + 128],
                                         xn3[tc % 2][:, dc * CH:(dc + 1) * CH],
                                         start=(dc == 0), stop=(dc == NDC - 1))
                            tinc(t)
                        on("tensor", ft1)
                        m1 = sT.v

                        def fs1(s, m=m1, ic=ic, pp=pp):
                            s.wait_ge(sT.h, m)
                            if ic >= 2:
                                s.wait_ge(sV.h, h_m[ic - 2])
                            s.activation(cvp[ic % 2], pp[:, 0:512], AF.Silu)
                            sinc(s)
                        on("scalar", fs1)
                        m_si = sS.v

                        def fv1(v, m=m_si, ic=ic, pp=pp):
                            v.wait_ge(sS.h, m)
                            v.tensor_tensor(hT[:, ic * CH:(ic + 1) * CH], cvp[ic % 2],
                                            pp[:, 512:1024], ALU.mult)
                            vinc(v)
                        on("vector", fv1)
                        h_m.append(sV.v)

                    for ts_ in range(NTT):
                        py = ps_q if ts_ % 2 == 0 else ps_mo
                        par = ts_ % 2

                        def ft2(t, e=e, tc=tc, ts_=ts_, py=py, par=par):
                            t.wait_ge(sV.h, h_m[NIT - 1])
                            hh, vv = w_rdv[2]
                            t.wait_ge(hh, vv)
                            if evp[par] > 0:
                                t.wait_ge(sV.h, evp[par])
                            for half in range(2):
                                for ic in range(NIT):
                                    t.matmul(py[:, half * 512:(half + 1) * 512],
                                             hT[:, ic * CH + ts_ * 128: ic * CH + ts_ * 128 + 128],
                                             w2s[:, ic * DM + half * 512: ic * DM + half * 512 + 512],
                                             start=(ic == 0), stop=(ic == NIT - 1))
                            tinc(t)
                        on("tensor", ft2)
                        m2 = sT.v

                        def fv2(v, m=m2, e=e, tc=tc, ts_=ts_, py=py, par=par):
                            v.wait_ge(sT.h, m)
                            if accp[par] is not None:
                                hh, vv = accp[par]
                                v.wait_ge(hh, vv)
                            ti = tc * NTT + ts_
                            v.tensor_scalar_mul(st3[par][:, :], py[:, :],
                                                c_coef[:, ti * NE + e: ti * NE + e + 1])
                            vinc(v)
                        on("vector", fv2)
                        evp[par] = sV.v
                        m_ev = sV.v

                        def fga(q, tc=tc, ts_=ts_, par=par, m=m_ev):
                            q.wait_ge(sV.h, m)
                            r0 = tc * CH + ts_ * 128
                            accp[par] = stAcc.dma(q, out_h[r0:r0 + 128, :],
                                                  st3[par][:, :], accum_op=ALU.add)
                        on("gpsimd", fga)
                    mm_w[(e, tc)] = sT.v
                    flush()
                mm_w[e] = sT.v

        phase_pre()
        wsT = phase0_weights()
        flush()
        phase0()
        if upto != "s1":
            for c in range(nch):
                chunk1(c)
                flush()
            if upto == "full":
                phase3()

        if "xn" in d_dbg or "c_coef" in d_dbg:
            def fdbg0(q):
                q.wait_ge(sV.h, sV.v)
                if "c_coef" in d_dbg:
                    stDbg.dma(q, d_dbg["c_coef"][:, 0:nch * NTT * NE],
                              c_coef[:, 0:nch * NTT * NE])
                if "xn" in d_dbg:
                    for hh, vv in stXnW.totals():
                        q.wait_ge(hh, vv)
                    stDbg.dma(q, d_dbg["xn"][:, :, 0:nch * CH],
                              sc_xnT[:, :, 0:nch * CH])
            on("sync", fdbg0)

        def ffin(q):
            q.wait_ge(sV.h, sV.v)
            q.wait_ge(sS.h, sS.v)
            q.wait_ge(sT.h, sT.v)
            for st_ in (stOut, stAcc, stDbg, stXnW, stKW, stWS):
                for hh, vv in st_.totals():
                    q.wait_ge(hh, vv)
        on("sync", ffin)
        flush()

    return nc, d_in, d_dbg



def _prep_inputs(kw):
    x = np.asarray(kw["L0_attn_x"])
    t = np.arange(S, dtype=np.float64)
    inv = 1.0 / (10000.0 ** (np.arange(0, ROPE_, 2, dtype=np.float64) / ROPE_))
    fr = np.outer(t, inv)
    cos_rep = np.tile(np.cos(fr), (1, H)).astype(ml_dtypes.bfloat16)
    sin_rep = np.tile(np.sin(fr), (1, H)).astype(ml_dtypes.bfloat16)

    ident = np.eye(128, dtype=ml_dtypes.bfloat16)
    ones1 = np.ones((1, 128), dtype=ml_dtypes.bfloat16)
    oneh = np.zeros((16, 16 * 128), dtype=ml_dtypes.bfloat16)
    for n in range(16):
        oneh[n, n * 128:(n + 1) * 128] = 1.0
    m4 = np.zeros((4, 128, CH), dtype=ml_dtypes.bfloat16)
    p = np.arange(128)[:, None]
    q = np.arange(CH)[None, :]
    for j in range(4):
        m4[j] = (q >= p + 128 * j).astype(ml_dtypes.bfloat16)

    def lay_pit(v, k):
        return np.ascontiguousarray(
            v.reshape(NIT, 128, k).transpose(1, 0, 2).reshape(128, NIT * k))

    com = dict(
        wcqT=np.ascontiguousarray(np.asarray(kw["L0_attn_W_cq"], np.float32).T),
        winT=np.ascontiguousarray(np.asarray(kw["L0_attn_W_in"], np.float32).T),
        wxpT=np.ascontiguousarray(np.asarray(kw["L0_attn_W_xproj"], np.float32).T),
        wdtT=np.ascontiguousarray(np.asarray(kw["L0_attn_W_dt"], np.float32).T),
        wmoT=np.ascontiguousarray(np.asarray(kw["L0_attn_W_mout"], np.float32).T),
        wprT=np.ascontiguousarray(np.asarray(kw["L0_attn_W_proj"], np.float32).T),
        alog=lay_pit(np.asarray(kw["L0_attn_A_log"], np.float32), NST),
        cw=lay_pit(np.asarray(kw["L0_attn_conv_w"], np.float32), 4),
        cb=lay_pit(np.asarray(kw["L0_attn_conv_b"], np.float32), 1),
        bdt=lay_pit(np.asarray(kw["L0_attn_b_dt"], np.float32), 1),
        dpar=lay_pit(np.asarray(kw["L0_attn_D_param"], np.float32), 1),
        grep=np.tile(np.asarray(kw["L0_attn_q_gain"], np.float32)[None, :], (128, 1)),
        alpha=np.tile(np.asarray(kw["L0_attn_merge_alpha"], np.float32)[None, :], (128, 1)),
        wr=np.ascontiguousarray(np.asarray(kw["L0_ffn_Wr"], np.float32)),
        br=np.tile(np.asarray(kw["L0_ffn_br"], np.float32)[None, :], (128, 1)),
        w1=np.ascontiguousarray(np.asarray(kw["L0_ffn_W1"], np.float32)),
        w3=np.ascontiguousarray(np.asarray(kw["L0_ffn_W3"], np.float32)),
        w2=np.ascontiguousarray(np.asarray(kw["L0_ffn_W2"], np.float32)),
        cosr=cos_rep, sinr=sin_rep,
        ident=ident, ones1=ones1, oneh=oneh, mask4=m4,
    )
    outs = []
    for b in range(8):
        d = dict(com)
        d["x"] = np.ascontiguousarray(x[b].astype(np.float32))
        outs.append(d)
    return outs


def kernel(**kw):
    in_maps = _prep_inputs(kw)
    nc, d_in, d_dbg = build(upto="full")
    res = run_bass_kernel_spmd(nc, in_maps, core_ids=list(range(8)))
    return np.stack([r["out"] for r in res.results], axis=0)
